# version 04
# baseline (speedup 1.0000x reference)
import contextlib
import numpy as np
import concourse.bass as bass
import concourse.mybir as mybir
from concourse.bass_utils import run_bass_kernel_spmd

F32 = mybir.dt.float32
BF16 = mybir.dt.bfloat16
AF = mybir.ActivationFunctionType
ALU = mybir.AluOpType
AX = mybir.AxisListType

N_CORES = 8
DUMP_STAGES = False
L1FFN_GROUPS = 4
L1FFN_EXPERTS = 8
B = 2
SEQ = 8192
D = 1024
TCH = 2048
NBLK = TCH // 128
LN_EPS = 1e-5



class Sem:

    def __init__(self, nc, es, name):
        self.h = es.enter_context(nc.semaphore(name))
        self.n = 0

    def inc(self, instr, k=1):
        instr.then_inc(self.h, k)
        self.n += k
        return instr

    def dma(self, instr):
        return self.inc(instr, 16)

    def wait(self, eng, n=None):
        eng.wait_ge(self.h, self.n if n is None else n)


class Ctx:
    def __init__(self, nc, es, sems=None, tag=""):
        self.nc = nc
        self.es = es
        self.sems = {} if sems is None else sems
        self.tag = tag

    def sem(self, name):
        if name not in self.sems:
            self.sems[name] = Sem(self.nc, self.es, name)
        return self.sems[name]

    def sub(self, es, subtag):
        return Ctx(self.nc, es, self.sems, self.tag + subtag)

    def sbuf(self, name, shape, dtype):
        return self.es.enter_context(
            self.nc.sbuf_tensor(self.tag + name, shape, dtype))

    def psum(self, name, shape, dtype=F32):
        return self.es.enter_context(
            self.nc.psum_tensor(self.tag + name, shape, dtype))


def fm(ap, c=8):
    return ap[:, :].rearrange("p (c t) -> p c t", c=c)


def barrier(block, ctx, skip=("scc",)):
    snap = [(s.h, s.n) for k, s in ctx.sems.items()
            if s.n > 0 and k not in skip]
    for en in ("sync", "gpsimd", "scalar", "vector", "tensor"):
        @getattr(block, en)
        def _(eng, snap=snap):
            for h, n in snap:
                eng.wait_ge(h, n)


def emit_ln_stats(block, xf, stat, sq, s_v, s_a, n_free=D, eps=LN_EPS):
    @block.vector
    def _(v):
        v.reduce_sum(stat[:, 0:1], xf, axis=AX.X, negate=True)
        v.drain()
        v.tensor_scalar_mul(stat[:, 0:1], stat[:, 0:1], 1.0 / n_free)
        s_v.inc(v.drain())

    @block.scalar
    def _(s):
        s_v.wait(s)
        s.activation(out=sq, in_=xf, func=AF.Square, bias=stat[:, 0:1], scale=1.0)
        s_a.inc(s.drain())

    @block.vector
    def _(v):
        s_a.wait(v)
        v.reduce_sum(stat[:, 1:2], sq, axis=AX.X)
        v.drain()
        v.tensor_scalar(stat[:, 1:2], stat[:, 1:2], 1.0 / n_free, eps,
                        ALU.mult, ALU.add)
        s_v.inc(v.drain())

    @block.scalar
    def _(s):
        s_v.wait(s)
        s.activation(out=stat[:, 2:3], in_=stat[:, 1:2], func=AF.Sqrt,
                     bias=0.0, scale=1.0)
        s_a.inc(s.drain())

    @block.vector
    def _(v):
        s_a.wait(v)
        v.reciprocal(stat[:, 3:4], stat[:, 2:3])
        s_v.inc(v.drain())



def emit_l0attn(nc, ctx, block, io):
    x_own = io["x_own"]; x_seq = io["x_seq"]
    wqt = io["wqt"]; wkt = io["wkt"]; wvt = io["wvt"]; wot = io["wot"]
    maskt = io["maskt"]; ident = io["ident"]
    x1 = io["x1"]; k_sc = io["k_sc"]; v_sc = io["v_sc"]

    q_all = ctx.sbuf("q_all", [128, 8 * TCH], BF16)
    maskm = ctx.sbuf("maskm", [128, 79 * 128], BF16)
    attn_f = ctx.sbuf("attn_f", [128, 8 * TCH], BF16)
    identb = ctx.sbuf("identb", [128, 128], BF16)
    bneg = ctx.sbuf("bneg", [128, 1], F32)
    onesf = ctx.sbuf("onesf", [128, 64], F32)
    stat = ctx.sbuf("stat", [128, 8], F32)
    sq = ctx.sbuf("sq", [128, D], F32)
    xf = [ctx.sbuf(f"xf{i}", [128, D], F32) for i in range(2)]
    xn_t = ctx.sbuf("xn_t", [128, D], BF16)
    t1 = ctx.sbuf("t1", [128, D], F32)
    out_sb = [ctx.sbuf(f"out_sb{i}", [128, D], F32) for i in range(2)]
    x1e = [ctx.sbuf(f"x1e{i}", [128, D], F32) for i in range(2)]

    sld = ctx.sem("sld")
    swl = ctx.sem("swl")
    s_t = ctx.sem("s_t")
    s_a = ctx.sem("s_a")
    s_v = ctx.sem("s_v")
    s_p = ctx.sem("s_p")
    sst = ctx.sem("sst")
    ssc = ctx.sem("ssc")

    @block.gpsimd
    def _(g):
        swl.dma(g.dma_start(out=identb[:], in_=ident[:]))
        swl.dma(g.dma_start(out=maskm[:], in_=maskt[:]))
        g.memset(bneg[:], -25.0)
        g.memset(onesf[:], 1.0)
        s_p.inc(g.nop())
    consts_g = s_p.n
    consts_w = swl.n

    with contextlib.ExitStack() as esA:
        ctxA = ctx.sub(esA, "A")
        wq_sb = ctxA.sbuf("wq_sb", [128, 8 * 1024], BF16)
        wk_sb = ctxA.sbuf("wk_sb", [128, 8 * 1024], BF16)
        wv_sb = ctxA.sbuf("wv_sb", [128, 8 * 1024], BF16)
        xnf_g = ctxA.sbuf("xnf_g", [128, 8 * 512], BF16)
        krow = [ctxA.sbuf(f"krow{i}", [128, 512], BF16) for i in range(2)]
        vtok = [ctxA.sbuf(f"vtok{i}", [128, 1024], BF16) for i in range(2)]
        ps_t = ctxA.psum("ps_t", [128, 1024], BF16)
        ps_ab = [ctxA.psum(f"ps_ab{i}", [128, 512], F32) for i in range(2)]

        @block.gpsimd
        def _(g):
            for c in range(8):
                swl.dma(g.dma_start(
                    out=fm(wq_sb)[:, c, :].rearrange("p (o u) -> p o u", o=8),
                    in_=wqt[c].rearrange("o p u -> p o u")))
                swl.dma(g.dma_start(
                    out=fm(wk_sb)[:, c, :].rearrange("p (o u) -> p o u", o=8),
                    in_=wkt[c].rearrange("o p u -> p o u")))
                swl.dma(g.dma_start(out=fm(wv_sb)[:, c, :], in_=wvt[c]))
        wts_loaded = swl.n

        ev_cnt = []
        ln_cnt = []
        kv_t = []
        kv_a = []
        k_store = []
        v_store = []
        q_ev = []

        def ln_block(src_dram, row0, i_par, blkid):
            @block.sync
            def _(sync):
                if len(ln_cnt) >= 2:
                    sync.wait_ge(s_v.h, ln_cnt[-2])
                sld.dma(sync.dma_start(out=xf[blkid % 2][:],
                                       in_=src_dram[row0:row0 + 128, :]))

            @block.vector
            def _(v):
                sld.wait(v)

            emit_ln_stats(block, xf[blkid % 2][:], stat, sq[:], s_v, s_a)

            @block.vector
            def _(v):
                if ev_cnt:
                    v.wait_ge(s_v.h, ev_cnt[-1])
                v.tensor_scalar(xn_t[:], xf[blkid % 2][:], stat[:, 0:1],
                                stat[:, 3:4], ALU.add, ALU.mult)
                s_v.inc(v.drain())
                ln_cnt.append(s_v.n)

            @block.tensor
            def _(t):
                swl.wait(t, consts_w)
                s_v.wait(t)
                for c in range(8):
                    ins = t.transpose(ps_t[:, c * 128:(c + 1) * 128],
                                      xn_t[:, c * 128:(c + 1) * 128], identb[:])
                s_t.inc(ins)

            @block.vector
            def _(v):
                s_t.wait(v)
                v.tensor_copy(fm(xnf_g)[:, :, i_par * 128:(i_par + 1) * 128],
                              fm(ps_t)[:, :, :])
                s_v.inc(v.drain())
                ev_cnt.append(s_v.n)

        blkid = 0
        for g in range(16):
            for i4 in range(4):
                ln_block(x_seq, g * 512 + i4 * 128, i4, blkid)
                blkid += 1
            for oc in range(8):
                kk = len(kv_t)

                @block.tensor
                def _(t, oc=oc, kk=kk):
                    t.wait_ge(swl.h, wts_loaded)
                    s_v.wait(t)
                    if kk >= 2:
                        t.wait_ge(s_v.h, kv_a[kk - 2])
                    for c in range(8):
                        ins = t.matmul(ps_ab[kk % 2][:],
                                       wk_sb[:, c * 1024 + oc * 128:c * 1024 + oc * 128 + 128],
                                       fm(xnf_g)[:, c, :],
                                       start=(c == 0), stop=(c == 7))
                    s_t.inc(ins)
                    kv_t.append(s_t.n)

                @block.vector
                def _(v, oc=oc, kk=kk):
                    s_t.wait(v)
                    if len(k_store) >= 2:
                        v.wait_ge(ssc.h, k_store[-2])
                    v.tensor_copy(krow[kk % 2][:], ps_ab[kk % 2][:])
                    s_v.inc(v.drain())
                    kv_a.append(s_v.n)

                @block.sync
                def _(sync, g=g, oc=oc, kk=kk):
                    sync.wait_ge(s_v.h, kv_a[kk])
                    ssc.dma(sync.dma_start(out=k_sc[oc][:, g * 512:(g + 1) * 512],
                                           in_=krow[kk % 2][:]))
                    k_store.append(ssc.n)

            for tb in range(4):
                for nh in range(2):
                    kk = len(kv_t)

                    @block.tensor
                    def _(t, tb=tb, nh=nh, kk=kk):
                        if kk >= 2:
                            t.wait_ge(s_v.h, kv_a[kk - 2])
                        for c in range(8):
                            ins = t.matmul(
                                ps_ab[kk % 2][:],
                                fm(xnf_g)[:, c, tb * 128:(tb + 1) * 128],
                                wv_sb[:, c * 1024 + nh * 512:c * 1024 + nh * 512 + 512],
                                start=(c == 0), stop=(c == 7))
                        s_t.inc(ins)
                        kv_t.append(s_t.n)

                    @block.vector
                    def _(v, tb=tb, nh=nh, kk=kk):
                        s_t.wait(v)
                        if nh == 0 and len(v_store) >= 2:
                            v.wait_ge(ssc.h, v_store[-2])
                        v.tensor_copy(vtok[tb % 2][:, nh * 512:(nh + 1) * 512],
                                      ps_ab[kk % 2][:])
                        s_v.inc(v.drain())
                        kv_a.append(s_v.n)

                @block.sync
                def _(sync, g=g, tb=tb):
                    sync.wait_ge(s_v.h, kv_a[-1])
                    r0 = g * 512 + tb * 128
                    for hq in range(4):
                        ins = sync.dma_start(
                            out=v_sc[:, r0:r0 + 128, :].rearrange(
                                "h p c -> p h c")[:, 4 * hq:4 * hq + 4, :],
                            in_=vtok[tb % 2][:, :].rearrange(
                                "p (h c) -> p h c", h=16)[:, 4 * hq:4 * hq + 4, :])
                        ssc.dma(ins)
                    v_store.append(ssc.n)

        for g in range(4):
            for i4 in range(4):
                ln_block(x_own, g * 512 + i4 * 128, i4, blkid)
                blkid += 1
            for oc in range(8):
                kk = len(kv_t)

                @block.tensor
                def _(t, oc=oc, kk=kk):
                    s_v.wait(t)
                    if kk >= 2:
                        t.wait_ge(s_v.h, kv_a[kk - 2])
                    for c in range(8):
                        ins = t.matmul(ps_ab[kk % 2][:],
                                       wq_sb[:, c * 1024 + oc * 128:c * 1024 + oc * 128 + 128],
                                       fm(xnf_g)[:, c, :],
                                       start=(c == 0), stop=(c == 7))
                    s_t.inc(ins)
                    kv_t.append(s_t.n)

                @block.vector
                def _(v, g=g, oc=oc, kk=kk):
                    s_t.wait(v)
                    v.tensor_copy(fm(q_all)[:, oc, g * 512:(g + 1) * 512],
                                  ps_ab[kk % 2][:])
                    s_v.inc(v.drain())
                    kv_a.append(s_v.n)
                    q_ev.append(s_v.n)

    with contextlib.ExitStack() as esB:
        ctxB = ctx.sub(esB, "B")
        kcb = [ctxB.sbuf("kcb0", [128, SEQ], BF16)] * 2
        ptb = ctxB.sbuf("ptb", [128, 32 * 512], BF16)
        vex = [ctxB.sbuf(f"vex{i}", [128, 64 * 65], BF16) for i in range(2)]
        wo_sb = ctxB.sbuf("wo_sb", [128, 8 * 1024], BF16)
        dsb = ctxB.sbuf("dsb", [128, 1024], F32)
        araw = ctxB.sbuf("araw", [128, 512], F32)
        ps_s = [ctxB.psum(f"ps_s{i}", [128, 512], F32) for i in range(4)]
        ps_av = ctxB.psum("ps_av", [128, 512], F32)
        ps_bc = ctxB.psum("ps_bc", [128, 512], F32)
        ps_o = ctxB.psum("ps_o", [128, 1024], F32)

        @block.gpsimd
        def _(g):
            g.wait_ge(s_v.h, q_ev[-1])
            g.wait_ge(ssc.h, ssc.n)
            for c in range(8):
                swl.dma(g.dma_start(out=fm(wo_sb)[:, c, :], in_=wot[c]))
        wo_loaded = swl.n

        @block.vector
        def _(v):
            for i in range(2):
                ins = v.memset(vex[i][:, :].rearrange(
                    "p (k c) -> p k c", c=65)[:, :, 64:65], 1.0)
                v.drain()
            s_p.inc(v.nop())
        vex_ones = s_p.n

        kload = []
        vload = []
        exp_a = []
        sc_t = []
        av_t = []
        av_mid = []
        msk_v = []
        den_a = []
        af_v = []

        for kc in range(8):
            @block.sync
            def _(sync, kc=kc):
                if kc == 0:
                    sync.wait_ge(s_v.h, q_ev[-1])
                if kc >= 1:
                    sync.wait_ge(s_t.h, av_t[-1])
                sld.dma(sync.dma_start(out=kcb[kc % 2][:], in_=k_sc[kc]))
                kload.append(sld.n)

            for h2 in range(2):
                h = 2 * kc + h2
                vb = h % 2

                @block.gpsimd
                def _(g, kc=kc, h=h, vb=vb):
                    g.wait_ge(ssc.h, ssc.n)
                    g.wait_ge(s_p.h, vex_ones)
                    if h >= 2:
                        g.wait_ge(s_t.h, av_t[(h - 1) * 4 - 1])
                    for q8 in range(16):
                        swl.dma(g.dma_start(
                            out=vex[vb][:, :].rearrange(
                                "p (k c) -> p k c", c=65)[:, 4 * q8:4 * q8 + 4, 0:64],
                            in_=v_sc[h].rearrange(
                                "(k p) c -> p k c", p=128)[:, 4 * q8:4 * q8 + 4, :]))
                    vload.append(swl.n)

                for qg in range(4):
                    hq = len(av_t)
                    for kh in range(2):
                        for kb32 in range(32):
                            kb = 32 * kh + kb32

                            @block.tensor
                            def _(t, kc=kc, h2=h2, qg=qg, kh=kh, kb32=kb32, kb=kb, hq=hq):
                                if kb32 == 0 and kh == 0:
                                    t.wait_ge(sld.h, kload[kc])
                                if kb32 == 0 and qg == 0 and kh == 0 and kc == 0 and h2 == 0:
                                    t.wait_ge(s_v.h, q_ev[-1])
                                if len(exp_a) >= 4:
                                    t.wait_ge(s_a.h, exp_a[-4])
                                ins = t.matmul(
                                    ps_s[len(sc_t) % 4][:],
                                    kcb[kc % 2][h2 * 64:h2 * 64 + 64, kb * 128:kb * 128 + 128],
                                    q_all[h2 * 64:h2 * 64 + 64,
                                          kc * 2048 + qg * 512:kc * 2048 + qg * 512 + 512],
                                    start=True, stop=True)
                                s_t.inc(ins)
                                sc_t.append(s_t.n)

                            @block.scalar
                            def _(s, kb32=kb32, kh=kh, hq=hq):
                                if not exp_a:
                                    s.wait_ge(s_p.h, consts_g)
                                s.wait_ge(s_t.h, sc_t[-1])
                                if kb32 == 0:
                                    if kh == 1:
                                        s.wait_ge(s_v.h, msk_v[-1])
                                        s.wait_ge(s_t.h, av_mid[-1])
                                    elif av_t:
                                        s.wait_ge(s_t.h, av_t[-1])
                                ins = s.activation(
                                    out=ptb[:, kb32 * 512:(kb32 + 1) * 512],
                                    in_=ps_s[(len(sc_t) - 1) % 4][:],
                                    func=AF.Exp, bias=bneg[:, 0:1], scale=0.125)
                                s_a.inc(ins)
                                exp_a.append(s_a.n)

                        @block.vector
                        def _(v, qg=qg, kh=kh):
                            if not msk_v:
                                v.wait_ge(swl.h, consts_w)
                            s_a.wait(v)
                            for j in range(4):
                                mu0 = (15 - 4 * qg - j + 32 * kh)
                                v.tensor_mul(
                                    fm(ptb, c=32)[:, :, j * 128:(j + 1) * 128],
                                    fm(ptb, c=32)[:, :, j * 128:(j + 1) * 128],
                                    fm(maskm, c=79)[:, mu0:mu0 + 32, :])
                                v.drain()
                            s_v.inc(v.nop())
                            msk_v.append(s_v.n)

                        for kb32 in range(32):
                            kb = 32 * kh + kb32

                            @block.tensor
                            def _(t, vb=vb, kh=kh, kb32=kb32, kb=kb, h=h):
                                if kb32 == 0:
                                    t.wait_ge(s_v.h, msk_v[-1])
                                    if kh == 0:
                                        t.wait_ge(swl.h, vload[h])
                                        if den_a:
                                            t.wait_ge(s_a.h, den_a[-1])
                                ins = t.matmul(
                                    ps_av[0:65, :],
                                    vex[vb][:, kb * 65:kb * 65 + 65],
                                    ptb[:, kb32 * 512:(kb32 + 1) * 512],
                                    start=(kh == 0 and kb32 == 0),
                                    stop=(kh == 1 and kb32 == 31))
                                s_t.inc(ins)
                                if kb32 == 31 and kh == 0:
                                    av_mid.append(s_t.n)

                    av_t.append(s_t.n)

                    @block.scalar
                    def _(s, hq=hq):
                        s.wait_ge(s_t.h, av_t[hq])
                        if af_v:
                            s.wait_ge(s_v.h, af_v[-1])
                        s.activation(out=dsb[64:65, 0:512], in_=ps_av[64:65, :],
                                     func=AF.Copy, bias=0.0, scale=1.0)
                        s.drain()
                        ins = s.activation(out=araw[0:64, :], in_=ps_av[0:64, :],
                                           func=AF.Copy, bias=0.0, scale=1.0)
                        s_a.inc(ins)
                        den_a.append(s_a.n)

                    @block.vector
                    def _(v):
                        s_a.wait(v)
                        v.reciprocal(dsb[64:65, 512:1024], dsb[64:65, 0:512])
                        s_v.inc(v.drain())

                    @block.tensor
                    def _(t):
                        t.wait_ge(s_p.h, consts_g)
                        s_v.wait(t)
                        ins = t.matmul(ps_bc[0:64, :], onesf[64:65, 0:64],
                                       dsb[64:65, 512:1024], start=True, stop=True)
                        s_t.inc(ins)

                    @block.vector
                    def _(v, kc=kc, h2=h2, qg=qg):
                        s_t.wait(v)
                        v.tensor_mul(
                            attn_f[h2 * 64:h2 * 64 + 64,
                                   kc * 2048 + qg * 512:kc * 2048 + qg * 512 + 512],
                            araw[0:64, :], ps_bc[0:64, :])
                        s_v.inc(v.drain())
                        af_v.append(s_v.n)

        ep_done = []
        ep_ln = []
        st_cnt = []
        for tb in range(NBLK):
            @block.sync
            def _(sync, tb=tb):
                if tb >= 2:
                    sync.wait_ge(s_p.h, ep_done[tb - 2])
                sld.dma(sync.dma_start(out=x1e[tb % 2][:],
                                       in_=x_own[tb * 128:(tb + 1) * 128, :]))

            @block.tensor
            def _(t, tb=tb):
                if tb == 0:
                    t.wait_ge(swl.h, wo_loaded)
                    s_v.wait(t)
                if tb >= 1:
                    t.wait_ge(s_v.h, ep_ln[tb - 1])
                for c in range(8):
                    for nh in range(2):
                        ins = t.matmul(
                            ps_o[:, nh * 512:(nh + 1) * 512],
                            fm(attn_f)[:, c, tb * 128:(tb + 1) * 128],
                            wo_sb[:, c * 1024 + nh * 512:c * 1024 + nh * 512 + 512],
                            start=(c == 0), stop=(c == 7))
                s_t.inc(ins)

            @block.vector
            def _(v):
                s_t.wait(v)

            emit_ln_stats(block, ps_o[:], stat, sq[:], s_v, s_a)

            @block.vector
            def _(v, tb=tb):
                if tb >= 1:
                    v.wait_ge(s_p.h, ep_done[tb - 1])
                v.tensor_scalar(t1[:], ps_o[:], stat[:, 0:1], stat[:, 3:4],
                                ALU.add, ALU.mult)
                s_v.inc(v.drain())
                ep_ln.append(s_v.n)

            @block.gpsimd
            def _(g, tb=tb):
                s_v.wait(g)
                sld.wait(g)
                if tb >= 2:
                    g.wait_ge(sst.h, st_cnt[tb - 2])
                ins = g.tensor_add(out_sb[tb % 2][:], t1[:], x1e[tb % 2][:])
                s_p.inc(ins)
                ep_done.append(s_p.n)

            @block.sync
            def _(sync, tb=tb):
                sync.wait_ge(s_p.h, ep_done[tb])
                sst.dma(sync.dma_start(out=x1[tb * 128:(tb + 1) * 128, :],
                                       in_=out_sb[tb % 2][:]))
                st_cnt.append(sst.n)

        @block.sync
        def _(sync):
            sst.wait(sync)



def emit_l0ffn(nc, ctx, block, io):
    x1 = io["x1"]; x2 = io["x2"]; ident = io["ident"]
    w1t = io["w1t"]; w2t = io["w2t"]; h1t = io["h1t"]; h2t = io["h2t"]
    w3t = io["w3t"]
    w1s = io["w1s"]; w2s = io["w2s"]; h1s = io["h1s"]; h2s = io["h2s"]
    w3s = io["w3s"]

    xnf = ctx.sbuf("xnf", [128, 8 * TCH], BF16)
    big2 = ctx.sbuf("big2", [128, 32 * 1024], BF16)
    hbuf = ctx.sbuf("hbuf", [128, 32 * 512], BF16)
    w3sb = big2
    h1o = big2[:, 0:32 * 512]
    xf = [ctx.sbuf(f"xf{i}", [128, D], F32) for i in range(2)]
    xn_t = ctx.sbuf("xn_t", [128, D], BF16)
    identb = ctx.sbuf("identb", [128, 128], BF16)
    scl = ctx.sbuf("scl", [128, 4 * 32], F32)
    w3sb_bc = ctx.sbuf("w3sb_bc", [128, D], F32)
    w3s_row = ctx.sbuf("w3s_row", [1, D], BF16)
    ones1 = ctx.sbuf("ones1", [1, 128], BF16)
    stat = ctx.sbuf("stat", [128, 8], F32)
    sq = ctx.sbuf("sq", [128, D], F32)
    usb = [ctx.sbuf(f"usb{i}", [128, 512], F32) for i in range(2)]
    wsl = [ctx.sbuf(f"wsl{i}", [128, 8 * 128], BF16) for i in range(4)]
    hsl = [ctx.sbuf(f"hsl{i}", [128, 32 * 128], BF16) for i in range(2)]
    t1 = ctx.sbuf("t1", [128, D], F32)
    out_sb = [ctx.sbuf(f"out_sb{i}", [128, D], F32) for i in range(2)]
    x1e = [ctx.sbuf(f"x1e{i}", [128, D], F32) for i in range(2)]

    ps_t = ctx.psum("ps_t", [128, 1024], BF16)
    ps_u = [ctx.psum(f"ps_u{i}", [128, 512], F32) for i in range(4)]
    ps_o = ctx.psum("ps_o", [128, 1024], F32)

    sld = ctx.sem("sld")
    swl = ctx.sem("swl")
    sw2 = ctx.sem("sw2")
    swu = [ctx.sem(f"swu{i}") for i in range(4)]
    swh = [ctx.sem(f"swh{i}") for i in range(2)]
    s_t = ctx.sem("s_t")
    s_a = ctx.sem("s_a")
    s_v = ctx.sem("s_v")
    s_p = ctx.sem("s_p")
    sst = ctx.sem("sst")
    sld_base = sld.n

    @block.gpsimd
    def _(g):
        swl.dma(g.dma_start(out=identb[:], in_=ident[:]))
        swl.dma(g.dma_start(out=scl[:, 0:32], in_=w1s[:]))
        swl.dma(g.dma_start(out=scl[:, 32:64], in_=w2s[:]))
        swl.dma(g.dma_start(out=scl[:, 64:96], in_=h1s[:]))
        swl.dma(g.dma_start(out=scl[:, 96:128], in_=h2s[:]))
        swl.dma(g.dma_start(out=w3s_row[:], in_=w3s[:]))
        g.memset(ones1[:], 1.0)
        s_p.inc(g.nop())
    consts_loaded = swl.n
    ones_ready = s_p.n

    @block.tensor
    def _(t):
        swl.wait(t, consts_loaded)
        t.wait_ge(s_p.h, ones_ready)
        t.matmul(ps_o[:, 0:512], ones1[:], w3s_row[:, 0:512], start=True, stop=True)
        ins = t.matmul(ps_o[:, 512:1024], ones1[:], w3s_row[:, 512:1024],
                       start=True, stop=True)
        s_t.inc(ins)
    bc_mm = s_t.n

    @block.vector
    def _(v):
        v.wait_ge(s_t.h, bc_mm)
        v.tensor_copy(w3sb_bc[:], ps_o[:])
        s_v.inc(v.drain())
    bc_done = s_v.n

    ln_done = []
    ev_done = []
    for i in range(NBLK):
        @block.sync
        def _(sync, i=i):
            if i >= 2:
                sync.wait_ge(s_v.h, ln_done[i - 2])
            sld.dma(sync.dma_start(out=xf[i % 2][:], in_=x1[i * 128:(i + 1) * 128, :]))

        @block.vector
        def _(v, i=i):
            sld.wait(v, sld_base + 16 * (i + 1))

        emit_ln_stats(block, xf[i % 2][:], stat, sq[:], s_v, s_a)

        @block.vector
        def _(v, i=i):
            if i >= 1:
                v.wait_ge(s_v.h, ev_done[i - 1])
            v.tensor_scalar(xn_t[:], xf[i % 2][:], stat[:, 0:1], stat[:, 3:4],
                            ALU.add, ALU.mult)
            s_v.inc(v.drain())
            ln_done.append(s_v.n)

        @block.tensor
        def _(t, i=i):
            s_v.wait(t)
            for c in range(8):
                ins = t.transpose(ps_t[:, c * 128:(c + 1) * 128],
                                  xn_t[:, c * 128:(c + 1) * 128], identb[:])
            s_t.inc(ins)

        @block.vector
        def _(v, i=i):
            s_t.wait(v)
            v.tensor_copy(fm(xnf)[:, :, i * 128:(i + 1) * 128],
                          fm(ps_t)[:, :, :])
            s_v.inc(v.drain())
            ev_done.append(s_v.n)

    up_t = []
    up_a = []
    up_p = []
    hp_t = []
    hp_a = []
    ep_done = []
    ep_ln = []
    st_cnt = []
    for grp in range(4):
        tok0 = grp * 512
        for oc in range(32):
            for kind in range(2):
                k = len(up_t)
                slot = 2 * kind + oc % 2
                wslb = wsl[slot]
                wt = w1t if kind == 0 else w2t

                @block.gpsimd
                def _(g, oc=oc, k=k, wslb=wslb, wt=wt, slot=slot):
                    if k >= 4:
                        g.wait_ge(s_t.h, up_t[k - 4])
                    swu[slot].dma(g.dma_start(out=wslb[:, :], in_=wt[oc]))
                ld_sem, ld_cnt = swu[slot].h, swu[slot].n

                @block.tensor
                def _(t, k=k, wslb=wslb, tok0=tok0, ld_sem=ld_sem, ld_cnt=ld_cnt):
                    t.wait_ge(ld_sem, ld_cnt)
                    if k == 0:
                        t.wait_ge(s_v.h, ev_done[-1])
                    if k >= 4:
                        t.wait_ge(s_a.h, up_a[k - 4])
                    for c in range(8):
                        ins = t.matmul(
                            ps_u[k % 4][:],
                            wslb[:, c * 128:(c + 1) * 128],
                            fm(xnf)[:, c, tok0:tok0 + 512],
                            start=(c == 0), stop=(c == 7))
                    s_t.inc(ins)
                    up_t.append(s_t.n)

                @block.scalar
                def _(s, oc=oc, kind=kind, k=k):
                    s_t.wait(s)
                    if len(up_p) >= 1:
                        s.wait_ge(s_p.h, up_p[-1])
                    if kind == 0:
                        ins = s.activation(out=usb[0][:], in_=ps_u[k % 4][:],
                                           func=AF.Silu, bias=0.0,
                                           scale=scl[:, oc:oc + 1])
                    else:
                        ins = s.activation(out=usb[1][:], in_=ps_u[k % 4][:],
                                           func=AF.Copy, bias=0.0,
                                           scale=scl[:, 32 + oc:33 + oc])
                    s_a.inc(ins)
                    up_a.append(s_a.n)

            @block.gpsimd
            def _(g, oc=oc):
                g.wait_ge(s_a.h, up_a[-1])
                ins = g.tensor_mul(hbuf[:, oc * 512:(oc + 1) * 512],
                                   usb[0][:], usb[1][:])
                s_p.inc(ins)
                up_p.append(s_p.n)

        for phase, (srcb, dst, wt, soff) in enumerate(
                [(hbuf[:], h1o, h1t, 64), (h1o, hbuf[:], h2t, 96)]):
            for oc in range(32):
                hk = len(hp_t)

                @block.gpsimd
                def _(g, oc=oc, wt=wt, hk=hk):
                    if hk >= 2:
                        g.wait_ge(s_t.h, hp_t[hk - 2])
                    swh[oc % 2].dma(g.dma_start(out=hsl[oc % 2][:, :], in_=wt[oc]))
                ld_sem, ld_cnt = swh[oc % 2].h, swh[oc % 2].n

                @block.tensor
                def _(t, oc=oc, phase=phase, srcb=srcb, hk=hk,
                      ld_sem=ld_sem, ld_cnt=ld_cnt):
                    t.wait_ge(ld_sem, ld_cnt)
                    if oc == 0:
                        if phase == 0:
                            s_p.wait(t)
                        else:
                            s_a.wait(t)
                    if hk >= 4:
                        t.wait_ge(s_a.h, hp_a[hk - 4])
                    for c in range(32):
                        ins = t.matmul(
                            ps_u[hk % 4][:],
                            hsl[oc % 2][:, c * 128:(c + 1) * 128],
                            srcb[:, c * 512:(c + 1) * 512],
                            start=(c == 0), stop=(c == 31))
                    s_t.inc(ins)
                    hp_t.append(s_t.n)

                @block.scalar
                def _(s, oc=oc, dst=dst, soff=soff, hk=hk):
                    s_t.wait(s)
                    ins = s.activation(
                        out=dst[:, oc * 512:(oc + 1) * 512],
                        in_=ps_u[hk % 4][:], func=AF.Gelu,
                        bias=0.0, scale=scl[:, soff + oc:soff + oc + 1])
                    s_a.inc(ins)
                    hp_a.append(s_a.n)

        @block.gpsimd
        def _(g):
            s_a.wait(g)
            for c0 in range(0, 32, 8):
                sw2.dma(g.dma_start(
                    out=w3sb[:, c0 * 1024:(c0 + 8) * 1024].rearrange(
                        "p (c o) -> p c o", c=8),
                    in_=w3t[c0:c0 + 8].rearrange("c p o -> p c o")))
        w3_loaded = sw2.n

        @block.vector
        def _(v):
            sw2.wait(v, w3_loaded)
            v.wait_ge(s_v.h, bc_done)
            for c in range(32):
                v.tensor_mul(w3sb[:, c * 1024:(c + 1) * 1024],
                             w3sb[:, c * 1024:(c + 1) * 1024], w3sb_bc[:])
                v.drain()
            s_v.inc(v.nop())
        w3_folded = s_v.n

        for tb in range(4):
            g_tb = grp * 4 + tb
            r0 = grp * 512 + tb * 128

            @block.sync
            def _(sync, r0=r0, g_tb=g_tb):
                if g_tb >= 2:
                    sync.wait_ge(s_p.h, ep_done[g_tb - 2])
                sld.dma(sync.dma_start(out=x1e[g_tb % 2][:],
                                       in_=x1[r0:r0 + 128, :]))

            @block.tensor
            def _(t, tb=tb, g_tb=g_tb, w3_folded=w3_folded):
                t.wait_ge(s_v.h, w3_folded)
                if tb == 0:
                    s_a.wait(t)
                if g_tb >= 1:
                    t.wait_ge(s_v.h, ep_ln[g_tb - 1])
                for c in range(32):
                    for nh in range(2):
                        ins = t.matmul(
                            ps_o[:, nh * 512:(nh + 1) * 512],
                            hbuf[:, c * 512 + tb * 128:c * 512 + tb * 128 + 128],
                            w3sb[:, c * 1024 + nh * 512:c * 1024 + nh * 512 + 512],
                            start=(c == 0), stop=(c == 31))
                s_t.inc(ins)

            @block.vector
            def _(v):
                s_t.wait(v)

            emit_ln_stats(block, ps_o[:], stat, sq[:], s_v, s_a)

            @block.vector
            def _(v, g_tb=g_tb):
                if g_tb >= 1:
                    v.wait_ge(s_p.h, ep_done[g_tb - 1])
                v.tensor_scalar(t1[:], ps_o[:], stat[:, 0:1], stat[:, 3:4],
                                ALU.add, ALU.mult)
                s_v.inc(v.drain())
                ep_ln.append(s_v.n)

            @block.gpsimd
            def _(g, g_tb=g_tb):
                s_v.wait(g)
                sld.wait(g)
                if g_tb >= 2:
                    g.wait_ge(sst.h, st_cnt[g_tb - 2])
                ins = g.tensor_add(out_sb[g_tb % 2][:], t1[:], x1e[g_tb % 2][:])
                s_p.inc(ins)
                ep_done.append(s_p.n)

            @block.sync
            def _(sync, r0=r0, g_tb=g_tb):
                sync.wait_ge(s_p.h, ep_done[g_tb])
                sst.dma(sync.dma_start(out=x2[r0:r0 + 128, :],
                                       in_=out_sb[g_tb % 2][:]))
                st_cnt.append(sst.n)

    @block.sync
    def _(sync):
        sst.wait(sync)



def emit_sig(nc, ctx, block, io):
    x2 = io["x2"]; encq = io["encq"]; ident = io["ident"]
    cc_in = io["cc_in"]; cc_out = io["cc_out"]
    halo_d = io["halo_d"]; sig_d = io["sig_d"]
    wsel = io["wsel"]; hsel = io["hsel"]

    xnf = ctx.sbuf("xnf", [128, 8 * TCH], BF16)
    eq_sb = ctx.sbuf("eq_sb", [128, 8 * 1024], BF16)
    qbuf = ctx.sbuf("qbuf", [128, TCH], BF16)
    xf = [ctx.sbuf(f"xf{i}", [128, D], F32) for i in range(2)]
    xn_t = ctx.sbuf("xn_t", [128, D], BF16)
    identb = ctx.sbuf("identb", [128, 128], BF16)
    stat = ctx.sbuf("stat", [128, 8], F32)
    sq = ctx.sbuf("sq", [128, D], F32)
    parts_sb = ctx.sbuf("parts_sb", [128, 16], F32)
    pg = ctx.sbuf("pg", [128, 8 * 16], F32)
    hg = ctx.sbuf("hg", [128, 8 * 1024], F32)
    halo_sb = ctx.sbuf("halo_sb", [128, D], F32)
    sig_col = ctx.sbuf("sig_col", [128, 8], F32)
    wsel_sb = ctx.sbuf("wsel_sb", [128, 16], F32)
    hsel_sb = ctx.sbuf("hsel_sb", [128, 8], F32)
    ps_t = ctx.psum("ps_t", [128, 1024], BF16)
    ps_a = [ctx.psum(f"ps_a{i}", [128, 512], F32) for i in range(2)]

    sld = ctx.sem("sld")
    swl = ctx.sem("swl")
    s_t = ctx.sem("s_t")
    s_a = ctx.sem("s_a")
    s_v = ctx.sem("s_v")
    ssc = ctx.sem("ssc")
    scc = ctx.sem("scc")
    swl_base = swl.n

    @block.gpsimd
    def _(g):
        swl.dma(g.dma_start(out=identb[:], in_=ident[:]))
        swl.dma(g.dma_start(out=wsel_sb[:], in_=wsel[:]))
        swl.dma(g.dma_start(out=hsel_sb[:], in_=hsel[:]))
        for n in range(8):
            swl.dma(g.dma_start(
                out=eq_sb[:, :].rearrange("p (c u) -> p c u", c=8)[:, :, n * 128:(n + 1) * 128],
                in_=encq[n].rearrange("(c p) u -> p c u", p=128)))
    consts_w = swl.n

    ln_done = []
    ev_done = []
    for i in range(NBLK):
        @block.sync
        def _(sync, i=i):
            if i >= 2:
                sync.wait_ge(s_v.h, ln_done[i - 2])
            sld.dma(sync.dma_start(out=xf[i % 2][:], in_=x2[i * 128:(i + 1) * 128, :]))

        @block.vector
        def _(v, i=i):
            sld.wait(v)

        emit_ln_stats(block, xf[i % 2][:], stat, sq[:], s_v, s_a)

        @block.vector
        def _(v, i=i):
            if i >= 1:
                v.wait_ge(s_v.h, ev_done[i - 1])
            v.tensor_scalar(xn_t[:], xf[i % 2][:], stat[:, 0:1], stat[:, 3:4],
                            ALU.add, ALU.mult)
            s_v.inc(v.drain())
            ln_done.append(s_v.n)

        @block.tensor
        def _(t, i=i):
            swl.wait(t, swl_base + 16)
            s_v.wait(t)
            for c in range(8):
                ins = t.transpose(ps_t[:, c * 128:(c + 1) * 128],
                                  xn_t[:, c * 128:(c + 1) * 128], identb[:])
            s_t.inc(ins)

        @block.vector
        def _(v, i=i):
            s_t.wait(v)
            v.tensor_copy(fm(xnf)[:, :, i * 128:(i + 1) * 128], fm(ps_t)[:, :, :])
            s_v.inc(v.drain())
            ev_done.append(s_v.n)

    @block.sync
    def _(sync):
        sync.wait_ge(s_v.h, ln_done[15])
        ssc.dma(sync.dma_start(out=cc_in[:, 16:1040], in_=xf[1][:]))

    qp_t = []
    qp_a = []
    qr_done = []
    for n in range(8):
        for g in range(4):
            kk = len(qp_t)

            @block.tensor
            def _(t, n=n, g=g, kk=kk):
                swl.wait(t, consts_w)
                if kk == 0:
                    t.wait_ge(s_v.h, ev_done[-1])
                if kk >= 2:
                    t.wait_ge(s_a.h, qp_a[kk - 2])
                for c in range(8):
                    ins = t.matmul(ps_a[kk % 2][:],
                                   eq_sb[:, c * 1024 + n * 128:c * 1024 + (n + 1) * 128],
                                   fm(xnf)[:, c, g * 512:(g + 1) * 512],
                                   start=(c == 0), stop=(c == 7))
                s_t.inc(ins)
                qp_t.append(s_t.n)

            @block.scalar
            def _(s, n=n, g=g, kk=kk):
                s_t.wait(s)
                if g == 0 and qr_done:
                    s.wait_ge(s_v.h, qr_done[-1])
                ins = s.activation(out=qbuf[:, g * 512:(g + 1) * 512],
                                   in_=ps_a[kk % 2][:], func=AF.Relu,
                                   bias=0.0, scale=1.0)
                s_a.inc(ins)
                qp_a.append(s_a.n)

        @block.vector
        def _(v, n=n):
            s_a.wait(v)
            v.reduce_sum(parts_sb[:, n:n + 1], qbuf[:, :], axis=AX.X)
            v.drain()
            v.reduce_sum(parts_sb[:, 8 + n:9 + n],
                         qbuf[:, :].rearrange("p (a b) -> p b a", b=8)[:, 0:1, :],
                         axis=AX.X)
            s_v.inc(v.drain())
            qr_done.append(s_v.n)

    @block.sync
    def _(sync):
        sync.wait_ge(s_v.h, qr_done[-1])
        ssc.dma(sync.dma_start(out=cc_in[:, 0:16], in_=parts_sb[:]))

    @block.gpsimd
    def _(g):
        g.wait_ge(ssc.h, ssc.n)
        ins = g.collective_compute(
            "AllGather", ALU.bypass,
            replica_groups=[list(range(N_CORES))],
            ins=[io["cc_in_h"].ap().opt()],
            outs=[io["cc_out_h"].ap().opt()])
        scc.inc(ins, 1)

    @block.gpsimd
    def _(g):
        g.wait_ge(scc.h, 1)
        swl.dma(g.dma_start(
            out=pg[:, :].rearrange("p (r j) -> p r j", r=8),
            in_=cc_out.rearrange("(r p) c -> p r c", p=128)[:, :, 0:16]))
        swl.dma(g.dma_start(
            out=hg[:, :].rearrange("p (r j) -> p r j", r=8),
            in_=cc_out.rearrange("(r p) c -> p r c", p=128)[:, :, 16:1040]))
    cc_loaded = swl.n

    @block.vector
    def _(v):
        v.wait_ge(swl.h, cc_loaded)
        v.memset(halo_sb[:], 0.0)
        v.drain()
        for r in range(8):
            v.scalar_tensor_tensor(
                halo_sb[:], hg[:, r * 1024:(r + 1) * 1024],
                hsel_sb[:, r:r + 1], halo_sb[:],
                op0=ALU.mult, op1=ALU.add)
            v.drain()
        v.memset(sig_col[:], 0.0)
        v.drain()
        for r in range(8):
            for k in range(2):
                j0 = r * 16 + k * 8
                v.scalar_tensor_tensor(
                    sig_col[:], pg[:, j0:j0 + 8],
                    wsel_sb[:, r * 2 + k:r * 2 + k + 1], sig_col[:],
                    op0=ALU.mult, op1=ALU.add)
                v.drain()
        s_v.inc(v.nop())
    comb_done = s_v.n

    @block.sync
    def _(sync):
        sync.wait_ge(s_v.h, comb_done)
        ssc.dma(sync.dma_start(out=halo_d[:], in_=halo_sb[:]))
        ssc.dma(sync.dma_start(out=sig_d[:], in_=sig_col[:]))
        ssc.wait(sync)



def emit_l1attn(nc, ctx, block, io):
    x2 = io["x2"]; x3 = io["x3"]
    encq = io["encq"]; encv = io["encv"]; dec = io["dec"]; heb = io["heb"]
    mw1 = io["mw1"]; mw2 = io["mw2"]; mb1 = io["mb1"]; mb2 = io["mb2"]
    base = io["base"]; band = io["band"]; ident = io["ident"]
    halo_d = io["halo_d"]; sig_d = io["sig_d"]

    TH = TCH + 128
    xnf = ctx.sbuf("xnf", [128, 8 * TH], BF16)
    zf = ctx.sbuf("zf", [128, 8 * TCH], BF16)
    onesf = ctx.sbuf("onesf", [128, 128], F32)
    onesc = ctx.sbuf("onesc", [128, 1], F32)
    identb = ctx.sbuf("identb", [128, 128], BF16)
    stat = ctx.sbuf("stat", [128, 8], F32)
    sq = ctx.sbuf("sq", [128, D], F32)
    xf = [ctx.sbuf(f"xf{i}", [128, D], F32) for i in range(2)]
    xn_t = ctx.sbuf("xn_t", [128, D], BF16)
    t1 = ctx.sbuf("t1", [128, D], F32)
    out_sb = [ctx.sbuf(f"out_sb{i}", [128, D], F32) for i in range(2)]
    x2e = [ctx.sbuf(f"x2e{i}", [128, D], F32) for i in range(2)]
    es_S = contextlib.ExitStack()
    ctxS = ctx.sub(es_S, "S")
    eq_sb = ctxS.sbuf("eq_sb", [128, 8 * 1024], BF16)
    ev_sb = ctxS.sbuf("ev_sb", [128, 8 * 1024], BF16)
    heb_sb = ctxS.sbuf("heb_sb", [128, 8 * 128], BF16)
    qb_buf = ctxS.sbuf("qb_buf", [128, TH], BF16)
    qt_buf = ctxS.sbuf("qt_buf", [128, TH], BF16)
    v_buf = ctxS.sbuf("v_buf", [128, TH], BF16)
    sq512 = ctxS.sbuf("sq512", [128, 512], F32)
    pt_sb = ctxS.sbuf("pt_sb", [128, 256], BF16)
    band_sb = ctxS.sbuf("band_sb", [128, 256], F32)
    rows_t = ctxS.sbuf("rows_t", [128, 6144], F32)
    sig_col = ctxS.sbuf("sig_col", [128, 16], F32)
    sig_colb = ctxS.sbuf("sig_colb", [128, 16], BF16)
    sqm_pp = ctxS.sbuf("sqm_pp", [128, 8], F32)
    wtp = ctxS.sbuf("wtp", [128, 16], F32)
    denb = ctxS.sbuf("denb", [128, 16], F32)
    rden = ctxS.sbuf("rden", [128, 16], F32)
    hb_sb = ctxS.sbuf("hb_sb", [128, 128], F32)
    t2_sb = ctxS.sbuf("t2_sb", [128, 128], F32)
    t3_sb = ctxS.sbuf("t3_sb", [128, 128], F32)

    ps_t = ctx.psum("ps_t", [128, 1024], BF16)
    ps_a = ctx.psum("ps_a", [128, 512], F32)
    ps_b = ctx.psum("ps_b", [128, 512], F32)
    ps_tp = ctx.psum("ps_tp", [128, 256], BF16)
    ps_y = ctx.psum("ps_y", [128, 128], F32)
    ps_ctx = ctx.psum("ps_ctx", [128, 128], F32)
    ps_o = ctx.psum("ps_o", [128, 1024], F32)

    sld = ctx.sem("sld")
    swl = ctx.sem("swl")
    s_t = ctx.sem("s_t")
    s_a = ctx.sem("s_a")
    s_v = ctx.sem("s_v")
    s_p = ctx.sem("s_p")
    sst = ctx.sem("sst")
    scc = ctx.sem("scc")
    swl_base = swl.n

    es_mw = contextlib.ExitStack()
    ctxM = ctx.sub(es_mw, "M")
    mw1_sb = ctxM.sbuf("mw1_sb", [128, 8 * 512], BF16)
    mw2_sb = ctxM.sbuf("mw2_sb", [128, 4 * 1024], BF16)

    @block.gpsimd
    def _(g):
        swl.dma(g.dma_start(out=identb[:], in_=ident[:]))
        swl.dma(g.dma_start(out=band_sb[:], in_=band[:]))
        swl.dma(g.dma_start(out=rows_t[64:65, 2560:3072], in_=mb1[:]))
        swl.dma(g.dma_start(out=rows_t[64:65, 3072:4096], in_=mb2[:]))
        swl.dma(g.dma_start(out=rows_t[64:65, 4096:5120], in_=base[:]))
        swl.dma(g.dma_start(out=sig_col[:, 0:8], in_=sig_d[:]))
        swl.dma(g.dma_start(out=sig_colb[:, 0:8], in_=sig_d[:]))
        for n in range(8):
            swl.dma(g.dma_start(
                out=eq_sb[:, :].rearrange("p (c u) -> p c u", c=8)[:, :, n * 128:(n + 1) * 128],
                in_=encq[n].rearrange("(c p) u -> p c u", p=128)))
            swl.dma(g.dma_start(
                out=ev_sb[:, :].rearrange("p (c u) -> p c u", c=8)[:, :, n * 128:(n + 1) * 128],
                in_=encv[n].rearrange("(c p) u -> p c u", p=128)))
            swl.dma(g.dma_start(out=heb_sb[:, n * 128:(n + 1) * 128], in_=heb[n]))
            swl.dma(g.dma_start(out=fm(mw1_sb)[:, n, :], in_=mw1[n]))
        for c in range(4):
            swl.dma(g.dma_start(out=fm(mw2_sb, c=4)[:, c, :], in_=mw2[c]))
    consts_w = swl.n

    @block.vector
    def _(v):
        v.memset(onesf[:], 1.0)
        v.drain()
        v.memset(onesc[:], 1.0)
        s_v.inc(v.drain())

    ln_done = []
    ev_done = []
    for i in range(17):
        @block.sync
        def _(sync, i=i):
            if i >= 2:
                sync.wait_ge(s_v.h, ln_done[i - 2])
            if i == 0:
                sld.dma(sync.dma_start(out=xf[0][:], in_=halo_d[:]))
            else:
                sld.dma(sync.dma_start(out=xf[i % 2][:],
                                       in_=x2[(i - 1) * 128:i * 128, :]))

        @block.vector
        def _(v):
            sld.wait(v)

        emit_ln_stats(block, xf[i % 2][:], stat, sq[:], s_v, s_a)

        @block.vector
        def _(v, i=i):
            if i >= 1:
                v.wait_ge(s_v.h, ev_done[i - 1])
            v.tensor_scalar(xn_t[:], xf[i % 2][:], stat[:, 0:1], stat[:, 3:4],
                            ALU.add, ALU.mult)
            s_v.inc(v.drain())
            ln_done.append(s_v.n)

        @block.tensor
        def _(t, i=i):
            swl.wait(t, swl_base + 16)
            s_v.wait(t)
            for c in range(8):
                ins = t.transpose(ps_t[:, c * 128:(c + 1) * 128],
                                  xn_t[:, c * 128:(c + 1) * 128], identb[:])
            s_t.inc(ins)

        @block.vector
        def _(v, i=i):
            s_t.wait(v)
            v.tensor_copy(fm(xnf)[:, :, i * 128:(i + 1) * 128], fm(ps_t)[:, :, :])
            s_v.inc(v.drain())
            ev_done.append(s_v.n)

    @block.tensor
    def _(t):
        swl.wait(t, consts_w)
        for c in range(8):
            ins = t.matmul(ps_b[0:1, :], sig_colb[:, c:c + 1],
                           mw1_sb[:, c * 512:(c + 1) * 512],
                           start=(c == 0), stop=(c == 7))
        s_t.inc(ins)

    @block.vector
    def _(v):
        s_t.wait(v)
        v.tensor_add(rows_t[64:65, 1024:1536], ps_b[0:1, :], rows_t[64:65, 2560:3072])
        s_v.inc(v.drain())

    @block.scalar
    def _(s):
        s_v.wait(s)
        s.activation(out=rows_t[64:65, 1024:1536], in_=rows_t[64:65, 1024:1536],
                     func=AF.Gelu, bias=0.0, scale=1.0)
        s_a.inc(s.drain())

    @block.tensor
    def _(t):
        s_a.wait(t)
        for c in range(4):
            ins = t.transpose(ps_y[:, 8 + c:9 + c],
                              rows_t[64:65, 1024 + c * 128:1024 + (c + 1) * 128],
                              onesf[64:65, 64:65])
        s_t.inc(ins)

    @block.vector
    def _(v):
        s_t.wait(v)
        v.tensor_copy(sig_col[:, 8:12], ps_y[:, 8:12])
        v.drain()
        v.tensor_copy(sig_colb[:, 8:12], sig_col[:, 8:12])
        s_v.inc(v.drain())

    @block.tensor
    def _(t):
        s_v.wait(t)
        for nh in range(2):
            for c in range(4):
                ins = t.matmul(ps_o[0:1, nh * 512:(nh + 1) * 512],
                               sig_colb[:, 8 + c:9 + c],
                               mw2_sb[:, c * 1024 + nh * 512:c * 1024 + nh * 512 + 512],
                               start=(c == 0), stop=(c == 3))
        s_t.inc(ins)

    @block.vector
    def _(v):
        s_t.wait(v)
        v.tensor_add(rows_t[64:65, 1536:2560], ps_o[0:1, :], rows_t[64:65, 3072:4096])
        v.drain()
        v.scalar_tensor_tensor(rows_t[64:65, 1536:2560], rows_t[64:65, 1536:2560],
                               0.1, rows_t[64:65, 4096:5120], op0=ALU.mult, op1=ALU.add)
        s_v.inc(v.drain())

    @block.scalar
    def _(s):
        s_v.wait(s)
        s.activation(out=rows_t[64:65, 5120:6144], in_=rows_t[64:65, 1536:2560],
                     func=AF.Abs, bias=0.0, scale=1.0)
        s.drain()
        s.activation(out=rows_t[64:65, 5120:6144], in_=rows_t[64:65, 5120:6144],
                     func=AF.Exp, bias=0.0, scale=-1.0)
        s.drain()
        s.activation(out=rows_t[64:65, 5120:6144], in_=rows_t[64:65, 5120:6144],
                     func=AF.Ln, bias=onesf[64:65, 64:65], scale=1.0)
        s_a.inc(s.drain())

    @block.vector
    def _(v):
        s_a.wait(v)
        v.tensor_scalar_max(rows_t[64:65, 1536:2560], rows_t[64:65, 1536:2560], 0.0)
        v.drain()
        v.tensor_add(rows_t[64:65, 1536:2560], rows_t[64:65, 1536:2560],
                     rows_t[64:65, 5120:6144])
        v.drain()
        v.tensor_scalar_add(rows_t[64:65, 1536:2560], rows_t[64:65, 1536:2560], 1e-6)
        s_v.inc(v.drain())

    @block.scalar
    def _(s):
        s_v.wait(s)
        s.activation(out=rows_t[64:65, 1536:2560], in_=rows_t[64:65, 1536:2560],
                     func=AF.Sqrt, bias=0.0, scale=1.0)
        s_a.inc(s.drain())

    @block.tensor
    def _(t):
        s_a.wait(t)
        for c in range(8):
            ins = t.transpose(ps_y[:, c:c + 1],
                              rows_t[64:65, 1536 + c * 128:1536 + (c + 1) * 128],
                              onesf[64:65, 64:65])
        s_t.inc(ins)

    @block.vector
    def _(v):
        s_t.wait(v)
        v.tensor_copy(sqm_pp[:, 0:8], ps_y[:, 0:8])
        s_v.inc(v.drain())
    sqm_done = s_v.n

    es_mw.close()
    dpre = ctxS.sbuf("dpre", [128, 16 * 256], F32)
    pbuf = ctxS.sbuf("pbuf", [128, 16 * 256], BF16)
    gbounds = [(0, 512), (512, 512), (1024, 512), (1536, 512), (2048, 128)]
    nq_t = []
    nq_a = []
    n_zf = []

    def ones_row(n, src_sbuf, dst_p):
        for gi, (t0_, tw) in enumerate(gbounds):
            kk = len(nq_t)

            @block.scalar
            def _(s, t0_=t0_, tw=tw, kk=kk):
                if nq_t:
                    s.wait_ge(s_t.h, nq_t[-1])
                ins = s.activation(out=sq512[:, 0:tw], in_=src_sbuf[:, t0_:t0_ + tw],
                                   func=AF.Square, bias=0.0, scale=1.0)
                s_a.inc(ins)
                sq_a = s_a.n

                @block.tensor
                def _(t, kk=kk, tw=tw, sq_a=sq_a):
                    t.wait_ge(s_a.h, sq_a)
                    if kk >= 2:
                        t.wait_ge(s_a.h, nq_a[kk - 2])
                    psx = [ps_a, ps_b][kk % 2]
                    ins = t.matmul(psx[0:1, 0:tw], onesc[:], sq512[:, 0:tw],
                                   start=True, stop=True)
                    s_t.inc(ins)
                    nq_t.append(s_t.n)

            @block.scalar
            def _(s, t0_=t0_, tw=tw, kk=kk, dst_p=dst_p):
                s.wait_ge(s_t.h, nq_t[kk])
                psx = [ps_a, ps_b][kk % 2]
                ins = s.activation(out=rows_t[dst_p:dst_p + 1, t0_:t0_ + tw],
                                   in_=psx[0:1, 0:tw], func=AF.Copy,
                                   bias=0.0, scale=1.0)
                s_a.inc(ins)
                nq_a.append(s_a.n)

    for n in range(8):
        for gi, (t0_, tw) in enumerate(gbounds):
            kk = len(nq_t)

            @block.tensor
            def _(t, n=n, t0_=t0_, tw=tw, kk=kk):
                if kk == 0:
                    t.wait_ge(s_v.h, ev_done[-1])
                    swl.wait(t, consts_w)
                if kk >= 2:
                    t.wait_ge(s_a.h, nq_a[kk - 2])
                psx = [ps_a, ps_b][kk % 2]
                for c in range(8):
                    ins = t.matmul(psx[:, 0:tw],
                                   eq_sb[:, c * 1024 + n * 128:c * 1024 + (n + 1) * 128],
                                   fm(xnf)[:, c, t0_:t0_ + tw],
                                   start=(c == 0), stop=(c == 7))
                s_t.inc(ins)
                nq_t.append(s_t.n)

            @block.scalar
            def _(s, n=n, gi=gi, t0_=t0_, tw=tw, kk=kk):
                s.wait_ge(s_t.h, nq_t[kk])
                if gi == 0 and n_zf:
                    s.wait_ge(s_v.h, n_zf[-1])
                psx = [ps_a, ps_b][kk % 2]
                ins = s.activation(out=qb_buf[:, t0_:t0_ + tw], in_=psx[:, 0:tw],
                                   func=AF.Relu, bias=0.0, scale=1.0)
                s_a.inc(ins)
                nq_a.append(s_a.n)

        for tb in range(17):
            kk = len(nq_t)

            @block.tensor
            def _(t, n=n, tb=tb, kk=kk):
                if kk >= 2:
                    t.wait_ge(s_a.h, nq_a[kk - 2])
                psx = [ps_a, ps_b][kk % 2]
                for c in range(8):
                    ins = t.matmul(psx[:, 0:128],
                                   fm(xnf)[:, c, tb * 128:(tb + 1) * 128],
                                   ev_sb[:, c * 1024 + n * 128:c * 1024 + (n + 1) * 128],
                                   start=(c == 0), stop=(c == 7))
                s_t.inc(ins)
                nq_t.append(s_t.n)

            @block.scalar
            def _(s, n=n, tb=tb, kk=kk):
                s.wait_ge(s_t.h, nq_t[kk])
                psx = [ps_a, ps_b][kk % 2]
                ins = s.activation(out=v_buf[:, tb * 128:(tb + 1) * 128],
                                   in_=psx[:, 0:128],
                                   func=AF.Relu, bias=0.0, scale=1.0)
                s_a.inc(ins)
                nq_a.append(s_a.n)

        @block.vector
        def _(v, n=n):
            s_a.wait(v)
            v.wait_ge(s_v.h, sqm_done)
            v.tensor_scalar_mul(qt_buf[:], qb_buf[:], sqm_pp[:, n:n + 1])
            s_v.inc(v.drain())

        @block.scalar
        def _(s, n=n):
            s_v.wait(s)
        ones_row(n, qt_buf[:, :], 0)
        ones_row(n, qb_buf[:, :], 32)

        @block.scalar
        def _(s):
            s.wait_ge(s_a.h, nq_a[-1])
            s.activation(out=rows_t[32:33, 0:TH],
                         in_=rows_t[32:33, 0:TH], func=AF.Sqrt,
                         bias=0.0, scale=1.0)
            s_a.inc(s.drain())

        @block.vector
        def _(v):
            s_a.wait(v)
            v.tensor_scalar_max(rows_t[32:33, 0:TH],
                                rows_t[32:33, 0:TH], 1e-12)
            v.drain()
            v.reciprocal(rows_t[32:33, 0:TH], rows_t[32:33, 0:TH])
            s_v.inc(v.drain())
        rnorm_done = s_v.n

        @block.tensor
        def _(t, n=n):
            s_v.wait(t)
            for qb in range(16):
                ins = t.transpose(ps_y[:, qb:qb + 1],
                                  rows_t[0:1, (qb + 1) * 128:(qb + 2) * 128],
                                  onesf[0:1, 0:1])
            s_t.inc(ins)

        @block.vector
        def _(v):
            s_t.wait(v)
            v.tensor_scalar_add(wtp[:, 0:16], ps_y[:, 0:16], 1e-8)
            s_v.inc(v.drain())
        wtp_done = s_v.n

        al_t = []
        al_v = []
        for qb in range(16):
            @block.tensor
            def _(t, n=n, qb=qb):
                t.wait_ge(s_v.h, wtp_done)
                if qb >= 2:
                    t.wait_ge(s_v.h, al_v[qb - 2])
                psw = [ps_a, ps_b][qb % 2]
                t.matmul(psw[0:128, 0:256], onesf[0:1, 0:128],
                         rows_t[0:1, qb * 128:qb * 128 + 256],
                         start=True, stop=True)
                ins = t.matmul(psw[0:128, 256:512],
                               qt_buf[:, (qb + 1) * 128:(qb + 2) * 128],
                               qt_buf[:, qb * 128:qb * 128 + 256],
                               start=True, stop=True)
                s_t.inc(ins)
                al_t.append(s_t.n)

            @block.vector
            def _(v, n=n, qb=qb):
                v.wait_ge(s_t.h, al_t[qb])
                psw = [ps_a, ps_b][qb % 2]
                v.tensor_add(dpre[:, qb * 256:(qb + 1) * 256],
                             psw[0:128, 0:256], band_sb[:])
                v.drain()
                v.scalar_tensor_tensor(dpre[:, qb * 256:(qb + 1) * 256],
                                       psw[0:128, 256:512], -2.0,
                                       dpre[:, qb * 256:(qb + 1) * 256],
                                       op0=ALU.mult, op1=ALU.add)
                v.drain()
                v.tensor_scalar(dpre[:, qb * 256:(qb + 1) * 256],
                                dpre[:, qb * 256:(qb + 1) * 256],
                                wtp[:, qb:qb + 1], 1e-8, ALU.add, ALU.max)
                s_v.inc(v.drain())
                al_v.append(s_v.n)

        @block.scalar
        def _(s, n=n):
            s_v.wait(s)
            for qb in range(16):
                ins = s.activation(out=dpre[:, qb * 256:(qb + 1) * 256],
                                   in_=dpre[:, qb * 256:(qb + 1) * 256],
                                   func=AF.Sqrt, bias=0.0, scale=1.0)
            s.drain()
            for qb in range(16):
                ins = s.activation(out=pbuf[:, qb * 256:(qb + 1) * 256],
                                   in_=dpre[:, qb * 256:(qb + 1) * 256],
                                   func=AF.Exp, bias=0.0, scale=-1.0,
                                   accum_out=denb[:, qb:qb + 1])
            s_a.inc(s.drain())

        @block.vector
        def _(v, n=n):
            s_a.wait(v)
            v.tensor_scalar_add(rden[:, 0:16], denb[:, 0:16], 1e-8)
            v.drain()
            v.reciprocal(rden[:, 0:16], rden[:, 0:16])
            s_v.inc(v.drain())
        rden_done = s_v.n

        for qb in range(16):
            @block.vector
            def _(v, n=n, qb=qb):
                v.wait_ge(s_v.h, rden_done)
                v.tensor_scalar_mul(pbuf[:, qb * 256:(qb + 1) * 256],
                                    pbuf[:, qb * 256:(qb + 1) * 256],
                                    rden[:, qb:qb + 1])
                s_v.inc(v.drain())
            pn_done = s_v.n

            @block.tensor
            def _(t, n=n, qb=qb):
                t.wait_ge(s_v.h, pn_done)
                for j in range(2):
                    ins = t.transpose(ps_tp[:, j * 128:(j + 1) * 128],
                                      pbuf[:, qb * 256 + j * 128:qb * 256 + (j + 1) * 128],
                                      identb[:])
                s_t.inc(ins)
            tp_done2 = s_t.n

            @block.vector
            def _(v, n=n, qb=qb):
                v.wait_ge(s_t.h, tp_done2)
                v.tensor_copy(pt_sb[:], ps_tp[:])
                s_v.inc(v.drain())
            pt_done = s_v.n

            @block.tensor
            def _(t, n=n, qb=qb):
                t.wait_ge(s_v.h, pt_done)
                for j in range(2):
                    t.matmul(ps_ctx[:], v_buf[:, (qb + j) * 128:(qb + j + 1) * 128],
                             pt_sb[:, j * 128:(j + 1) * 128],
                             start=(j == 0), stop=(j == 1))
                ins = t.matmul(ps_y[:], heb_sb[:, n * 128:(n + 1) * 128],
                               qb_buf[:, (qb + 1) * 128:(qb + 2) * 128],
                               start=True, stop=True)
                s_t.inc(ins)
            hc_done = s_t.n

            @block.scalar
            def _(s, n=n, qb=qb):
                s.wait_ge(s_t.h, hc_done)
                s.activation(out=hb_sb[:], in_=ps_y[:], func=AF.Copy,
                             bias=0.0, scale=1.0)
                s_a.inc(s.drain())
            hb_done = s_a.n

            @block.tensor
            def _(t, n=n, qb=qb):
                t.wait_ge(s_a.h, hb_done)
                t.wait_ge(s_v.h, rnorm_done)
                ins = t.matmul(ps_y[:], onesf[32:33, 0:128],
                               rows_t[32:33, (qb + 1) * 128:(qb + 2) * 128],
                               start=True, stop=True)
                s_t.inc(ins)
            rb_done = s_t.n

            @block.vector
            def _(v, n=n, qb=qb):
                v.wait_ge(s_t.h, rb_done)
                v.tensor_mul(t2_sb[:], hb_sb[:], ps_y[:])
                v.drain()
                v.tensor_add(t3_sb[:], t2_sb[:], ps_ctx[:])
                v.drain()
                v.tensor_mul(zf[:, n * TCH + qb * 128:n * TCH + (qb + 1) * 128],
                             t3_sb[:], qb_buf[:, (qb + 1) * 128:(qb + 2) * 128])
                s_v.inc(v.drain())
                n_zf.append(s_v.n)

    es_S.close()
    dec_sb = ctx.sbuf("dec_sb", [128, 8 * 1024], BF16)

    @block.gpsimd
    def _(g):
        g.wait_ge(s_v.h, n_zf[-1])
        for c in range(8):
            swl.dma(g.dma_start(out=fm(dec_sb)[:, c, :], in_=dec[c]))
    dec_loaded = swl.n

    ep_done = []
    ep_ln = []
    st_cnt = []
    for tb in range(NBLK):
        @block.sync
        def _(sync, tb=tb):
            if tb >= 2:
                sync.wait_ge(s_p.h, ep_done[tb - 2])
            sld.dma(sync.dma_start(out=x2e[tb % 2][:],
                                   in_=x2[tb * 128:(tb + 1) * 128, :]))

        @block.tensor
        def _(t, tb=tb):
            if tb == 0:
                t.wait_ge(s_v.h, n_zf[-1])
                t.wait_ge(swl.h, dec_loaded)
            if tb >= 1:
                t.wait_ge(s_v.h, ep_ln[tb - 1])
            for c in range(8):
                for nh in range(2):
                    ins = t.matmul(
                        ps_o[:, nh * 512:(nh + 1) * 512],
                        zf[:, c * TCH + tb * 128:c * TCH + (tb + 1) * 128],
                        dec_sb[:, c * 1024 + nh * 512:c * 1024 + nh * 512 + 512],
                        start=(c == 0), stop=(c == 7))
            s_t.inc(ins)

        @block.vector
        def _(v):
            s_t.wait(v)

        emit_ln_stats(block, ps_o[:], stat, sq[:], s_v, s_a)

        @block.vector
        def _(v, tb=tb):
            if tb >= 1:
                v.wait_ge(s_p.h, ep_done[tb - 1])
            v.tensor_scalar(t1[:], ps_o[:], stat[:, 0:1], stat[:, 3:4],
                            ALU.add, ALU.mult)
            s_v.inc(v.drain())
            ep_ln.append(s_v.n)

        @block.gpsimd
        def _(g, tb=tb):
            s_v.wait(g)
            sld.wait(g)
            if tb >= 2:
                g.wait_ge(sst.h, st_cnt[tb - 2])
            ins = g.tensor_add(out_sb[tb % 2][:], t1[:], x2e[tb % 2][:])
            s_p.inc(ins)
            ep_done.append(s_p.n)

        @block.sync
        def _(sync, tb=tb):
            sync.wait_ge(s_p.h, ep_done[tb])
            sst.dma(sync.dma_start(out=x3[tb * 128:(tb + 1) * 128, :],
                                   in_=out_sb[tb % 2][:]))
            st_cnt.append(sst.n)

    @block.sync
    def _(sync):
        sst.wait(sync)



def emit_l1ffn(nc, ctx, block, io):
    x3 = io["x3"]; x4 = io["x4"]; ident = io["ident"]
    wrt = io["wrt"]; br = io["br"]; w1e = io["w1e"]; b1l = io["b1l"]
    w2e = io["w2e"]; b2i = io["b2i"]; gam = io["gam"]; bet = io["bet"]

    xnf = ctx.sbuf("xnf", [128, 8 * TCH], BF16)
    ubuf = ctx.sbuf("ubuf", [128, 32 * 512], BF16)
    w2sb = ctx.sbuf("w2sb", [128, 32 * 1024], BF16)
    acc = ctx.sbuf("acc", [128, 4 * 1024], F32)
    stats_all = ctx.sbuf("stats_all", [128, 32], F32)
    rw_all = ctx.sbuf("rw_all", [128, 8 * 16], F32)
    rwT_all = ctx.sbuf("rwT_all", [8, 16 * 128], BF16)
    rwb = ctx.sbuf("rwb", [128, 8], BF16)
    wr_sb = ctx.sbuf("wr_sb", [128, 8 * 8], BF16)
    br_sb = ctx.sbuf("br_sb", [1, 8], BF16)
    b1_sb = ctx.sbuf("b1_sb", [128, 8 * 32], F32)
    b2_sb = ctx.sbuf("b2_sb", [8, 1024], BF16)
    gb_bc = ctx.sbuf("gb_bc", [128, 2048], F32)
    g_row = ctx.sbuf("g_row", [1, 1024], BF16)
    b_row = ctx.sbuf("b_row", [1, 1024], BF16)
    onesb = ctx.sbuf("onesb", [1, 128], BF16)
    wsl = [ctx.sbuf(f"wsl{i}", [128, 8 * 128], BF16) for i in range(2)]
    logit_sb = ctx.sbuf("logit_sb", [128, 8], F32)
    top_sb = ctx.sbuf("top_sb", [128, 8], F32)
    vv_sb = ctx.sbuf("vv_sb", [128, 4], F32)
    identb = ctx.sbuf("identb", [128, 128], BF16)
    stat = ctx.sbuf("stat", [128, 8], F32)
    sq = ctx.sbuf("sq", [128, D], F32)
    xf = [ctx.sbuf(f"xf{i}", [128, D], F32) for i in range(2)]
    xn_t = ctx.sbuf("xn_t", [128, D], BF16)
    t1 = ctx.sbuf("t1", [128, D], F32)
    t2 = ctx.sbuf("t2", [128, D], F32)
    inner = ctx.sbuf("inner", [128, D], F32)
    out_sb = [ctx.sbuf(f"out_sb{i}", [128, D], F32) for i in range(2)]
    x3e = xf

    ps_t = ctx.psum("ps_t", [128, 1024], BF16)
    ps_u = [ctx.psum(f"ps_u{i}", [128, 512], F32) for i in range(2)]
    ps_o = ctx.psum("ps_o", [128, 1024], F32)
    ps_r = ctx.psum("ps_r", [128, 8], F32)
    ps_rt = ctx.psum("ps_rt", [8, 128], BF16)

    sld = ctx.sem("sld")
    swl = ctx.sem("swl")
    sw2 = ctx.sem("sw2")
    swu = [ctx.sem(f"swu{i}") for i in range(2)]
    s_t = ctx.sem("s_t")
    s_a = ctx.sem("s_a")
    s_v = ctx.sem("s_v")
    s_p = ctx.sem("s_p")
    sst = ctx.sem("sst")
    swl_base = swl.n

    @block.gpsimd
    def _(g):
        swl.dma(g.dma_start(out=identb[:], in_=ident[:]))
        for c in range(8):
            swl.dma(g.dma_start(out=wr_sb[:, c * 8:(c + 1) * 8], in_=wrt[c]))
        swl.dma(g.dma_start(out=br_sb[:], in_=br[:]))
        swl.dma(g.dma_start(out=b1_sb[:], in_=b1l[:]))
        swl.dma(g.dma_start(out=b2_sb[:], in_=b2i[:]))
        swl.dma(g.dma_start(out=g_row[:], in_=gam[:]))
        swl.dma(g.dma_start(out=b_row[:], in_=bet[:]))
    consts_w = swl.n

    @block.vector
    def _(v):
        v.memset(onesb[:], 1.0)
        s_v.inc(v.drain())
    ones_done = s_v.n

    @block.tensor
    def _(t):
        swl.wait(t, consts_w)
        t.wait_ge(s_v.h, ones_done)
        for nh in range(2):
            ins = t.matmul(ps_o[:, nh * 512:(nh + 1) * 512], onesb[:],
                           g_row[0:1, nh * 512:(nh + 1) * 512],
                           start=True, stop=True)
        s_t.inc(ins)

    @block.vector
    def _(v):
        s_t.wait(v)
        v.tensor_copy(gb_bc[:, 0:1024], ps_o[:])
        s_v.inc(v.drain())

    @block.tensor
    def _(t):
        s_v.wait(t)
        for nh in range(2):
            ins = t.matmul(ps_o[:, nh * 512:(nh + 1) * 512], onesb[:],
                           b_row[0:1, nh * 512:(nh + 1) * 512],
                           start=True, stop=True)
        s_t.inc(ins)

    @block.vector
    def _(v):
        s_t.wait(v)
        v.tensor_copy(gb_bc[:, 1024:2048], ps_o[:])
        s_v.inc(v.drain())
    gb_done = s_v.n

    ln_done = []
    ev_done = []
    for i in range(NBLK):
        @block.sync
        def _(sync, i=i):
            if i >= 2:
                sync.wait_ge(s_v.h, ln_done[i - 2])
            sld.dma(sync.dma_start(out=xf[i % 2][:], in_=x3[i * 128:(i + 1) * 128, :]))

        @block.vector
        def _(v, i=i):
            sld.wait(v)

        emit_ln_stats(block, xf[i % 2][:], stat, sq[:], s_v, s_a)

        @block.vector
        def _(v, i=i):
            if i >= 1:
                v.wait_ge(s_v.h, ev_done[i - 1])
            v.tensor_copy(stats_all[:, 2 * i:2 * i + 1], stat[:, 0:1])
            v.drain()
            v.tensor_copy(stats_all[:, 2 * i + 1:2 * i + 2], stat[:, 3:4])
            v.drain()
            v.tensor_scalar(xn_t[:], xf[i % 2][:], stat[:, 0:1], stat[:, 3:4],
                            ALU.add, ALU.mult)
            s_v.inc(v.drain())
            ln_done.append(s_v.n)

        @block.tensor
        def _(t, i=i):
            swl.wait(t, swl_base + 16)
            s_v.wait(t)
            for c in range(8):
                ins = t.transpose(ps_t[:, c * 128:(c + 1) * 128],
                                  xn_t[:, c * 128:(c + 1) * 128], identb[:])
            s_t.inc(ins)

        @block.vector
        def _(v, i=i):
            s_t.wait(v)
            v.tensor_copy(fm(xnf)[:, :, i * 128:(i + 1) * 128], fm(ps_t)[:, :, :])
            s_v.inc(v.drain())
            ev_done.append(s_v.n)

    rt_done = []
    for i in range(NBLK):
        @block.tensor
        def _(t, i=i):
            if i == 0:
                t.wait_ge(s_v.h, ev_done[-1])
            if i >= 1:
                t.wait_ge(s_a.h, s_a.n)
            for c in range(8):
                t.matmul(ps_r[:], fm(xnf)[:, c, i * 128:(i + 1) * 128],
                         wr_sb[:, c * 8:(c + 1) * 8],
                         start=(c == 0), stop=False)
            ins = t.matmul(ps_r[:], onesb[:], br_sb[:], start=False, stop=True)
            s_t.inc(ins)

        @block.scalar
        def _(s, i=i):
            s_t.wait(s)
            ins = s.activation(out=logit_sb[:], in_=ps_r[:], func=AF.Copy,
                               bias=0.0, scale=1.0)
            s_a.inc(ins)

        @block.vector
        def _(v, i=i):
            s_a.wait(v)
            v.reduce_max(vv_sb[:, 0:1], logit_sb[:], axis=AX.X)
            v.drain()
            v.tensor_scalar(top_sb[:], logit_sb[:], vv_sb[:, 0:1], None,
                            ALU.is_equal)
            v.drain()
            v.scalar_tensor_tensor(top_sb[:], top_sb[:], -1e30, logit_sb[:],
                                   op0=ALU.mult, op1=ALU.add)
            v.drain()
            v.reduce_max(vv_sb[:, 1:2], top_sb[:], axis=AX.X)
            v.drain()
            v.tensor_sub(vv_sb[:, 2:3], vv_sb[:, 0:1], vv_sb[:, 1:2])
            s_v.inc(v.drain())

        @block.scalar
        def _(s, i=i):
            s_v.wait(s)
            s.activation(out=vv_sb[:, 3:4], in_=vv_sb[:, 2:3], func=AF.Sigmoid,
                         bias=0.0, scale=1.0)
            s_a.inc(s.drain())

        @block.vector
        def _(v, i=i):
            s_a.wait(v)
            rw = rw_all[:, i * 8:(i + 1) * 8]
            v.tensor_scalar(sq[:, 0:8], top_sb[:], vv_sb[:, 1:2], None,
                            ALU.is_equal)
            v.drain()
            v.tensor_scalar(sq[:, 8:16], logit_sb[:], vv_sb[:, 0:1], None,
                            ALU.is_equal)
            v.drain()
            v.tensor_scalar(vv_sb[:, 2:3], vv_sb[:, 3:4], -1.0, 1.0,
                            ALU.mult, ALU.add)
            v.drain()
            v.tensor_scalar_mul(rw, sq[:, 8:16], vv_sb[:, 3:4])
            v.drain()
            v.tensor_scalar_mul(sq[:, 0:8], sq[:, 0:8], vv_sb[:, 2:3])
            v.drain()
            v.tensor_add(rw, rw, sq[:, 0:8])
            v.drain()
            v.tensor_copy(rwb[:], rw)
            s_v.inc(v.drain())

        @block.tensor
        def _(t, i=i):
            s_v.wait(t)
            ins = t.transpose(ps_rt[:], rwb[:], identb[:])
            s_t.inc(ins)

        @block.vector
        def _(v, i=i):
            s_t.wait(v)
            v.tensor_copy(rwT_all[:, i * 128:(i + 1) * 128], ps_rt[0:8, :])
            s_v.inc(v.drain())
            rt_done.append(s_v.n)

    up_t = []
    up_a = []
    ob2_a = []
    oev_v = []
    ep_in = []
    ep_done = []
    ep_ln = []
    st_cnt = []
    w2l_cnt = []

    for grp in range(L1FFN_GROUPS):
        for tb in range(4):
            g_tb = grp * 4 + tb

            @block.tensor
            def _(t, grp=grp, tb=tb, g_tb=g_tb):
                if g_tb == 0:
                    t.wait_ge(s_v.h, rt_done[-1])
                if tb == 0 and oev_v:
                    t.wait_ge(s_v.h, oev_v[-1])
                if tb >= 1:
                    t.wait_ge(s_a.h, ob2_a[-1])
                for nh in range(2):
                    ins = t.matmul(
                        ps_o[:, nh * 512:(nh + 1) * 512],
                        rwT_all[:, g_tb * 128:(g_tb + 1) * 128],
                        b2_sb[:, nh * 512:(nh + 1) * 512],
                        start=True, stop=True)
                s_t.inc(ins)

            @block.scalar
            def _(s, grp=grp, tb=tb, g_tb=g_tb):
                s_t.wait(s)
                if len(ep_in) >= (grp - 1) * 4 + tb + 1 and grp >= 1:
                    s.wait_ge(s_v.h, ep_in[(grp - 1) * 4 + tb])
                ins = s.activation(out=acc[:, tb * 1024:(tb + 1) * 1024],
                                   in_=ps_o[:], func=AF.Copy, bias=0.0, scale=1.0)
                s_a.inc(ins)
                ob2_a.append(s_a.n)

        for e in range(L1FFN_EXPERTS):
            for oc in range(32):
                kk = len(up_t)

                @block.gpsimd
                def _(g, e=e, oc=oc, kk=kk):
                    if kk >= 2:
                        g.wait_ge(s_t.h, up_t[kk - 2])
                    swu[kk % 2].dma(g.dma_start(out=wsl[kk % 2][:, :], in_=w1e[e, oc]))
                ld_sem, ld_cnt = swu[kk % 2].h, swu[kk % 2].n

                @block.tensor
                def _(t, grp=grp, e=e, oc=oc, kk=kk, ld_sem=ld_sem, ld_cnt=ld_cnt):
                    t.wait_ge(ld_sem, ld_cnt)
                    if kk >= 2:
                        t.wait_ge(s_a.h, up_a[kk - 2])
                    for c in range(8):
                        ins = t.matmul(
                            ps_u[kk % 2][:],
                            wsl[kk % 2][:, c * 128:(c + 1) * 128],
                            fm(xnf)[:, c, grp * 512:(grp + 1) * 512],
                            start=(c == 0), stop=(c == 7))
                    s_t.inc(ins)
                    up_t.append(s_t.n)

                @block.scalar
                def _(s, grp=grp, e=e, oc=oc, kk=kk):
                    s.wait_ge(s_t.h, up_t[kk])
                    if oc == 0 and oev_v:
                        s.wait_ge(s_v.h, oev_v[-1])
                    ins = s.activation(out=ubuf[:, oc * 512:(oc + 1) * 512],
                                       in_=ps_u[kk % 2][:], func=AF.Relu,
                                       bias=b1_sb[:, e * 32 + oc:e * 32 + oc + 1],
                                       scale=1.0)
                    s_a.inc(ins)
                    up_a.append(s_a.n)

            @block.gpsimd
            def _(g, grp=grp, e=e):
                if w2l_cnt:
                    g.wait_ge(s_t.h, w2l_cnt[-1])
                for c0 in range(0, 32, 8):
                    sw2.dma(g.dma_start(
                        out=w2sb[:, c0 * 1024:(c0 + 8) * 1024].rearrange(
                            "p (c o) -> p c o", c=8),
                        in_=w2e[e, c0:c0 + 8].rearrange("c p o -> p c o")))
            w2_loaded = sw2.n

            for tb in range(4):
                @block.tensor
                def _(t, grp=grp, e=e, tb=tb, w2_loaded=w2_loaded):
                    t.wait_ge(sw2.h, w2_loaded)
                    s_a.wait(t)
                    if oev_v:
                        t.wait_ge(s_v.h, oev_v[-1])
                    for c in range(32):
                        for nh in range(2):
                            ins = t.matmul(
                                ps_o[:, nh * 512:(nh + 1) * 512],
                                ubuf[:, c * 512 + tb * 128:c * 512 + tb * 128 + 128],
                                w2sb[:, c * 1024 + nh * 512:c * 1024 + nh * 512 + 512],
                                start=(c == 0), stop=(c == 31))
                    s_t.inc(ins)
                o_t = s_t.n

                @block.vector
                def _(v, grp=grp, e=e, tb=tb, o_t=o_t):
                    v.wait_ge(s_t.h, o_t)
                    if e == 0:
                        v.wait_ge(s_a.h, ob2_a[grp * 4 + tb])
                    g_tb = grp * 4 + tb
                    v.scalar_tensor_tensor(
                        acc[:, tb * 1024:(tb + 1) * 1024], ps_o[:],
                        rw_all[:, g_tb * 8 + e:g_tb * 8 + e + 1],
                        acc[:, tb * 1024:(tb + 1) * 1024],
                        op0=ALU.mult, op1=ALU.add)
                    s_v.inc(v.drain())
                    oev_v.append(s_v.n)

            w2l_cnt.append(s_t.n)

        for tb in range(4):
            g_tb = grp * 4 + tb

            @block.sync
            def _(sync, g_tb=g_tb):
                if g_tb < 2:
                    sync.wait_ge(s_v.h, ln_done[-1])
                if g_tb >= 2:
                    sync.wait_ge(s_p.h, ep_done[g_tb - 2])
                sld.dma(sync.dma_start(out=x3e[g_tb % 2][:],
                                       in_=x3[g_tb * 128:(g_tb + 1) * 128, :]))

            @block.vector
            def _(v, grp=grp, tb=tb, g_tb=g_tb):
                s_v.wait(v)
                sld.wait(v)
                v.wait_ge(s_v.h, gb_done)
                v.tensor_scalar(t1[:], x3e[g_tb % 2][:],
                                stats_all[:, 2 * g_tb:2 * g_tb + 1],
                                stats_all[:, 2 * g_tb + 1:2 * g_tb + 2],
                                ALU.add, ALU.mult)
                v.drain()
                v.tensor_add(inner[:], t1[:], acc[:, tb * 1024:(tb + 1) * 1024])
                s_v.inc(v.drain())
                ep_in.append(s_v.n)

            emit_ln_stats(block, inner[:], stat, sq[:], s_v, s_a)

            @block.vector
            def _(v, g_tb=g_tb):
                v.tensor_scalar(t1[:], inner[:], stat[:, 0:1], stat[:, 3:4],
                                ALU.add, ALU.mult)
                v.drain()
                v.tensor_mul(t2[:], t1[:], gb_bc[:, 0:1024])
                v.drain()
                v.tensor_add(t2[:], t2[:], gb_bc[:, 1024:2048])
                s_v.inc(v.drain())

            emit_ln_stats(block, t2[:], stat, sq[:], s_v, s_a)

            @block.vector
            def _(v, g_tb=g_tb):
                if g_tb >= 1:
                    v.wait_ge(s_p.h, ep_done[g_tb - 1])
                v.tensor_scalar(t1[:], t2[:], stat[:, 0:1], stat[:, 3:4],
                                ALU.add, ALU.mult)
                s_v.inc(v.drain())
                ep_ln.append(s_v.n)

            @block.gpsimd
            def _(g, g_tb=g_tb):
                s_v.wait(g)
                sld.wait(g)
                if g_tb >= 2:
                    g.wait_ge(sst.h, st_cnt[g_tb - 2])
                ins = g.tensor_add(out_sb[g_tb % 2][:], t1[:], x3e[g_tb % 2][:])
                s_p.inc(ins)
                ep_done.append(s_p.n)

            @block.sync
            def _(sync, g_tb=g_tb):
                sync.wait_ge(s_p.h, ep_done[g_tb])
                sst.dma(sync.dma_start(out=x4[g_tb * 128:(g_tb + 1) * 128, :],
                                       in_=out_sb[g_tb % 2][:]))
                st_cnt.append(sst.n)

    @block.sync
    def _(sync):
        sst.wait(sync)



def build_fused():
    nc = bass.Bass("TRN2", target_bir_lowering=False)

    def ei(name, shape, dtype=F32):
        return nc.dram_tensor(name, shape, dtype, kind="ExternalInput").ap()

    io = {}
    io["x_seq"] = ei("x_seq", [SEQ, D])
    io["x_own"] = ei("x_own", [TCH, D])
    io["wqt"] = ei("wqt", [8, 8, 128, 128])
    io["wkt"] = ei("wkt", [8, 8, 128, 128])
    io["wvt"] = ei("wvt", [8, 128, 1024])
    io["wot"] = ei("wot", [8, 128, 1024])
    io["maskt"] = ei("maskt", [128, 79 * 128])
    io["ident"] = ei("ident", [128, 128])
    io["w1t"] = ei("w1t", [32, 128, 1024])
    io["w2t"] = ei("w2t", [32, 128, 1024])
    io["h1t"] = ei("h1t", [32, 128, 4096])
    io["h2t"] = ei("h2t", [32, 128, 4096])
    io["w3t"] = ei("w3t", [32, 128, 1024])
    io["w1s"] = ei("w1s", [128, 32])
    io["w2s"] = ei("w2s", [128, 32])
    io["h1s"] = ei("h1s", [128, 32])
    io["h2s"] = ei("h2s", [128, 32])
    io["w3s"] = ei("w3s", [1, 1024])
    io["encq"] = ei("encq", [8, 1024, 128])
    io["encv"] = ei("encv", [8, 1024, 128])
    io["dec"] = ei("dec", [8, 128, 1024])
    io["heb"] = ei("heb", [8, 128, 128])
    io["mw1"] = ei("mw1", [8, 128, 512])
    io["mw2"] = ei("mw2", [4, 128, 1024])
    io["mb1"] = ei("mb1", [1, 512])
    io["mb2"] = ei("mb2", [1, 1024])
    io["base"] = ei("base", [1, 1024])
    io["band"] = ei("band", [128, 256])
    io["wsel"] = ei("wsel", [128, 16])
    io["hsel"] = ei("hsel", [128, 8])
    io["wrt"] = ei("wrt", [8, 128, 8])
    io["br"] = ei("br", [1, 8])
    io["w1e"] = ei("w1e", [8, 32, 128, 1024])
    io["b1l"] = ei("b1l", [128, 8 * 32])
    io["w2e"] = ei("w2e", [8, 32, 128, 1024])
    io["b2i"] = ei("b2i", [8, 1024])
    io["gam"] = ei("gam", [1, 1024])
    io["bet"] = ei("bet", [1, 1024])

    io["x4"] = nc.dram_tensor("x4", [TCH, D], F32, kind="ExternalOutput").ap()

    kind_i = "ExternalOutput" if DUMP_STAGES else "Internal"
    io["x1"] = nc.dram_tensor("x1", [TCH, D], F32, kind=kind_i).ap()
    io["x2"] = nc.dram_tensor("x2", [TCH, D], F32, kind=kind_i).ap()
    io["x3"] = nc.dram_tensor("x3", [TCH, D], F32, kind=kind_i).ap()
    io["k_sc"] = nc.dram_tensor("k_sc", [8, 128, SEQ], BF16, kind="Internal").ap()
    io["v_sc"] = nc.dram_tensor("v_sc", [16, SEQ, 64], BF16, kind="Internal").ap()
    cc_in_h = nc.dram_tensor("cc_in", [128, 1040], F32, kind="Internal")
    cc_out_h = nc.dram_tensor("cc_out", [N_CORES * 128, 1040], F32, kind="Internal")
    io["cc_in"] = cc_in_h.ap()
    io["cc_out"] = cc_out_h.ap()
    io["cc_in_h"] = cc_in_h
    io["cc_out_h"] = cc_out_h
    io["halo_d"] = nc.dram_tensor("halo_d", [128, D], F32, kind=kind_i).ap()
    io["sig_d"] = nc.dram_tensor("sig_d", [128, 8], F32, kind=kind_i).ap()

    es = contextlib.ExitStack()
    with es:
        root = Ctx(nc, es)
        for name in ("sld", "swl", "sw2", "swu0", "swu1", "swu2", "swu3",
                     "swh0", "swh1", "s_t", "s_a", "s_v", "s_p", "sst",
                     "ssc", "scc"):
            root.sem(name)
        block = es.enter_context(nc.Block())
        emitters = [("a", emit_l0attn), ("f", emit_l0ffn), ("s", emit_sig),
                    ("h", emit_l1attn), ("m", emit_l1ffn)]
        for idx, (tag, fn) in enumerate(emitters):
            if idx > 0:
                barrier(block, root)
            with contextlib.ExitStack() as es_sec:
                ctx = Ctx(nc, es_sec, root.sems, tag + "_")
                fn(nc, ctx, block, io)
    return nc



def _chunk_w_ic_oc(w):
    i_n, o_n = w.shape
    return np.ascontiguousarray(
        w.reshape(i_n // 128, 128, o_n // 128, 128).transpose(0, 2, 1, 3))


def _chunk_w_oc_ic(w):
    i_n, o_n = w.shape
    return np.ascontiguousarray(
        w.reshape(i_n // 128, 128, o_n // 128, 128).transpose(2, 0, 1, 3))


def prep_l0attn_shared(inputs):
    d = {}
    d["wqt"] = _chunk_w_ic_oc(np.ascontiguousarray(inputs["L0_attn_Wq"].T))
    d["wkt"] = _chunk_w_ic_oc(np.ascontiguousarray(inputs["L0_attn_Wk"].T))
    d["wvt"] = np.ascontiguousarray(
        np.ascontiguousarray(inputs["L0_attn_Wv"].T).reshape(8, 128, 1024))
    d["wot"] = np.ascontiguousarray(
        np.ascontiguousarray(inputs["L0_attn_Wo"].T).reshape(8, 128, 1024))
    d["ident"] = np.eye(128, dtype=np.float32)
    return d


def make_l0attn_mask(p):
    r = np.arange(128)[None, :]
    c = np.arange(128)[:, None]
    tiles = []
    for mu in range(79):
        delta = 16 * p + 15 - mu
        dd = 128 * delta + r - c
        act = (dd >= 0) & ((dd <= 255) | (dd % 64 == 0))
        tiles.append(act.astype(np.float32))
    return np.ascontiguousarray(np.concatenate(tiles, axis=1))


def _pmaj(w4):
    n_oc, n_ic = w4.shape[0], w4.shape[1]
    return np.ascontiguousarray(
        w4.transpose(0, 2, 1, 3).reshape(n_oc, 128, n_ic * 128))


def prep_l0ffn_shared(inputs):
    d = {}
    d["w1t"] = _pmaj(_chunk_w_oc_ic(np.ascontiguousarray(inputs["L0_ffn_w1_w"].T)))
    d["w2t"] = _pmaj(_chunk_w_oc_ic(np.ascontiguousarray(inputs["L0_ffn_w2_w"].T)))
    d["h1t"] = _pmaj(_chunk_w_oc_ic(np.ascontiguousarray(inputs["L0_ffn_h1_w"].T)))
    d["h2t"] = _pmaj(_chunk_w_oc_ic(np.ascontiguousarray(inputs["L0_ffn_h2_w"].T)))
    d["w3t"] = np.ascontiguousarray(
        np.ascontiguousarray(inputs["L0_ffn_w3_w"].T).reshape(32, 128, 1024))
    for nm, key in [("w1s", "L0_ffn_w1_s"), ("w2s", "L0_ffn_w2_s"),
                    ("h1s", "L0_ffn_h1_s"), ("h2s", "L0_ffn_h2_s")]:
        d[nm] = np.ascontiguousarray(inputs[key].reshape(32, 128).T)
    d["w3s"] = np.ascontiguousarray(inputs["L0_ffn_w3_s"].reshape(1, 1024))
    return d


def prep_l1attn_shared(inputs):
    d = {}
    d["encq"] = np.ascontiguousarray(inputs["L1_attn_enc_q"])
    d["encv"] = np.ascontiguousarray(inputs["L1_attn_enc_v"])
    d["dec"] = np.ascontiguousarray(inputs["L1_attn_decoder"].reshape(8, 128, 1024))
    d["heb"] = np.ascontiguousarray(inputs["L1_attn_hebbian"])
    d["mw1"] = np.ascontiguousarray(inputs["L1_attn_m_w1"].reshape(8, 128, 512))
    d["mw2"] = np.ascontiguousarray(inputs["L1_attn_m_w2"].reshape(4, 128, 1024))
    d["mb1"] = np.ascontiguousarray(inputs["L1_attn_m_b1"].reshape(1, 512))
    d["mb2"] = np.ascontiguousarray(inputs["L1_attn_m_b2"].reshape(1, 1024))
    d["base"] = np.ascontiguousarray(inputs["L1_attn_base_metric"].reshape(1, 1024))
    rr = np.arange(128)[:, None]
    ss = np.arange(256)[None, :]
    act = (ss >= rr + 97) & (ss <= rr + 128)
    d["band"] = np.where(act, 0.0, 1e30).astype(np.float32)
    return d


def prep_l1ffn_shared(inputs):
    d = {}
    d["wrt"] = np.ascontiguousarray(
        np.ascontiguousarray(inputs["L1_ffn_Wr"].T).reshape(8, 128, 8))
    d["br"] = np.ascontiguousarray(inputs["L1_ffn_br"].reshape(1, 8))
    w1 = inputs["L1_ffn_W1"]
    d["w1e"] = np.stack([_pmaj(_chunk_w_oc_ic(np.ascontiguousarray(w1[e].T)))
                         for e in range(8)])
    d["b1l"] = np.ascontiguousarray(
        inputs["L1_ffn_b1"].reshape(8, 32, 128).transpose(2, 0, 1).reshape(128, 256))
    w2 = inputs["L1_ffn_W2"]
    d["w2e"] = np.stack([np.ascontiguousarray(w2[e].T).reshape(32, 128, 1024)
                         for e in range(8)])
    d["b2i"] = np.ascontiguousarray(inputs["L1_ffn_b2"])
    d["gam"] = np.ascontiguousarray(inputs["L1_ffn_gamma"].reshape(1, 1024))
    d["bet"] = np.ascontiguousarray(inputs["L1_ffn_beta"].reshape(1, 1024))
    return d


def make_wsel(core):
    b = core // 4
    w = np.zeros((16,), np.float32)
    for r in range(8):
        if r // 4 == b:
            w[r * 2 + 0] = 1.0 / SEQ
            w[r * 2 + 1] = 0.5 / 1024.0
    return np.ascontiguousarray(np.tile(w[None, :], (128, 1)))


def make_hsel(core):
    h = np.zeros((8,), np.float32)
    if core % 4 != 0:
        h[core - 1] = 1.0
    return np.ascontiguousarray(np.tile(h[None, :], (128, 1)))


def kernel(**inputs):
    x = np.ascontiguousarray(inputs["L0_attn_x"], dtype=np.float32)

    sh = {}
    sh.update(prep_l0attn_shared(inputs))
    sh.update(prep_l0ffn_shared(inputs))
    sh.update(prep_l1attn_shared(inputs))
    sh.update(prep_l1ffn_shared(inputs))

    nc = build_fused()
    in_maps = []
    for c in range(N_CORES):
        b, p = c // 4, c % 4
        m = dict(sh)
        m["x_seq"] = np.ascontiguousarray(x[b])
        m["x_own"] = np.ascontiguousarray(x[b][p * TCH:(p + 1) * TCH])
        m["maskt"] = make_l0attn_mask(p)
        m["wsel"] = make_wsel(c)
        m["hsel"] = make_hsel(c)
        in_maps.append(m)
    res = run_bass_kernel_spmd(nc, in_maps, core_ids=list(range(N_CORES)))
    x4 = np.stack([np.asarray(res.results[c]["x4"]) for c in range(N_CORES)])
    return x4.reshape(2, SEQ, D).astype(np.float32)


# version 05
# speedup vs baseline: 1.0256x; 1.0256x over previous
import contextlib
import numpy as np
import concourse.bass as bass
import concourse.mybir as mybir
from concourse.bass_utils import run_bass_kernel_spmd

F32 = mybir.dt.float32
BF16 = mybir.dt.bfloat16
AF = mybir.ActivationFunctionType
ALU = mybir.AluOpType
AX = mybir.AxisListType

N_CORES = 8
DUMP_STAGES = False
L1FFN_GROUPS = 4
L1FFN_EXPERTS = 8
B = 2
SEQ = 8192
D = 1024
TCH = 2048
NBLK = TCH // 128
LN_EPS = 1e-5



class Sem:

    def __init__(self, nc, es, name):
        self.h = es.enter_context(nc.semaphore(name))
        self.n = 0

    def inc(self, instr, k=1):
        instr.then_inc(self.h, k)
        self.n += k
        return instr

    def dma(self, instr):
        return self.inc(instr, 16)

    def wait(self, eng, n=None):
        eng.wait_ge(self.h, self.n if n is None else n)


class Ctx:
    def __init__(self, nc, es, sems=None, tag=""):
        self.nc = nc
        self.es = es
        self.sems = {} if sems is None else sems
        self.tag = tag

    def sem(self, name):
        if name not in self.sems:
            self.sems[name] = Sem(self.nc, self.es, name)
        return self.sems[name]

    def sub(self, es, subtag):
        return Ctx(self.nc, es, self.sems, self.tag + subtag)

    def sbuf(self, name, shape, dtype):
        return self.es.enter_context(
            self.nc.sbuf_tensor(self.tag + name, shape, dtype))

    def psum(self, name, shape, dtype=F32):
        return self.es.enter_context(
            self.nc.psum_tensor(self.tag + name, shape, dtype))


def fm(ap, c=8):
    return ap[:, :].rearrange("p (c t) -> p c t", c=c)


def barrier(block, ctx, skip=("scc",)):
    snap = [(s.h, s.n) for k, s in ctx.sems.items()
            if s.n > 0 and k not in skip]
    for en in ("sync", "gpsimd", "scalar", "vector", "tensor"):
        @getattr(block, en)
        def _(eng, snap=snap):
            for h, n in snap:
                eng.wait_ge(h, n)


def emit_ln_stats(block, xf, stat, sq, s_v, s_a, n_free=D, eps=LN_EPS):
    @block.vector
    def _(v):
        v.reduce_sum(stat[:, 0:1], xf, axis=AX.X, negate=True)
        v.drain()
        v.tensor_scalar_mul(stat[:, 0:1], stat[:, 0:1], 1.0 / n_free)
        s_v.inc(v.drain())

    @block.scalar
    def _(s):
        s_v.wait(s)
        s.activation(out=sq, in_=xf, func=AF.Square, bias=stat[:, 0:1], scale=1.0)
        s_a.inc(s.drain())

    @block.vector
    def _(v):
        s_a.wait(v)
        v.reduce_sum(stat[:, 1:2], sq, axis=AX.X)
        v.drain()
        v.tensor_scalar(stat[:, 1:2], stat[:, 1:2], 1.0 / n_free, eps,
                        ALU.mult, ALU.add)
        s_v.inc(v.drain())

    @block.scalar
    def _(s):
        s_v.wait(s)
        s.activation(out=stat[:, 2:3], in_=stat[:, 1:2], func=AF.Sqrt,
                     bias=0.0, scale=1.0)
        s_a.inc(s.drain())

    @block.vector
    def _(v):
        s_a.wait(v)
        v.reciprocal(stat[:, 3:4], stat[:, 2:3])
        s_v.inc(v.drain())



def emit_l0attn(nc, ctx, block, io):
    x_own = io["x_own"]; x_seq = io["x_seq"]
    wqt = io["wqt"]; wkt = io["wkt"]; wvt = io["wvt"]; wot = io["wot"]
    maskt = io["maskt"]; ident = io["ident"]
    x1 = io["x1"]; k_sc = io["k_sc"]; v_sc = io["v_sc"]

    q_all = ctx.sbuf("q_all", [128, 8 * TCH], BF16)
    maskm = ctx.sbuf("maskm", [128, 79 * 128], BF16)
    attn_f = ctx.sbuf("attn_f", [128, 8 * TCH], BF16)
    identb = ctx.sbuf("identb", [128, 128], BF16)
    bneg = ctx.sbuf("bneg", [128, 1], F32)
    onesf = ctx.sbuf("onesf", [128, 64], F32)
    stat = ctx.sbuf("stat", [128, 8], F32)
    sq = ctx.sbuf("sq", [128, D], F32)
    xf = [ctx.sbuf(f"xf{i}", [128, D], F32) for i in range(2)]
    xn_t = ctx.sbuf("xn_t", [128, D], BF16)
    t1 = ctx.sbuf("t1", [128, D], F32)
    out_sb = [ctx.sbuf(f"out_sb{i}", [128, D], F32) for i in range(2)]
    x1e = [ctx.sbuf(f"x1e{i}", [128, D], F32) for i in range(2)]

    sld = ctx.sem("sld")
    swl = ctx.sem("swl")
    s_t = ctx.sem("s_t")
    s_a = ctx.sem("s_a")
    s_v = ctx.sem("s_v")
    s_p = ctx.sem("s_p")
    sst = ctx.sem("sst")
    ssc = ctx.sem("ssc")

    @block.gpsimd
    def _(g):
        swl.dma(g.dma_start(out=identb[:], in_=ident[:]))
        swl.dma(g.dma_start(out=maskm[:], in_=maskt[:]))
        g.memset(bneg[:], -25.0)
        g.memset(onesf[:], 1.0)
        s_p.inc(g.nop())
    consts_g = s_p.n
    consts_w = swl.n

    with contextlib.ExitStack() as esA:
        ctxA = ctx.sub(esA, "A")
        wq_sb = ctxA.sbuf("wq_sb", [128, 8 * 1024], BF16)
        wk_sb = ctxA.sbuf("wk_sb", [128, 8 * 1024], BF16)
        wv_sb = ctxA.sbuf("wv_sb", [128, 8 * 1024], BF16)
        xnf_g = ctxA.sbuf("xnf_g", [128, 8 * 512], BF16)
        krow = [ctxA.sbuf(f"krow{i}", [128, 512], BF16) for i in range(2)]
        vtok = [ctxA.sbuf(f"vtok{i}", [128, 1024], BF16) for i in range(2)]
        ps_t = ctxA.psum("ps_t", [128, 1024], BF16)
        ps_ab = [ctxA.psum(f"ps_ab{i}", [128, 512], F32) for i in range(2)]

        @block.gpsimd
        def _(g):
            for c in range(8):
                swl.dma(g.dma_start(
                    out=fm(wq_sb)[:, c, :].rearrange("p (o u) -> p o u", o=8),
                    in_=wqt[c].rearrange("o p u -> p o u")))
                swl.dma(g.dma_start(
                    out=fm(wk_sb)[:, c, :].rearrange("p (o u) -> p o u", o=8),
                    in_=wkt[c].rearrange("o p u -> p o u")))
                swl.dma(g.dma_start(out=fm(wv_sb)[:, c, :], in_=wvt[c]))
        wts_loaded = swl.n

        ev_cnt = []
        ln_cnt = []
        kv_t = []
        kv_a = []
        k_store = []
        v_store = []
        q_ev = []

        def ln_block(src_dram, row0, i_par, blkid):
            @block.sync
            def _(sync):
                if len(ln_cnt) >= 2:
                    sync.wait_ge(s_v.h, ln_cnt[-2])
                sld.dma(sync.dma_start(out=xf[blkid % 2][:],
                                       in_=src_dram[row0:row0 + 128, :]))

            @block.vector
            def _(v):
                sld.wait(v)

            emit_ln_stats(block, xf[blkid % 2][:], stat, sq[:], s_v, s_a)

            @block.vector
            def _(v):
                if ev_cnt:
                    v.wait_ge(s_v.h, ev_cnt[-1])
                v.tensor_scalar(xn_t[:], xf[blkid % 2][:], stat[:, 0:1],
                                stat[:, 3:4], ALU.add, ALU.mult)
                s_v.inc(v.drain())
                ln_cnt.append(s_v.n)

            @block.tensor
            def _(t):
                swl.wait(t, consts_w)
                s_v.wait(t)
                for c in range(8):
                    ins = t.transpose(ps_t[:, c * 128:(c + 1) * 128],
                                      xn_t[:, c * 128:(c + 1) * 128], identb[:])
                s_t.inc(ins)

            @block.vector
            def _(v):
                s_t.wait(v)
                v.tensor_copy(fm(xnf_g)[:, :, i_par * 128:(i_par + 1) * 128],
                              fm(ps_t)[:, :, :])
                s_v.inc(v.drain())
                ev_cnt.append(s_v.n)

        blkid = 0
        for g in range(16):
            for i4 in range(4):
                ln_block(x_seq, g * 512 + i4 * 128, i4, blkid)
                blkid += 1
            for oc in range(8):
                kk = len(kv_t)

                @block.tensor
                def _(t, oc=oc, kk=kk):
                    t.wait_ge(swl.h, wts_loaded)
                    s_v.wait(t)
                    if kk >= 2:
                        t.wait_ge(s_v.h, kv_a[kk - 2])
                    for c in range(8):
                        ins = t.matmul(ps_ab[kk % 2][:],
                                       wk_sb[:, c * 1024 + oc * 128:c * 1024 + oc * 128 + 128],
                                       fm(xnf_g)[:, c, :],
                                       start=(c == 0), stop=(c == 7))
                    s_t.inc(ins)
                    kv_t.append(s_t.n)

                @block.vector
                def _(v, oc=oc, kk=kk):
                    s_t.wait(v)
                    if len(k_store) >= 2:
                        v.wait_ge(ssc.h, k_store[-2])
                    v.tensor_copy(krow[kk % 2][:], ps_ab[kk % 2][:])
                    s_v.inc(v.drain())
                    kv_a.append(s_v.n)

                @block.sync
                def _(sync, g=g, oc=oc, kk=kk):
                    sync.wait_ge(s_v.h, kv_a[kk])
                    ssc.dma(sync.dma_start(out=k_sc[oc][:, g * 512:(g + 1) * 512],
                                           in_=krow[kk % 2][:]))
                    k_store.append(ssc.n)

            for tb in range(4):
                for nh in range(2):
                    kk = len(kv_t)

                    @block.tensor
                    def _(t, tb=tb, nh=nh, kk=kk):
                        if kk >= 2:
                            t.wait_ge(s_v.h, kv_a[kk - 2])
                        for c in range(8):
                            ins = t.matmul(
                                ps_ab[kk % 2][:],
                                fm(xnf_g)[:, c, tb * 128:(tb + 1) * 128],
                                wv_sb[:, c * 1024 + nh * 512:c * 1024 + nh * 512 + 512],
                                start=(c == 0), stop=(c == 7))
                        s_t.inc(ins)
                        kv_t.append(s_t.n)

                    @block.vector
                    def _(v, tb=tb, nh=nh, kk=kk):
                        s_t.wait(v)
                        if nh == 0 and len(v_store) >= 2:
                            v.wait_ge(ssc.h, v_store[-2])
                        v.tensor_copy(vtok[tb % 2][:, nh * 512:(nh + 1) * 512],
                                      ps_ab[kk % 2][:])
                        s_v.inc(v.drain())
                        kv_a.append(s_v.n)

                @block.sync
                def _(sync, g=g, tb=tb):
                    sync.wait_ge(s_v.h, kv_a[-1])
                    r0 = g * 512 + tb * 128
                    for hq in range(4):
                        ins = sync.dma_start(
                            out=v_sc[:, r0:r0 + 128, :].rearrange(
                                "h p c -> p h c")[:, 4 * hq:4 * hq + 4, :],
                            in_=vtok[tb % 2][:, :].rearrange(
                                "p (h c) -> p h c", h=16)[:, 4 * hq:4 * hq + 4, :])
                        ssc.dma(ins)
                    v_store.append(ssc.n)

        for g in range(4):
            for i4 in range(4):
                ln_block(x_own, g * 512 + i4 * 128, i4, blkid)
                blkid += 1
            for oc in range(8):
                kk = len(kv_t)

                @block.tensor
                def _(t, oc=oc, kk=kk):
                    s_v.wait(t)
                    if kk >= 2:
                        t.wait_ge(s_v.h, kv_a[kk - 2])
                    for c in range(8):
                        ins = t.matmul(ps_ab[kk % 2][:],
                                       wq_sb[:, c * 1024 + oc * 128:c * 1024 + oc * 128 + 128],
                                       fm(xnf_g)[:, c, :],
                                       start=(c == 0), stop=(c == 7))
                    s_t.inc(ins)
                    kv_t.append(s_t.n)

                @block.vector
                def _(v, g=g, oc=oc, kk=kk):
                    s_t.wait(v)
                    v.tensor_copy(fm(q_all)[:, oc, g * 512:(g + 1) * 512],
                                  ps_ab[kk % 2][:])
                    s_v.inc(v.drain())
                    kv_a.append(s_v.n)
                    q_ev.append(s_v.n)

    with contextlib.ExitStack() as esB:
        ctxB = ctx.sub(esB, "B")
        kcb = [ctxB.sbuf("kcb0", [128, SEQ], BF16)] * 2
        ptb = ctxB.sbuf("ptb", [128, 32 * 512], BF16)
        vex = [ctxB.sbuf(f"vex{i}", [128, 64 * 65], BF16) for i in range(2)]
        wo_sb = ctxB.sbuf("wo_sb", [128, 8 * 1024], BF16)
        dsb = ctxB.sbuf("dsb", [128, 1024], F32)
        araw = ctxB.sbuf("araw", [128, 512], F32)
        ps_s = [ctxB.psum(f"ps_s{i}", [128, 512], F32) for i in range(4)]
        ps_av = ctxB.psum("ps_av", [128, 512], F32)
        ps_bc = ctxB.psum("ps_bc", [128, 512], F32)
        ps_o = ctxB.psum("ps_o", [128, 1024], F32)

        @block.gpsimd
        def _(g):
            g.wait_ge(s_v.h, q_ev[-1])
            g.wait_ge(ssc.h, ssc.n)
            for c in range(8):
                swl.dma(g.dma_start(out=fm(wo_sb)[:, c, :], in_=wot[c]))
        wo_loaded = swl.n

        @block.vector
        def _(v):
            for i in range(2):
                ins = v.memset(vex[i][:, :].rearrange(
                    "p (k c) -> p k c", c=65)[:, :, 64:65], 1.0)
                v.drain()
            s_p.inc(v.nop())
        vex_ones = s_p.n

        kload = []
        vload = []
        exp_a = []
        sc_t = []
        av_t = []
        av_mid = []
        msk_v = []
        den_a = []
        af_v = []

        for kc in range(8):
            @block.sync
            def _(sync, kc=kc):
                if kc == 0:
                    sync.wait_ge(s_v.h, q_ev[-1])
                if kc >= 1:
                    sync.wait_ge(s_t.h, av_t[-1])
                sld.dma(sync.dma_start(out=kcb[kc % 2][:], in_=k_sc[kc]))
                kload.append(sld.n)

            for h2 in range(2):
                h = 2 * kc + h2
                vb = h % 2

                @block.gpsimd
                def _(g, kc=kc, h=h, vb=vb):
                    g.wait_ge(ssc.h, ssc.n)
                    g.wait_ge(s_p.h, vex_ones)
                    if h >= 2:
                        g.wait_ge(s_t.h, av_t[(h - 1) * 4 - 1])
                    for q8 in range(16):
                        swl.dma(g.dma_start(
                            out=vex[vb][:, :].rearrange(
                                "p (k c) -> p k c", c=65)[:, 4 * q8:4 * q8 + 4, 0:64],
                            in_=v_sc[h].rearrange(
                                "(k p) c -> p k c", p=128)[:, 4 * q8:4 * q8 + 4, :]))
                    vload.append(swl.n)

                for qg in range(4):
                    hq = len(av_t)
                    for kh in range(2):
                        for kb32 in range(32):
                            kb = 32 * kh + kb32

                            @block.tensor
                            def _(t, kc=kc, h2=h2, qg=qg, kh=kh, kb32=kb32, kb=kb, hq=hq):
                                if kb32 == 0 and kh == 0:
                                    t.wait_ge(sld.h, kload[kc])
                                if kb32 == 0 and qg == 0 and kh == 0 and kc == 0 and h2 == 0:
                                    t.wait_ge(s_v.h, q_ev[-1])
                                if len(exp_a) >= 4:
                                    t.wait_ge(s_a.h, exp_a[-4])
                                ins = t.matmul(
                                    ps_s[len(sc_t) % 4][:],
                                    kcb[kc % 2][h2 * 64:h2 * 64 + 64, kb * 128:kb * 128 + 128],
                                    q_all[h2 * 64:h2 * 64 + 64,
                                          kc * 2048 + qg * 512:kc * 2048 + qg * 512 + 512],
                                    start=True, stop=True)
                                s_t.inc(ins)
                                sc_t.append(s_t.n)

                            @block.scalar
                            def _(s, kb32=kb32, kh=kh, hq=hq):
                                if not exp_a:
                                    s.wait_ge(s_p.h, consts_g)
                                s.wait_ge(s_t.h, sc_t[-1])
                                if kb32 == 0:
                                    if kh == 1:
                                        s.wait_ge(s_v.h, msk_v[-1])
                                        s.wait_ge(s_t.h, av_mid[-1])
                                    elif av_t:
                                        s.wait_ge(s_t.h, av_t[-1])
                                ins = s.activation(
                                    out=ptb[:, kb32 * 512:(kb32 + 1) * 512],
                                    in_=ps_s[(len(sc_t) - 1) % 4][:],
                                    func=AF.Exp, bias=bneg[:, 0:1], scale=0.125)
                                s_a.inc(ins)
                                exp_a.append(s_a.n)

                        @block.vector
                        def _(v, qg=qg, kh=kh):
                            if not msk_v:
                                v.wait_ge(swl.h, consts_w)
                            s_a.wait(v)
                            for j in range(4):
                                mu0 = (15 - 4 * qg - j + 32 * kh)
                                v.tensor_mul(
                                    fm(ptb, c=32)[:, :, j * 128:(j + 1) * 128],
                                    fm(ptb, c=32)[:, :, j * 128:(j + 1) * 128],
                                    fm(maskm, c=79)[:, mu0:mu0 + 32, :])
                                v.drain()
                            s_v.inc(v.nop())
                            msk_v.append(s_v.n)

                        for kb32 in range(32):
                            kb = 32 * kh + kb32

                            @block.tensor
                            def _(t, vb=vb, kh=kh, kb32=kb32, kb=kb, h=h):
                                if kb32 == 0:
                                    t.wait_ge(s_v.h, msk_v[-1])
                                    if kh == 0:
                                        t.wait_ge(swl.h, vload[h])
                                        if den_a:
                                            t.wait_ge(s_a.h, den_a[-1])
                                ins = t.matmul(
                                    ps_av[0:65, :],
                                    vex[vb][:, kb * 65:kb * 65 + 65],
                                    ptb[:, kb32 * 512:(kb32 + 1) * 512],
                                    start=(kh == 0 and kb32 == 0),
                                    stop=(kh == 1 and kb32 == 31))
                                s_t.inc(ins)
                                if kb32 == 31 and kh == 0:
                                    av_mid.append(s_t.n)

                    av_t.append(s_t.n)

                    @block.scalar
                    def _(s, hq=hq):
                        s.wait_ge(s_t.h, av_t[hq])
                        if af_v:
                            s.wait_ge(s_v.h, af_v[-1])
                        s.activation(out=dsb[64:65, 0:512], in_=ps_av[64:65, :],
                                     func=AF.Copy, bias=0.0, scale=1.0)
                        s.drain()
                        ins = s.activation(out=araw[0:64, :], in_=ps_av[0:64, :],
                                           func=AF.Copy, bias=0.0, scale=1.0)
                        s_a.inc(ins)
                        den_a.append(s_a.n)

                    @block.vector
                    def _(v):
                        s_a.wait(v)
                        v.reciprocal(dsb[64:65, 512:1024], dsb[64:65, 0:512])
                        s_v.inc(v.drain())

                    @block.tensor
                    def _(t):
                        t.wait_ge(s_p.h, consts_g)
                        s_v.wait(t)
                        ins = t.matmul(ps_bc[0:64, :], onesf[64:65, 0:64],
                                       dsb[64:65, 512:1024], start=True, stop=True)
                        s_t.inc(ins)

                    @block.vector
                    def _(v, kc=kc, h2=h2, qg=qg):
                        s_t.wait(v)
                        v.tensor_mul(
                            attn_f[h2 * 64:h2 * 64 + 64,
                                   kc * 2048 + qg * 512:kc * 2048 + qg * 512 + 512],
                            araw[0:64, :], ps_bc[0:64, :])
                        s_v.inc(v.drain())
                        af_v.append(s_v.n)

        ep_done = []
        ep_ln = []
        st_cnt = []
        for tb in range(NBLK):
            @block.sync
            def _(sync, tb=tb):
                if tb >= 2:
                    sync.wait_ge(s_p.h, ep_done[tb - 2])
                sld.dma(sync.dma_start(out=x1e[tb % 2][:],
                                       in_=x_own[tb * 128:(tb + 1) * 128, :]))

            @block.tensor
            def _(t, tb=tb):
                if tb == 0:
                    t.wait_ge(swl.h, wo_loaded)
                    s_v.wait(t)
                if tb >= 1:
                    t.wait_ge(s_v.h, ep_ln[tb - 1])
                for c in range(8):
                    for nh in range(2):
                        ins = t.matmul(
                            ps_o[:, nh * 512:(nh + 1) * 512],
                            fm(attn_f)[:, c, tb * 128:(tb + 1) * 128],
                            wo_sb[:, c * 1024 + nh * 512:c * 1024 + nh * 512 + 512],
                            start=(c == 0), stop=(c == 7))
                s_t.inc(ins)

            @block.vector
            def _(v):
                s_t.wait(v)

            emit_ln_stats(block, ps_o[:], stat, sq[:], s_v, s_a)

            @block.vector
            def _(v, tb=tb):
                if tb >= 1:
                    v.wait_ge(s_p.h, ep_done[tb - 1])
                v.tensor_scalar(t1[:], ps_o[:], stat[:, 0:1], stat[:, 3:4],
                                ALU.add, ALU.mult)
                s_v.inc(v.drain())
                ep_ln.append(s_v.n)

            @block.vector
            def _(v, tb=tb):
                sld.wait(v)
                if tb >= 2:
                    v.wait_ge(sst.h, st_cnt[tb - 2])
                ins = v.tensor_add(out_sb[tb % 2][:], t1[:], x1e[tb % 2][:])
                s_p.inc(ins)
                ep_done.append(s_p.n)

            @block.sync
            def _(sync, tb=tb):
                sync.wait_ge(s_p.h, ep_done[tb])
                sst.dma(sync.dma_start(out=x1[tb * 128:(tb + 1) * 128, :],
                                       in_=out_sb[tb % 2][:]))
                st_cnt.append(sst.n)

        @block.sync
        def _(sync):
            sst.wait(sync)



def emit_l0ffn(nc, ctx, block, io):
    x1 = io["x1"]; x2 = io["x2"]; ident = io["ident"]
    w1t = io["w1t"]; w2t = io["w2t"]; h1t = io["h1t"]; h2t = io["h2t"]
    w3t = io["w3t"]
    w1s = io["w1s"]; w2s = io["w2s"]; h1s = io["h1s"]; h2s = io["h2s"]
    w3s = io["w3s"]

    xnf = ctx.sbuf("xnf", [128, 8 * TCH], BF16)
    big2 = ctx.sbuf("big2", [128, 32 * 1024], BF16)
    hbuf = ctx.sbuf("hbuf", [128, 32 * 512], BF16)
    w3sb = big2
    h1o = big2[:, 0:32 * 512]
    xf = [ctx.sbuf(f"xf{i}", [128, D], F32) for i in range(2)]
    xn_t = ctx.sbuf("xn_t", [128, D], BF16)
    identb = ctx.sbuf("identb", [128, 128], BF16)
    scl = ctx.sbuf("scl", [128, 4 * 32], F32)
    w3sb_bc = ctx.sbuf("w3sb_bc", [128, D], F32)
    w3s_row = ctx.sbuf("w3s_row", [1, D], BF16)
    ones1 = ctx.sbuf("ones1", [1, 128], BF16)
    stat = ctx.sbuf("stat", [128, 8], F32)
    sq = ctx.sbuf("sq", [128, D], F32)
    usb = [ctx.sbuf(f"usb{i}", [128, 512], F32) for i in range(2)]
    wsl = [ctx.sbuf(f"wsl{i}", [128, 8 * 128], BF16) for i in range(4)]
    hsl = [ctx.sbuf(f"hsl{i}", [128, 32 * 128], BF16) for i in range(2)]
    t1 = ctx.sbuf("t1", [128, D], F32)
    out_sb = [ctx.sbuf(f"out_sb{i}", [128, D], F32) for i in range(2)]
    x1e = [ctx.sbuf(f"x1e{i}", [128, D], F32) for i in range(2)]

    ps_t = ctx.psum("ps_t", [128, 1024], BF16)
    ps_u = [ctx.psum(f"ps_u{i}", [128, 512], F32) for i in range(4)]
    ps_o = ctx.psum("ps_o", [128, 1024], F32)

    sld = ctx.sem("sld")
    swl = ctx.sem("swl")
    sw2 = ctx.sem("sw2")
    swu = [ctx.sem(f"swu{i}") for i in range(4)]
    swh = [ctx.sem(f"swh{i}") for i in range(2)]
    s_t = ctx.sem("s_t")
    s_a = ctx.sem("s_a")
    s_v = ctx.sem("s_v")
    s_p = ctx.sem("s_p")
    sst = ctx.sem("sst")
    sld_base = sld.n

    @block.gpsimd
    def _(g):
        swl.dma(g.dma_start(out=identb[:], in_=ident[:]))
        swl.dma(g.dma_start(out=scl[:, 0:32], in_=w1s[:]))
        swl.dma(g.dma_start(out=scl[:, 32:64], in_=w2s[:]))
        swl.dma(g.dma_start(out=scl[:, 64:96], in_=h1s[:]))
        swl.dma(g.dma_start(out=scl[:, 96:128], in_=h2s[:]))
        swl.dma(g.dma_start(out=w3s_row[:], in_=w3s[:]))
        g.memset(ones1[:], 1.0)
        s_p.inc(g.nop())
    consts_loaded = swl.n
    ones_ready = s_p.n

    @block.tensor
    def _(t):
        swl.wait(t, consts_loaded)
        t.wait_ge(s_p.h, ones_ready)
        t.matmul(ps_o[:, 0:512], ones1[:], w3s_row[:, 0:512], start=True, stop=True)
        ins = t.matmul(ps_o[:, 512:1024], ones1[:], w3s_row[:, 512:1024],
                       start=True, stop=True)
        s_t.inc(ins)
    bc_mm = s_t.n

    @block.vector
    def _(v):
        v.wait_ge(s_t.h, bc_mm)
        v.tensor_copy(w3sb_bc[:], ps_o[:])
        s_v.inc(v.drain())
    bc_done = s_v.n

    ln_done = []
    ev_done = []
    for i in range(NBLK):
        @block.sync
        def _(sync, i=i):
            if i >= 2:
                sync.wait_ge(s_v.h, ln_done[i - 2])
            sld.dma(sync.dma_start(out=xf[i % 2][:], in_=x1[i * 128:(i + 1) * 128, :]))

        @block.vector
        def _(v, i=i):
            sld.wait(v, sld_base + 16 * (i + 1))

        emit_ln_stats(block, xf[i % 2][:], stat, sq[:], s_v, s_a)

        @block.vector
        def _(v, i=i):
            if i >= 1:
                v.wait_ge(s_v.h, ev_done[i - 1])
            v.tensor_scalar(xn_t[:], xf[i % 2][:], stat[:, 0:1], stat[:, 3:4],
                            ALU.add, ALU.mult)
            s_v.inc(v.drain())
            ln_done.append(s_v.n)

        @block.tensor
        def _(t, i=i):
            s_v.wait(t)
            for c in range(8):
                ins = t.transpose(ps_t[:, c * 128:(c + 1) * 128],
                                  xn_t[:, c * 128:(c + 1) * 128], identb[:])
            s_t.inc(ins)

        @block.vector
        def _(v, i=i):
            s_t.wait(v)
            v.tensor_copy(fm(xnf)[:, :, i * 128:(i + 1) * 128],
                          fm(ps_t)[:, :, :])
            s_v.inc(v.drain())
            ev_done.append(s_v.n)

    up_t = []
    up_a = []
    up_p = []
    hp_t = []
    hp_a = []
    ep_done = []
    ep_ln = []
    st_cnt = []
    for grp in range(4):
        tok0 = grp * 512
        for oc in range(32):
            for kind in range(2):
                k = len(up_t)
                slot = 2 * kind + oc % 2
                wslb = wsl[slot]
                wt = w1t if kind == 0 else w2t

                @block.gpsimd
                def _(g, oc=oc, k=k, wslb=wslb, wt=wt, slot=slot):
                    if k >= 4:
                        g.wait_ge(s_t.h, up_t[k - 4])
                    swu[slot].dma(g.dma_start(out=wslb[:, :], in_=wt[oc]))
                ld_sem, ld_cnt = swu[slot].h, swu[slot].n

                @block.tensor
                def _(t, k=k, wslb=wslb, tok0=tok0, ld_sem=ld_sem, ld_cnt=ld_cnt):
                    t.wait_ge(ld_sem, ld_cnt)
                    if k == 0:
                        t.wait_ge(s_v.h, ev_done[-1])
                    if k >= 4:
                        t.wait_ge(s_a.h, up_a[k - 4])
                    for c in range(8):
                        ins = t.matmul(
                            ps_u[k % 4][:],
                            wslb[:, c * 128:(c + 1) * 128],
                            fm(xnf)[:, c, tok0:tok0 + 512],
                            start=(c == 0), stop=(c == 7))
                    s_t.inc(ins)
                    up_t.append(s_t.n)

                @block.scalar
                def _(s, oc=oc, kind=kind, k=k):
                    s_t.wait(s)
                    if len(up_p) >= 1:
                        s.wait_ge(s_p.h, up_p[-1])
                    if kind == 0:
                        ins = s.activation(out=usb[0][:], in_=ps_u[k % 4][:],
                                           func=AF.Silu, bias=0.0,
                                           scale=scl[:, oc:oc + 1])
                    else:
                        ins = s.activation(out=usb[1][:], in_=ps_u[k % 4][:],
                                           func=AF.Copy, bias=0.0,
                                           scale=scl[:, 32 + oc:33 + oc])
                    s_a.inc(ins)
                    up_a.append(s_a.n)

            @block.vector
            def _(v, oc=oc):
                v.wait_ge(s_a.h, up_a[-1])
                ins = v.tensor_mul(hbuf[:, oc * 512:(oc + 1) * 512],
                                   usb[0][:], usb[1][:])
                s_p.inc(ins)
                up_p.append(s_p.n)

        for phase, (srcb, dst, wt, soff) in enumerate(
                [(hbuf[:], h1o, h1t, 64), (h1o, hbuf[:], h2t, 96)]):
            for oc in range(32):
                hk = len(hp_t)

                @block.gpsimd
                def _(g, oc=oc, wt=wt, hk=hk):
                    if hk >= 2:
                        g.wait_ge(s_t.h, hp_t[hk - 2])
                    swh[oc % 2].dma(g.dma_start(out=hsl[oc % 2][:, :], in_=wt[oc]))
                ld_sem, ld_cnt = swh[oc % 2].h, swh[oc % 2].n

                @block.tensor
                def _(t, oc=oc, phase=phase, srcb=srcb, hk=hk,
                      ld_sem=ld_sem, ld_cnt=ld_cnt):
                    t.wait_ge(ld_sem, ld_cnt)
                    if oc == 0:
                        if phase == 0:
                            s_p.wait(t)
                        else:
                            s_a.wait(t)
                    if hk >= 4:
                        t.wait_ge(s_a.h, hp_a[hk - 4])
                    for c in range(32):
                        ins = t.matmul(
                            ps_u[hk % 4][:],
                            hsl[oc % 2][:, c * 128:(c + 1) * 128],
                            srcb[:, c * 512:(c + 1) * 512],
                            start=(c == 0), stop=(c == 31))
                    s_t.inc(ins)
                    hp_t.append(s_t.n)

                @block.scalar
                def _(s, oc=oc, dst=dst, soff=soff, hk=hk):
                    s_t.wait(s)
                    ins = s.activation(
                        out=dst[:, oc * 512:(oc + 1) * 512],
                        in_=ps_u[hk % 4][:], func=AF.Gelu,
                        bias=0.0, scale=scl[:, soff + oc:soff + oc + 1])
                    s_a.inc(ins)
                    hp_a.append(s_a.n)

        @block.gpsimd
        def _(g):
            s_a.wait(g)
            for c0 in range(0, 32, 8):
                sw2.dma(g.dma_start(
                    out=w3sb[:, c0 * 1024:(c0 + 8) * 1024].rearrange(
                        "p (c o) -> p c o", c=8),
                    in_=w3t[c0:c0 + 8].rearrange("c p o -> p c o")))
        w3_loaded = sw2.n

        @block.vector
        def _(v):
            sw2.wait(v, w3_loaded)
            v.wait_ge(s_v.h, bc_done)
            for c in range(32):
                v.tensor_mul(w3sb[:, c * 1024:(c + 1) * 1024],
                             w3sb[:, c * 1024:(c + 1) * 1024], w3sb_bc[:])
                v.drain()
            s_v.inc(v.nop())
        w3_folded = s_v.n

        for tb in range(4):
            g_tb = grp * 4 + tb
            r0 = grp * 512 + tb * 128

            @block.sync
            def _(sync, r0=r0, g_tb=g_tb):
                if g_tb >= 2:
                    sync.wait_ge(s_p.h, ep_done[g_tb - 2])
                sld.dma(sync.dma_start(out=x1e[g_tb % 2][:],
                                       in_=x1[r0:r0 + 128, :]))

            @block.tensor
            def _(t, tb=tb, g_tb=g_tb, w3_folded=w3_folded):
                t.wait_ge(s_v.h, w3_folded)
                if tb == 0:
                    s_a.wait(t)
                if g_tb >= 1:
                    t.wait_ge(s_v.h, ep_ln[g_tb - 1])
                for c in range(32):
                    for nh in range(2):
                        ins = t.matmul(
                            ps_o[:, nh * 512:(nh + 1) * 512],
                            hbuf[:, c * 512 + tb * 128:c * 512 + tb * 128 + 128],
                            w3sb[:, c * 1024 + nh * 512:c * 1024 + nh * 512 + 512],
                            start=(c == 0), stop=(c == 31))
                s_t.inc(ins)

            @block.vector
            def _(v):
                s_t.wait(v)

            emit_ln_stats(block, ps_o[:], stat, sq[:], s_v, s_a)

            @block.vector
            def _(v, g_tb=g_tb):
                if g_tb >= 1:
                    v.wait_ge(s_p.h, ep_done[g_tb - 1])
                v.tensor_scalar(t1[:], ps_o[:], stat[:, 0:1], stat[:, 3:4],
                                ALU.add, ALU.mult)
                s_v.inc(v.drain())
                ep_ln.append(s_v.n)

            @block.vector
            def _(v, g_tb=g_tb):
                sld.wait(v)
                if g_tb >= 2:
                    v.wait_ge(sst.h, st_cnt[g_tb - 2])
                ins = v.tensor_add(out_sb[g_tb % 2][:], t1[:], x1e[g_tb % 2][:])
                s_p.inc(ins)
                ep_done.append(s_p.n)

            @block.sync
            def _(sync, r0=r0, g_tb=g_tb):
                sync.wait_ge(s_p.h, ep_done[g_tb])
                sst.dma(sync.dma_start(out=x2[r0:r0 + 128, :],
                                       in_=out_sb[g_tb % 2][:]))
                st_cnt.append(sst.n)

    @block.sync
    def _(sync):
        sst.wait(sync)



def emit_sig(nc, ctx, block, io):
    x2 = io["x2"]; encq = io["encq"]; ident = io["ident"]
    cc_in = io["cc_in"]; cc_out = io["cc_out"]
    halo_d = io["halo_d"]; sig_d = io["sig_d"]
    wsel = io["wsel"]; hsel = io["hsel"]

    xnf = ctx.sbuf("xnf", [128, 8 * TCH], BF16)
    eq_sb = ctx.sbuf("eq_sb", [128, 8 * 1024], BF16)
    qbuf = ctx.sbuf("qbuf", [128, TCH], BF16)
    xf = [ctx.sbuf(f"xf{i}", [128, D], F32) for i in range(2)]
    xn_t = ctx.sbuf("xn_t", [128, D], BF16)
    identb = ctx.sbuf("identb", [128, 128], BF16)
    stat = ctx.sbuf("stat", [128, 8], F32)
    sq = ctx.sbuf("sq", [128, D], F32)
    parts_sb = ctx.sbuf("parts_sb", [128, 16], F32)
    pg = ctx.sbuf("pg", [128, 8 * 16], F32)
    hg = ctx.sbuf("hg", [128, 8 * 1024], F32)
    halo_sb = ctx.sbuf("halo_sb", [128, D], F32)
    sig_col = ctx.sbuf("sig_col", [128, 8], F32)
    wsel_sb = ctx.sbuf("wsel_sb", [128, 16], F32)
    hsel_sb = ctx.sbuf("hsel_sb", [128, 8], F32)
    ps_t = ctx.psum("ps_t", [128, 1024], BF16)
    ps_a = [ctx.psum(f"ps_a{i}", [128, 512], F32) for i in range(2)]

    sld = ctx.sem("sld")
    swl = ctx.sem("swl")
    s_t = ctx.sem("s_t")
    s_a = ctx.sem("s_a")
    s_v = ctx.sem("s_v")
    ssc = ctx.sem("ssc")
    scc = ctx.sem("scc")
    swl_base = swl.n

    @block.gpsimd
    def _(g):
        swl.dma(g.dma_start(out=identb[:], in_=ident[:]))
        swl.dma(g.dma_start(out=wsel_sb[:], in_=wsel[:]))
        swl.dma(g.dma_start(out=hsel_sb[:], in_=hsel[:]))
        for n in range(8):
            swl.dma(g.dma_start(
                out=eq_sb[:, :].rearrange("p (c u) -> p c u", c=8)[:, :, n * 128:(n + 1) * 128],
                in_=encq[n].rearrange("(c p) u -> p c u", p=128)))
    consts_w = swl.n

    ln_done = []
    ev_done = []
    for i in range(NBLK):
        @block.sync
        def _(sync, i=i):
            if i >= 2:
                sync.wait_ge(s_v.h, ln_done[i - 2])
            sld.dma(sync.dma_start(out=xf[i % 2][:], in_=x2[i * 128:(i + 1) * 128, :]))

        @block.vector
        def _(v, i=i):
            sld.wait(v)

        emit_ln_stats(block, xf[i % 2][:], stat, sq[:], s_v, s_a)

        @block.vector
        def _(v, i=i):
            if i >= 1:
                v.wait_ge(s_v.h, ev_done[i - 1])
            v.tensor_scalar(xn_t[:], xf[i % 2][:], stat[:, 0:1], stat[:, 3:4],
                            ALU.add, ALU.mult)
            s_v.inc(v.drain())
            ln_done.append(s_v.n)

        @block.tensor
        def _(t, i=i):
            swl.wait(t, swl_base + 16)
            s_v.wait(t)
            for c in range(8):
                ins = t.transpose(ps_t[:, c * 128:(c + 1) * 128],
                                  xn_t[:, c * 128:(c + 1) * 128], identb[:])
            s_t.inc(ins)

        @block.vector
        def _(v, i=i):
            s_t.wait(v)
            v.tensor_copy(fm(xnf)[:, :, i * 128:(i + 1) * 128], fm(ps_t)[:, :, :])
            s_v.inc(v.drain())
            ev_done.append(s_v.n)

    @block.sync
    def _(sync):
        sync.wait_ge(s_v.h, ln_done[15])
        ssc.dma(sync.dma_start(out=cc_in[:, 16:1040], in_=xf[1][:]))

    qp_t = []
    qp_a = []
    qr_done = []
    for n in range(8):
        for g in range(4):
            kk = len(qp_t)

            @block.tensor
            def _(t, n=n, g=g, kk=kk):
                swl.wait(t, consts_w)
                if kk == 0:
                    t.wait_ge(s_v.h, ev_done[-1])
                if kk >= 2:
                    t.wait_ge(s_a.h, qp_a[kk - 2])
                for c in range(8):
                    ins = t.matmul(ps_a[kk % 2][:],
                                   eq_sb[:, c * 1024 + n * 128:c * 1024 + (n + 1) * 128],
                                   fm(xnf)[:, c, g * 512:(g + 1) * 512],
                                   start=(c == 0), stop=(c == 7))
                s_t.inc(ins)
                qp_t.append(s_t.n)

            @block.scalar
            def _(s, n=n, g=g, kk=kk):
                s_t.wait(s)
                if g == 0 and qr_done:
                    s.wait_ge(s_v.h, qr_done[-1])
                ins = s.activation(out=qbuf[:, g * 512:(g + 1) * 512],
                                   in_=ps_a[kk % 2][:], func=AF.Relu,
                                   bias=0.0, scale=1.0)
                s_a.inc(ins)
                qp_a.append(s_a.n)

        @block.vector
        def _(v, n=n):
            s_a.wait(v)
            v.reduce_sum(parts_sb[:, n:n + 1], qbuf[:, :], axis=AX.X)
            v.drain()
            v.reduce_sum(parts_sb[:, 8 + n:9 + n],
                         qbuf[:, :].rearrange("p (a b) -> p b a", b=8)[:, 0:1, :],
                         axis=AX.X)
            s_v.inc(v.drain())
            qr_done.append(s_v.n)

    @block.sync
    def _(sync):
        sync.wait_ge(s_v.h, qr_done[-1])
        ssc.dma(sync.dma_start(out=cc_in[:, 0:16], in_=parts_sb[:]))

    @block.gpsimd
    def _(g):
        g.wait_ge(ssc.h, ssc.n)
        ins = g.collective_compute(
            "AllGather", ALU.bypass,
            replica_groups=[list(range(N_CORES))],
            ins=[io["cc_in_h"].ap().opt()],
            outs=[io["cc_out_h"].ap().opt()])
        scc.inc(ins, 1)

    @block.gpsimd
    def _(g):
        g.wait_ge(scc.h, 1)
        swl.dma(g.dma_start(
            out=pg[:, :].rearrange("p (r j) -> p r j", r=8),
            in_=cc_out.rearrange("(r p) c -> p r c", p=128)[:, :, 0:16]))
        swl.dma(g.dma_start(
            out=hg[:, :].rearrange("p (r j) -> p r j", r=8),
            in_=cc_out.rearrange("(r p) c -> p r c", p=128)[:, :, 16:1040]))
    cc_loaded = swl.n

    @block.vector
    def _(v):
        v.wait_ge(swl.h, cc_loaded)
        v.memset(halo_sb[:], 0.0)
        v.drain()
        for r in range(8):
            v.scalar_tensor_tensor(
                halo_sb[:], hg[:, r * 1024:(r + 1) * 1024],
                hsel_sb[:, r:r + 1], halo_sb[:],
                op0=ALU.mult, op1=ALU.add)
            v.drain()
        v.memset(sig_col[:], 0.0)
        v.drain()
        for r in range(8):
            for k in range(2):
                j0 = r * 16 + k * 8
                v.scalar_tensor_tensor(
                    sig_col[:], pg[:, j0:j0 + 8],
                    wsel_sb[:, r * 2 + k:r * 2 + k + 1], sig_col[:],
                    op0=ALU.mult, op1=ALU.add)
                v.drain()
        s_v.inc(v.nop())
    comb_done = s_v.n

    @block.sync
    def _(sync):
        sync.wait_ge(s_v.h, comb_done)
        ssc.dma(sync.dma_start(out=halo_d[:], in_=halo_sb[:]))
        ssc.dma(sync.dma_start(out=sig_d[:], in_=sig_col[:]))
        ssc.wait(sync)



def emit_l1attn(nc, ctx, block, io):
    x2 = io["x2"]; x3 = io["x3"]
    encq = io["encq"]; encv = io["encv"]; dec = io["dec"]; heb = io["heb"]
    mw1 = io["mw1"]; mw2 = io["mw2"]; mb1 = io["mb1"]; mb2 = io["mb2"]
    base = io["base"]; band = io["band"]; ident = io["ident"]
    halo_d = io["halo_d"]; sig_d = io["sig_d"]

    TH = TCH + 128
    xnf = ctx.sbuf("xnf", [128, 8 * TH], BF16)
    zf = ctx.sbuf("zf", [128, 8 * TCH], BF16)
    onesf = ctx.sbuf("onesf", [128, 128], F32)
    onesc = ctx.sbuf("onesc", [128, 1], F32)
    identb = ctx.sbuf("identb", [128, 128], BF16)
    stat = ctx.sbuf("stat", [128, 8], F32)
    sq = ctx.sbuf("sq", [128, D], F32)
    xf = [ctx.sbuf(f"xf{i}", [128, D], F32) for i in range(2)]
    xn_t = ctx.sbuf("xn_t", [128, D], BF16)
    t1 = ctx.sbuf("t1", [128, D], F32)
    out_sb = [ctx.sbuf(f"out_sb{i}", [128, D], F32) for i in range(2)]
    x2e = [ctx.sbuf(f"x2e{i}", [128, D], F32) for i in range(2)]
    es_S = contextlib.ExitStack()
    ctxS = ctx.sub(es_S, "S")
    eq_sb = ctxS.sbuf("eq_sb", [128, 8 * 1024], BF16)
    ev_sb = ctxS.sbuf("ev_sb", [128, 8 * 1024], BF16)
    heb_sb = ctxS.sbuf("heb_sb", [128, 8 * 128], BF16)
    qb_buf = ctxS.sbuf("qb_buf", [128, TH], BF16)
    qt_buf = ctxS.sbuf("qt_buf", [128, TH], BF16)
    v_buf = ctxS.sbuf("v_buf", [128, TH], BF16)
    sq512 = ctxS.sbuf("sq512", [128, 512], F32)
    pt_sb = ctxS.sbuf("pt_sb", [128, 256], BF16)
    band_sb = ctxS.sbuf("band_sb", [128, 256], F32)
    rows_t = ctxS.sbuf("rows_t", [128, 6144], F32)
    sig_col = ctxS.sbuf("sig_col", [128, 16], F32)
    sig_colb = ctxS.sbuf("sig_colb", [128, 16], BF16)
    sqm_pp = ctxS.sbuf("sqm_pp", [128, 8], F32)
    wtp = ctxS.sbuf("wtp", [128, 16], F32)
    denb = ctxS.sbuf("denb", [128, 16], F32)
    rden = ctxS.sbuf("rden", [128, 16], F32)
    hb_sb = ctxS.sbuf("hb_sb", [128, 128], F32)
    t2_sb = ctxS.sbuf("t2_sb", [128, 128], F32)
    t3_sb = ctxS.sbuf("t3_sb", [128, 128], F32)

    ps_t = ctx.psum("ps_t", [128, 1024], BF16)
    ps_a = ctx.psum("ps_a", [128, 512], F32)
    ps_b = ctx.psum("ps_b", [128, 512], F32)
    ps_tp = ctx.psum("ps_tp", [128, 256], BF16)
    ps_y = ctx.psum("ps_y", [128, 128], F32)
    ps_ctx = ctx.psum("ps_ctx", [128, 128], F32)
    ps_o = ctx.psum("ps_o", [128, 1024], F32)

    sld = ctx.sem("sld")
    swl = ctx.sem("swl")
    s_t = ctx.sem("s_t")
    s_a = ctx.sem("s_a")
    s_v = ctx.sem("s_v")
    s_p = ctx.sem("s_p")
    sst = ctx.sem("sst")
    scc = ctx.sem("scc")
    swl_base = swl.n

    es_mw = contextlib.ExitStack()
    ctxM = ctx.sub(es_mw, "M")
    mw1_sb = ctxM.sbuf("mw1_sb", [128, 8 * 512], BF16)
    mw2_sb = ctxM.sbuf("mw2_sb", [128, 4 * 1024], BF16)

    @block.gpsimd
    def _(g):
        swl.dma(g.dma_start(out=identb[:], in_=ident[:]))
        swl.dma(g.dma_start(out=band_sb[:], in_=band[:]))
        swl.dma(g.dma_start(out=rows_t[64:65, 2560:3072], in_=mb1[:]))
        swl.dma(g.dma_start(out=rows_t[64:65, 3072:4096], in_=mb2[:]))
        swl.dma(g.dma_start(out=rows_t[64:65, 4096:5120], in_=base[:]))
        swl.dma(g.dma_start(out=sig_col[:, 0:8], in_=sig_d[:]))
        swl.dma(g.dma_start(out=sig_colb[:, 0:8], in_=sig_d[:]))
        for n in range(8):
            swl.dma(g.dma_start(
                out=eq_sb[:, :].rearrange("p (c u) -> p c u", c=8)[:, :, n * 128:(n + 1) * 128],
                in_=encq[n].rearrange("(c p) u -> p c u", p=128)))
            swl.dma(g.dma_start(
                out=ev_sb[:, :].rearrange("p (c u) -> p c u", c=8)[:, :, n * 128:(n + 1) * 128],
                in_=encv[n].rearrange("(c p) u -> p c u", p=128)))
            swl.dma(g.dma_start(out=heb_sb[:, n * 128:(n + 1) * 128], in_=heb[n]))
            swl.dma(g.dma_start(out=fm(mw1_sb)[:, n, :], in_=mw1[n]))
        for c in range(4):
            swl.dma(g.dma_start(out=fm(mw2_sb, c=4)[:, c, :], in_=mw2[c]))
    consts_w = swl.n

    @block.vector
    def _(v):
        v.memset(onesf[:], 1.0)
        v.drain()
        v.memset(onesc[:], 1.0)
        s_v.inc(v.drain())

    ln_done = []
    ev_done = []
    for i in range(17):
        @block.sync
        def _(sync, i=i):
            if i >= 2:
                sync.wait_ge(s_v.h, ln_done[i - 2])
            if i == 0:
                sld.dma(sync.dma_start(out=xf[0][:], in_=halo_d[:]))
            else:
                sld.dma(sync.dma_start(out=xf[i % 2][:],
                                       in_=x2[(i - 1) * 128:i * 128, :]))

        @block.vector
        def _(v):
            sld.wait(v)

        emit_ln_stats(block, xf[i % 2][:], stat, sq[:], s_v, s_a)

        @block.vector
        def _(v, i=i):
            if i >= 1:
                v.wait_ge(s_v.h, ev_done[i - 1])
            v.tensor_scalar(xn_t[:], xf[i % 2][:], stat[:, 0:1], stat[:, 3:4],
                            ALU.add, ALU.mult)
            s_v.inc(v.drain())
            ln_done.append(s_v.n)

        @block.tensor
        def _(t, i=i):
            swl.wait(t, swl_base + 16)
            s_v.wait(t)
            for c in range(8):
                ins = t.transpose(ps_t[:, c * 128:(c + 1) * 128],
                                  xn_t[:, c * 128:(c + 1) * 128], identb[:])
            s_t.inc(ins)

        @block.vector
        def _(v, i=i):
            s_t.wait(v)
            v.tensor_copy(fm(xnf)[:, :, i * 128:(i + 1) * 128], fm(ps_t)[:, :, :])
            s_v.inc(v.drain())
            ev_done.append(s_v.n)

    @block.tensor
    def _(t):
        swl.wait(t, consts_w)
        for c in range(8):
            ins = t.matmul(ps_b[0:1, :], sig_colb[:, c:c + 1],
                           mw1_sb[:, c * 512:(c + 1) * 512],
                           start=(c == 0), stop=(c == 7))
        s_t.inc(ins)

    @block.vector
    def _(v):
        s_t.wait(v)
        v.tensor_add(rows_t[64:65, 1024:1536], ps_b[0:1, :], rows_t[64:65, 2560:3072])
        s_v.inc(v.drain())

    @block.scalar
    def _(s):
        s_v.wait(s)
        s.activation(out=rows_t[64:65, 1024:1536], in_=rows_t[64:65, 1024:1536],
                     func=AF.Gelu, bias=0.0, scale=1.0)
        s_a.inc(s.drain())

    @block.tensor
    def _(t):
        s_a.wait(t)
        for c in range(4):
            ins = t.transpose(ps_y[:, 8 + c:9 + c],
                              rows_t[64:65, 1024 + c * 128:1024 + (c + 1) * 128],
                              onesf[64:65, 64:65])
        s_t.inc(ins)

    @block.vector
    def _(v):
        s_t.wait(v)
        v.tensor_copy(sig_col[:, 8:12], ps_y[:, 8:12])
        v.drain()
        v.tensor_copy(sig_colb[:, 8:12], sig_col[:, 8:12])
        s_v.inc(v.drain())

    @block.tensor
    def _(t):
        s_v.wait(t)
        for nh in range(2):
            for c in range(4):
                ins = t.matmul(ps_o[0:1, nh * 512:(nh + 1) * 512],
                               sig_colb[:, 8 + c:9 + c],
                               mw2_sb[:, c * 1024 + nh * 512:c * 1024 + nh * 512 + 512],
                               start=(c == 0), stop=(c == 3))
        s_t.inc(ins)

    @block.vector
    def _(v):
        s_t.wait(v)
        v.tensor_add(rows_t[64:65, 1536:2560], ps_o[0:1, :], rows_t[64:65, 3072:4096])
        v.drain()
        v.scalar_tensor_tensor(rows_t[64:65, 1536:2560], rows_t[64:65, 1536:2560],
                               0.1, rows_t[64:65, 4096:5120], op0=ALU.mult, op1=ALU.add)
        s_v.inc(v.drain())

    @block.scalar
    def _(s):
        s_v.wait(s)
        s.activation(out=rows_t[64:65, 5120:6144], in_=rows_t[64:65, 1536:2560],
                     func=AF.Abs, bias=0.0, scale=1.0)
        s.drain()
        s.activation(out=rows_t[64:65, 5120:6144], in_=rows_t[64:65, 5120:6144],
                     func=AF.Exp, bias=0.0, scale=-1.0)
        s.drain()
        s.activation(out=rows_t[64:65, 5120:6144], in_=rows_t[64:65, 5120:6144],
                     func=AF.Ln, bias=onesf[64:65, 64:65], scale=1.0)
        s_a.inc(s.drain())

    @block.vector
    def _(v):
        s_a.wait(v)
        v.tensor_scalar_max(rows_t[64:65, 1536:2560], rows_t[64:65, 1536:2560], 0.0)
        v.drain()
        v.tensor_add(rows_t[64:65, 1536:2560], rows_t[64:65, 1536:2560],
                     rows_t[64:65, 5120:6144])
        v.drain()
        v.tensor_scalar_add(rows_t[64:65, 1536:2560], rows_t[64:65, 1536:2560], 1e-6)
        s_v.inc(v.drain())

    @block.scalar
    def _(s):
        s_v.wait(s)
        s.activation(out=rows_t[64:65, 1536:2560], in_=rows_t[64:65, 1536:2560],
                     func=AF.Sqrt, bias=0.0, scale=1.0)
        s_a.inc(s.drain())

    @block.tensor
    def _(t):
        s_a.wait(t)
        for c in range(8):
            ins = t.transpose(ps_y[:, c:c + 1],
                              rows_t[64:65, 1536 + c * 128:1536 + (c + 1) * 128],
                              onesf[64:65, 64:65])
        s_t.inc(ins)

    @block.vector
    def _(v):
        s_t.wait(v)
        v.tensor_copy(sqm_pp[:, 0:8], ps_y[:, 0:8])
        s_v.inc(v.drain())
    sqm_done = s_v.n

    es_mw.close()
    dpre = ctxS.sbuf("dpre", [128, 16 * 256], F32)
    pbuf = ctxS.sbuf("pbuf", [128, 16 * 256], BF16)
    gbounds = [(0, 512), (512, 512), (1024, 512), (1536, 512), (2048, 128)]
    nq_t = []
    nq_a = []
    n_zf = []

    def ones_row(n, src_sbuf, dst_p):
        for gi, (t0_, tw) in enumerate(gbounds):
            kk = len(nq_t)

            @block.scalar
            def _(s, t0_=t0_, tw=tw, kk=kk):
                if nq_t:
                    s.wait_ge(s_t.h, nq_t[-1])
                ins = s.activation(out=sq512[:, 0:tw], in_=src_sbuf[:, t0_:t0_ + tw],
                                   func=AF.Square, bias=0.0, scale=1.0)
                s_a.inc(ins)
                sq_a = s_a.n

                @block.tensor
                def _(t, kk=kk, tw=tw, sq_a=sq_a):
                    t.wait_ge(s_a.h, sq_a)
                    if kk >= 2:
                        t.wait_ge(s_a.h, nq_a[kk - 2])
                    psx = [ps_a, ps_b][kk % 2]
                    ins = t.matmul(psx[0:1, 0:tw], onesc[:], sq512[:, 0:tw],
                                   start=True, stop=True)
                    s_t.inc(ins)
                    nq_t.append(s_t.n)

            @block.scalar
            def _(s, t0_=t0_, tw=tw, kk=kk, dst_p=dst_p):
                s.wait_ge(s_t.h, nq_t[kk])
                psx = [ps_a, ps_b][kk % 2]
                ins = s.activation(out=rows_t[dst_p:dst_p + 1, t0_:t0_ + tw],
                                   in_=psx[0:1, 0:tw], func=AF.Copy,
                                   bias=0.0, scale=1.0)
                s_a.inc(ins)
                nq_a.append(s_a.n)

    for n in range(8):
        for gi, (t0_, tw) in enumerate(gbounds):
            kk = len(nq_t)

            @block.tensor
            def _(t, n=n, t0_=t0_, tw=tw, kk=kk):
                if kk == 0:
                    t.wait_ge(s_v.h, ev_done[-1])
                    swl.wait(t, consts_w)
                if kk >= 2:
                    t.wait_ge(s_a.h, nq_a[kk - 2])
                psx = [ps_a, ps_b][kk % 2]
                for c in range(8):
                    ins = t.matmul(psx[:, 0:tw],
                                   eq_sb[:, c * 1024 + n * 128:c * 1024 + (n + 1) * 128],
                                   fm(xnf)[:, c, t0_:t0_ + tw],
                                   start=(c == 0), stop=(c == 7))
                s_t.inc(ins)
                nq_t.append(s_t.n)

            @block.scalar
            def _(s, n=n, gi=gi, t0_=t0_, tw=tw, kk=kk):
                s.wait_ge(s_t.h, nq_t[kk])
                if gi == 0 and n_zf:
                    s.wait_ge(s_v.h, n_zf[-1])
                psx = [ps_a, ps_b][kk % 2]
                ins = s.activation(out=qb_buf[:, t0_:t0_ + tw], in_=psx[:, 0:tw],
                                   func=AF.Relu, bias=0.0, scale=1.0)
                s_a.inc(ins)
                nq_a.append(s_a.n)

        for tb in range(17):
            kk = len(nq_t)

            @block.tensor
            def _(t, n=n, tb=tb, kk=kk):
                if kk >= 2:
                    t.wait_ge(s_a.h, nq_a[kk - 2])
                psx = [ps_a, ps_b][kk % 2]
                for c in range(8):
                    ins = t.matmul(psx[:, 0:128],
                                   fm(xnf)[:, c, tb * 128:(tb + 1) * 128],
                                   ev_sb[:, c * 1024 + n * 128:c * 1024 + (n + 1) * 128],
                                   start=(c == 0), stop=(c == 7))
                s_t.inc(ins)
                nq_t.append(s_t.n)

            @block.scalar
            def _(s, n=n, tb=tb, kk=kk):
                s.wait_ge(s_t.h, nq_t[kk])
                psx = [ps_a, ps_b][kk % 2]
                ins = s.activation(out=v_buf[:, tb * 128:(tb + 1) * 128],
                                   in_=psx[:, 0:128],
                                   func=AF.Relu, bias=0.0, scale=1.0)
                s_a.inc(ins)
                nq_a.append(s_a.n)

        @block.vector
        def _(v, n=n):
            s_a.wait(v)
            v.wait_ge(s_v.h, sqm_done)
            v.tensor_scalar_mul(qt_buf[:], qb_buf[:], sqm_pp[:, n:n + 1])
            s_v.inc(v.drain())

        @block.scalar
        def _(s, n=n):
            s_v.wait(s)
        ones_row(n, qt_buf[:, :], 0)
        ones_row(n, qb_buf[:, :], 32)

        @block.scalar
        def _(s):
            s.wait_ge(s_a.h, nq_a[-1])
            s.activation(out=rows_t[32:33, 0:TH],
                         in_=rows_t[32:33, 0:TH], func=AF.Sqrt,
                         bias=0.0, scale=1.0)
            s_a.inc(s.drain())

        @block.vector
        def _(v):
            s_a.wait(v)
            v.tensor_scalar_max(rows_t[32:33, 0:TH],
                                rows_t[32:33, 0:TH], 1e-12)
            v.drain()
            v.reciprocal(rows_t[32:33, 0:TH], rows_t[32:33, 0:TH])
            s_v.inc(v.drain())
        rnorm_done = s_v.n

        @block.tensor
        def _(t, n=n):
            s_v.wait(t)
            for qb in range(16):
                ins = t.transpose(ps_y[:, qb:qb + 1],
                                  rows_t[0:1, (qb + 1) * 128:(qb + 2) * 128],
                                  onesf[0:1, 0:1])
            s_t.inc(ins)

        @block.vector
        def _(v):
            s_t.wait(v)
            v.tensor_scalar_add(wtp[:, 0:16], ps_y[:, 0:16], 1e-8)
            s_v.inc(v.drain())
        wtp_done = s_v.n

        al_t = []
        al_v = []
        for qb in range(16):
            @block.tensor
            def _(t, n=n, qb=qb):
                t.wait_ge(s_v.h, wtp_done)
                if qb >= 2:
                    t.wait_ge(s_v.h, al_v[qb - 2])
                psw = [ps_a, ps_b][qb % 2]
                t.matmul(psw[0:128, 0:256], onesf[0:1, 0:128],
                         rows_t[0:1, qb * 128:qb * 128 + 256],
                         start=True, stop=True)
                ins = t.matmul(psw[0:128, 256:512],
                               qt_buf[:, (qb + 1) * 128:(qb + 2) * 128],
                               qt_buf[:, qb * 128:qb * 128 + 256],
                               start=True, stop=True)
                s_t.inc(ins)
                al_t.append(s_t.n)

            @block.vector
            def _(v, n=n, qb=qb):
                v.wait_ge(s_t.h, al_t[qb])
                psw = [ps_a, ps_b][qb % 2]
                v.tensor_add(dpre[:, qb * 256:(qb + 1) * 256],
                             psw[0:128, 0:256], band_sb[:])
                v.drain()
                v.scalar_tensor_tensor(dpre[:, qb * 256:(qb + 1) * 256],
                                       psw[0:128, 256:512], -2.0,
                                       dpre[:, qb * 256:(qb + 1) * 256],
                                       op0=ALU.mult, op1=ALU.add)
                v.drain()
                v.tensor_scalar(dpre[:, qb * 256:(qb + 1) * 256],
                                dpre[:, qb * 256:(qb + 1) * 256],
                                wtp[:, qb:qb + 1], 1e-8, ALU.add, ALU.max)
                s_v.inc(v.drain())
                al_v.append(s_v.n)

        @block.scalar
        def _(s, n=n):
            s_v.wait(s)
            for qb in range(16):
                ins = s.activation(out=dpre[:, qb * 256:(qb + 1) * 256],
                                   in_=dpre[:, qb * 256:(qb + 1) * 256],
                                   func=AF.Sqrt, bias=0.0, scale=1.0)
            s.drain()
            for qb in range(16):
                ins = s.activation(out=pbuf[:, qb * 256:(qb + 1) * 256],
                                   in_=dpre[:, qb * 256:(qb + 1) * 256],
                                   func=AF.Exp, bias=0.0, scale=-1.0,
                                   accum_out=denb[:, qb:qb + 1])
            s_a.inc(s.drain())

        @block.vector
        def _(v, n=n):
            s_a.wait(v)
            v.tensor_scalar_add(rden[:, 0:16], denb[:, 0:16], 1e-8)
            v.drain()
            v.reciprocal(rden[:, 0:16], rden[:, 0:16])
            s_v.inc(v.drain())
        rden_done = s_v.n

        for qb in range(16):
            @block.vector
            def _(v, n=n, qb=qb):
                v.wait_ge(s_v.h, rden_done)
                v.tensor_scalar_mul(pbuf[:, qb * 256:(qb + 1) * 256],
                                    pbuf[:, qb * 256:(qb + 1) * 256],
                                    rden[:, qb:qb + 1])
                s_v.inc(v.drain())
            pn_done = s_v.n

            @block.tensor
            def _(t, n=n, qb=qb):
                t.wait_ge(s_v.h, pn_done)
                for j in range(2):
                    ins = t.transpose(ps_tp[:, j * 128:(j + 1) * 128],
                                      pbuf[:, qb * 256 + j * 128:qb * 256 + (j + 1) * 128],
                                      identb[:])
                s_t.inc(ins)
            tp_done2 = s_t.n

            @block.vector
            def _(v, n=n, qb=qb):
                v.wait_ge(s_t.h, tp_done2)
                v.tensor_copy(pt_sb[:], ps_tp[:])
                s_v.inc(v.drain())
            pt_done = s_v.n

            @block.tensor
            def _(t, n=n, qb=qb):
                t.wait_ge(s_v.h, pt_done)
                for j in range(2):
                    t.matmul(ps_ctx[:], v_buf[:, (qb + j) * 128:(qb + j + 1) * 128],
                             pt_sb[:, j * 128:(j + 1) * 128],
                             start=(j == 0), stop=(j == 1))
                ins = t.matmul(ps_y[:], heb_sb[:, n * 128:(n + 1) * 128],
                               qb_buf[:, (qb + 1) * 128:(qb + 2) * 128],
                               start=True, stop=True)
                s_t.inc(ins)
            hc_done = s_t.n

            @block.scalar
            def _(s, n=n, qb=qb):
                s.wait_ge(s_t.h, hc_done)
                s.activation(out=hb_sb[:], in_=ps_y[:], func=AF.Copy,
                             bias=0.0, scale=1.0)
                s_a.inc(s.drain())
            hb_done = s_a.n

            @block.tensor
            def _(t, n=n, qb=qb):
                t.wait_ge(s_a.h, hb_done)
                t.wait_ge(s_v.h, rnorm_done)
                ins = t.matmul(ps_y[:], onesf[32:33, 0:128],
                               rows_t[32:33, (qb + 1) * 128:(qb + 2) * 128],
                               start=True, stop=True)
                s_t.inc(ins)
            rb_done = s_t.n

            @block.vector
            def _(v, n=n, qb=qb):
                v.wait_ge(s_t.h, rb_done)
                v.tensor_mul(t2_sb[:], hb_sb[:], ps_y[:])
                v.drain()
                v.tensor_add(t3_sb[:], t2_sb[:], ps_ctx[:])
                v.drain()
                v.tensor_mul(zf[:, n * TCH + qb * 128:n * TCH + (qb + 1) * 128],
                             t3_sb[:], qb_buf[:, (qb + 1) * 128:(qb + 2) * 128])
                s_v.inc(v.drain())
                n_zf.append(s_v.n)

    es_S.close()
    dec_sb = ctx.sbuf("dec_sb", [128, 8 * 1024], BF16)

    @block.gpsimd
    def _(g):
        g.wait_ge(s_v.h, n_zf[-1])
        for c in range(8):
            swl.dma(g.dma_start(out=fm(dec_sb)[:, c, :], in_=dec[c]))
    dec_loaded = swl.n

    ep_done = []
    ep_ln = []
    st_cnt = []
    for tb in range(NBLK):
        @block.sync
        def _(sync, tb=tb):
            if tb >= 2:
                sync.wait_ge(s_p.h, ep_done[tb - 2])
            sld.dma(sync.dma_start(out=x2e[tb % 2][:],
                                   in_=x2[tb * 128:(tb + 1) * 128, :]))

        @block.tensor
        def _(t, tb=tb):
            if tb == 0:
                t.wait_ge(s_v.h, n_zf[-1])
                t.wait_ge(swl.h, dec_loaded)
            if tb >= 1:
                t.wait_ge(s_v.h, ep_ln[tb - 1])
            for c in range(8):
                for nh in range(2):
                    ins = t.matmul(
                        ps_o[:, nh * 512:(nh + 1) * 512],
                        zf[:, c * TCH + tb * 128:c * TCH + (tb + 1) * 128],
                        dec_sb[:, c * 1024 + nh * 512:c * 1024 + nh * 512 + 512],
                        start=(c == 0), stop=(c == 7))
            s_t.inc(ins)

        @block.vector
        def _(v):
            s_t.wait(v)

        emit_ln_stats(block, ps_o[:], stat, sq[:], s_v, s_a)

        @block.vector
        def _(v, tb=tb):
            if tb >= 1:
                v.wait_ge(s_p.h, ep_done[tb - 1])
            v.tensor_scalar(t1[:], ps_o[:], stat[:, 0:1], stat[:, 3:4],
                            ALU.add, ALU.mult)
            s_v.inc(v.drain())
            ep_ln.append(s_v.n)

        @block.vector
        def _(v, tb=tb):
            sld.wait(v)
            if tb >= 2:
                v.wait_ge(sst.h, st_cnt[tb - 2])
            ins = v.tensor_add(out_sb[tb % 2][:], t1[:], x2e[tb % 2][:])
            s_p.inc(ins)
            ep_done.append(s_p.n)

        @block.sync
        def _(sync, tb=tb):
            sync.wait_ge(s_p.h, ep_done[tb])
            sst.dma(sync.dma_start(out=x3[tb * 128:(tb + 1) * 128, :],
                                   in_=out_sb[tb % 2][:]))
            st_cnt.append(sst.n)

    @block.sync
    def _(sync):
        sst.wait(sync)



def emit_l1ffn(nc, ctx, block, io):
    x3 = io["x3"]; x4 = io["x4"]; ident = io["ident"]
    wrt = io["wrt"]; br = io["br"]; w1e = io["w1e"]; b1l = io["b1l"]
    w2e = io["w2e"]; b2i = io["b2i"]; gam = io["gam"]; bet = io["bet"]

    xnf = ctx.sbuf("xnf", [128, 8 * TCH], BF16)
    ubuf = ctx.sbuf("ubuf", [128, 32 * 512], BF16)
    w2sb = ctx.sbuf("w2sb", [128, 32 * 1024], BF16)
    acc = ctx.sbuf("acc", [128, 4 * 1024], F32)
    stats_all = ctx.sbuf("stats_all", [128, 32], F32)
    rw_all = ctx.sbuf("rw_all", [128, 8 * 16], F32)
    rwT_all = ctx.sbuf("rwT_all", [8, 16 * 128], BF16)
    rwb = ctx.sbuf("rwb", [128, 8], BF16)
    wr_sb = ctx.sbuf("wr_sb", [128, 8 * 8], BF16)
    br_sb = ctx.sbuf("br_sb", [1, 8], BF16)
    b1_sb = ctx.sbuf("b1_sb", [128, 8 * 32], F32)
    b2_sb = ctx.sbuf("b2_sb", [8, 1024], BF16)
    gb_bc = ctx.sbuf("gb_bc", [128, 2048], F32)
    g_row = ctx.sbuf("g_row", [1, 1024], BF16)
    b_row = ctx.sbuf("b_row", [1, 1024], BF16)
    onesb = ctx.sbuf("onesb", [1, 128], BF16)
    wsl = [ctx.sbuf(f"wsl{i}", [128, 8 * 128], BF16) for i in range(2)]
    logit_sb = ctx.sbuf("logit_sb", [128, 8], F32)
    top_sb = ctx.sbuf("top_sb", [128, 8], F32)
    vv_sb = ctx.sbuf("vv_sb", [128, 4], F32)
    identb = ctx.sbuf("identb", [128, 128], BF16)
    stat = ctx.sbuf("stat", [128, 8], F32)
    sq = ctx.sbuf("sq", [128, D], F32)
    xf = [ctx.sbuf(f"xf{i}", [128, D], F32) for i in range(2)]
    xn_t = ctx.sbuf("xn_t", [128, D], BF16)
    t1 = ctx.sbuf("t1", [128, D], F32)
    t2 = ctx.sbuf("t2", [128, D], F32)
    inner = ctx.sbuf("inner", [128, D], F32)
    out_sb = [ctx.sbuf(f"out_sb{i}", [128, D], F32) for i in range(2)]
    x3e = xf

    ps_t = ctx.psum("ps_t", [128, 1024], BF16)
    ps_u = [ctx.psum(f"ps_u{i}", [128, 512], F32) for i in range(2)]
    ps_o = ctx.psum("ps_o", [128, 1024], F32)
    ps_r = ctx.psum("ps_r", [128, 8], F32)
    ps_rt = ctx.psum("ps_rt", [8, 128], BF16)

    sld = ctx.sem("sld")
    swl = ctx.sem("swl")
    sw2 = ctx.sem("sw2")
    swu = [ctx.sem(f"swu{i}") for i in range(2)]
    s_t = ctx.sem("s_t")
    s_a = ctx.sem("s_a")
    s_v = ctx.sem("s_v")
    s_p = ctx.sem("s_p")
    sst = ctx.sem("sst")
    swl_base = swl.n

    @block.gpsimd
    def _(g):
        swl.dma(g.dma_start(out=identb[:], in_=ident[:]))
        for c in range(8):
            swl.dma(g.dma_start(out=wr_sb[:, c * 8:(c + 1) * 8], in_=wrt[c]))
        swl.dma(g.dma_start(out=br_sb[:], in_=br[:]))
        swl.dma(g.dma_start(out=b1_sb[:], in_=b1l[:]))
        swl.dma(g.dma_start(out=b2_sb[:], in_=b2i[:]))
        swl.dma(g.dma_start(out=g_row[:], in_=gam[:]))
        swl.dma(g.dma_start(out=b_row[:], in_=bet[:]))
    consts_w = swl.n

    @block.vector
    def _(v):
        v.memset(onesb[:], 1.0)
        s_v.inc(v.drain())
    ones_done = s_v.n

    @block.tensor
    def _(t):
        swl.wait(t, consts_w)
        t.wait_ge(s_v.h, ones_done)
        for nh in range(2):
            ins = t.matmul(ps_o[:, nh * 512:(nh + 1) * 512], onesb[:],
                           g_row[0:1, nh * 512:(nh + 1) * 512],
                           start=True, stop=True)
        s_t.inc(ins)

    @block.vector
    def _(v):
        s_t.wait(v)
        v.tensor_copy(gb_bc[:, 0:1024], ps_o[:])
        s_v.inc(v.drain())

    @block.tensor
    def _(t):
        s_v.wait(t)
        for nh in range(2):
            ins = t.matmul(ps_o[:, nh * 512:(nh + 1) * 512], onesb[:],
                           b_row[0:1, nh * 512:(nh + 1) * 512],
                           start=True, stop=True)
        s_t.inc(ins)

    @block.vector
    def _(v):
        s_t.wait(v)
        v.tensor_copy(gb_bc[:, 1024:2048], ps_o[:])
        s_v.inc(v.drain())
    gb_done = s_v.n

    ln_done = []
    ev_done = []
    for i in range(NBLK):
        @block.sync
        def _(sync, i=i):
            if i >= 2:
                sync.wait_ge(s_v.h, ln_done[i - 2])
            sld.dma(sync.dma_start(out=xf[i % 2][:], in_=x3[i * 128:(i + 1) * 128, :]))

        @block.vector
        def _(v, i=i):
            sld.wait(v)

        emit_ln_stats(block, xf[i % 2][:], stat, sq[:], s_v, s_a)

        @block.vector
        def _(v, i=i):
            if i >= 1:
                v.wait_ge(s_v.h, ev_done[i - 1])
            v.tensor_copy(stats_all[:, 2 * i:2 * i + 1], stat[:, 0:1])
            v.drain()
            v.tensor_copy(stats_all[:, 2 * i + 1:2 * i + 2], stat[:, 3:4])
            v.drain()
            v.tensor_scalar(xn_t[:], xf[i % 2][:], stat[:, 0:1], stat[:, 3:4],
                            ALU.add, ALU.mult)
            s_v.inc(v.drain())
            ln_done.append(s_v.n)

        @block.tensor
        def _(t, i=i):
            swl.wait(t, swl_base + 16)
            s_v.wait(t)
            for c in range(8):
                ins = t.transpose(ps_t[:, c * 128:(c + 1) * 128],
                                  xn_t[:, c * 128:(c + 1) * 128], identb[:])
            s_t.inc(ins)

        @block.vector
        def _(v, i=i):
            s_t.wait(v)
            v.tensor_copy(fm(xnf)[:, :, i * 128:(i + 1) * 128], fm(ps_t)[:, :, :])
            s_v.inc(v.drain())
            ev_done.append(s_v.n)

    rt_done = []
    for i in range(NBLK):
        @block.tensor
        def _(t, i=i):
            if i == 0:
                t.wait_ge(s_v.h, ev_done[-1])
            if i >= 1:
                t.wait_ge(s_a.h, s_a.n)
            for c in range(8):
                t.matmul(ps_r[:], fm(xnf)[:, c, i * 128:(i + 1) * 128],
                         wr_sb[:, c * 8:(c + 1) * 8],
                         start=(c == 0), stop=False)
            ins = t.matmul(ps_r[:], onesb[:], br_sb[:], start=False, stop=True)
            s_t.inc(ins)

        @block.scalar
        def _(s, i=i):
            s_t.wait(s)
            ins = s.activation(out=logit_sb[:], in_=ps_r[:], func=AF.Copy,
                               bias=0.0, scale=1.0)
            s_a.inc(ins)

        @block.vector
        def _(v, i=i):
            s_a.wait(v)
            v.reduce_max(vv_sb[:, 0:1], logit_sb[:], axis=AX.X)
            v.drain()
            v.tensor_scalar(top_sb[:], logit_sb[:], vv_sb[:, 0:1], None,
                            ALU.is_equal)
            v.drain()
            v.scalar_tensor_tensor(top_sb[:], top_sb[:], -1e30, logit_sb[:],
                                   op0=ALU.mult, op1=ALU.add)
            v.drain()
            v.reduce_max(vv_sb[:, 1:2], top_sb[:], axis=AX.X)
            v.drain()
            v.tensor_sub(vv_sb[:, 2:3], vv_sb[:, 0:1], vv_sb[:, 1:2])
            s_v.inc(v.drain())

        @block.scalar
        def _(s, i=i):
            s_v.wait(s)
            s.activation(out=vv_sb[:, 3:4], in_=vv_sb[:, 2:3], func=AF.Sigmoid,
                         bias=0.0, scale=1.0)
            s_a.inc(s.drain())

        @block.vector
        def _(v, i=i):
            s_a.wait(v)
            rw = rw_all[:, i * 8:(i + 1) * 8]
            v.tensor_scalar(sq[:, 0:8], top_sb[:], vv_sb[:, 1:2], None,
                            ALU.is_equal)
            v.drain()
            v.tensor_scalar(sq[:, 8:16], logit_sb[:], vv_sb[:, 0:1], None,
                            ALU.is_equal)
            v.drain()
            v.tensor_scalar(vv_sb[:, 2:3], vv_sb[:, 3:4], -1.0, 1.0,
                            ALU.mult, ALU.add)
            v.drain()
            v.tensor_scalar_mul(rw, sq[:, 8:16], vv_sb[:, 3:4])
            v.drain()
            v.tensor_scalar_mul(sq[:, 0:8], sq[:, 0:8], vv_sb[:, 2:3])
            v.drain()
            v.tensor_add(rw, rw, sq[:, 0:8])
            v.drain()
            v.tensor_copy(rwb[:], rw)
            s_v.inc(v.drain())

        @block.tensor
        def _(t, i=i):
            s_v.wait(t)
            ins = t.transpose(ps_rt[:], rwb[:], identb[:])
            s_t.inc(ins)

        @block.vector
        def _(v, i=i):
            s_t.wait(v)
            v.tensor_copy(rwT_all[:, i * 128:(i + 1) * 128], ps_rt[0:8, :])
            s_v.inc(v.drain())
            rt_done.append(s_v.n)

    up_t = []
    up_a = []
    ob2_a = []
    oev_v = []
    ep_in = []
    ep_done = []
    ep_ln = []
    st_cnt = []
    w2l_cnt = []

    for grp in range(L1FFN_GROUPS):
        for tb in range(4):
            g_tb = grp * 4 + tb

            @block.tensor
            def _(t, grp=grp, tb=tb, g_tb=g_tb):
                if g_tb == 0:
                    t.wait_ge(s_v.h, rt_done[-1])
                if tb == 0 and oev_v:
                    t.wait_ge(s_v.h, oev_v[-1])
                if tb >= 1:
                    t.wait_ge(s_a.h, ob2_a[-1])
                for nh in range(2):
                    ins = t.matmul(
                        ps_o[:, nh * 512:(nh + 1) * 512],
                        rwT_all[:, g_tb * 128:(g_tb + 1) * 128],
                        b2_sb[:, nh * 512:(nh + 1) * 512],
                        start=True, stop=True)
                s_t.inc(ins)

            @block.scalar
            def _(s, grp=grp, tb=tb, g_tb=g_tb):
                s_t.wait(s)
                if len(ep_in) >= (grp - 1) * 4 + tb + 1 and grp >= 1:
                    s.wait_ge(s_v.h, ep_in[(grp - 1) * 4 + tb])
                ins = s.activation(out=acc[:, tb * 1024:(tb + 1) * 1024],
                                   in_=ps_o[:], func=AF.Copy, bias=0.0, scale=1.0)
                s_a.inc(ins)
                ob2_a.append(s_a.n)

        for e in range(L1FFN_EXPERTS):
            @block.gpsimd
            def _(g, grp=grp, e=e):
                if w2l_cnt:
                    g.wait_ge(s_t.h, w2l_cnt[-1])
                for c0 in range(0, 32, 8):
                    sw2.dma(g.dma_start(
                        out=w2sb[:, c0 * 1024:(c0 + 8) * 1024].rearrange(
                            "p (c o) -> p c o", c=8),
                        in_=w2e[e, c0:c0 + 8].rearrange("c p o -> p c o")))
            w2_loaded = sw2.n

            for oc in range(32):
                kk = len(up_t)

                @block.gpsimd
                def _(g, e=e, oc=oc, kk=kk):
                    if kk >= 2:
                        g.wait_ge(s_t.h, up_t[kk - 2])
                    swu[kk % 2].dma(g.dma_start(out=wsl[kk % 2][:, :], in_=w1e[e, oc]))
                ld_sem, ld_cnt = swu[kk % 2].h, swu[kk % 2].n

                @block.tensor
                def _(t, grp=grp, e=e, oc=oc, kk=kk, ld_sem=ld_sem, ld_cnt=ld_cnt):
                    t.wait_ge(ld_sem, ld_cnt)
                    if kk >= 2:
                        t.wait_ge(s_a.h, up_a[kk - 2])
                    for c in range(8):
                        ins = t.matmul(
                            ps_u[kk % 2][:],
                            wsl[kk % 2][:, c * 128:(c + 1) * 128],
                            fm(xnf)[:, c, grp * 512:(grp + 1) * 512],
                            start=(c == 0), stop=(c == 7))
                    s_t.inc(ins)
                    up_t.append(s_t.n)

                @block.scalar
                def _(s, grp=grp, e=e, oc=oc, kk=kk):
                    s.wait_ge(s_t.h, up_t[kk])
                    if oc == 0 and oev_v:
                        s.wait_ge(s_v.h, oev_v[-1])
                    ins = s.activation(out=ubuf[:, oc * 512:(oc + 1) * 512],
                                       in_=ps_u[kk % 2][:], func=AF.Relu,
                                       bias=b1_sb[:, e * 32 + oc:e * 32 + oc + 1],
                                       scale=1.0)
                    s_a.inc(ins)
                    up_a.append(s_a.n)

            for tb in range(4):
                @block.tensor
                def _(t, grp=grp, e=e, tb=tb, w2_loaded=w2_loaded):
                    t.wait_ge(sw2.h, w2_loaded)
                    s_a.wait(t)
                    if oev_v:
                        t.wait_ge(s_v.h, oev_v[-1])
                    for c in range(32):
                        for nh in range(2):
                            ins = t.matmul(
                                ps_o[:, nh * 512:(nh + 1) * 512],
                                ubuf[:, c * 512 + tb * 128:c * 512 + tb * 128 + 128],
                                w2sb[:, c * 1024 + nh * 512:c * 1024 + nh * 512 + 512],
                                start=(c == 0), stop=(c == 31))
                    s_t.inc(ins)
                o_t = s_t.n

                @block.vector
                def _(v, grp=grp, e=e, tb=tb, o_t=o_t):
                    v.wait_ge(s_t.h, o_t)
                    if e == 0:
                        v.wait_ge(s_a.h, ob2_a[grp * 4 + tb])
                    g_tb = grp * 4 + tb
                    v.scalar_tensor_tensor(
                        acc[:, tb * 1024:(tb + 1) * 1024], ps_o[:],
                        rw_all[:, g_tb * 8 + e:g_tb * 8 + e + 1],
                        acc[:, tb * 1024:(tb + 1) * 1024],
                        op0=ALU.mult, op1=ALU.add)
                    s_v.inc(v.drain())
                    oev_v.append(s_v.n)

            w2l_cnt.append(s_t.n)

        for tb in range(4):
            g_tb = grp * 4 + tb

            @block.sync
            def _(sync, g_tb=g_tb):
                if g_tb < 2:
                    sync.wait_ge(s_v.h, ln_done[-1])
                if g_tb >= 2:
                    sync.wait_ge(s_p.h, ep_done[g_tb - 2])
                sld.dma(sync.dma_start(out=x3e[g_tb % 2][:],
                                       in_=x3[g_tb * 128:(g_tb + 1) * 128, :]))

            @block.vector
            def _(v, grp=grp, tb=tb, g_tb=g_tb):
                s_v.wait(v)
                sld.wait(v)
                v.wait_ge(s_v.h, gb_done)
                v.tensor_scalar(t1[:], x3e[g_tb % 2][:],
                                stats_all[:, 2 * g_tb:2 * g_tb + 1],
                                stats_all[:, 2 * g_tb + 1:2 * g_tb + 2],
                                ALU.add, ALU.mult)
                v.drain()
                v.tensor_add(inner[:], t1[:], acc[:, tb * 1024:(tb + 1) * 1024])
                s_v.inc(v.drain())
                ep_in.append(s_v.n)

            emit_ln_stats(block, inner[:], stat, sq[:], s_v, s_a)

            @block.vector
            def _(v, g_tb=g_tb):
                v.tensor_scalar(t1[:], inner[:], stat[:, 0:1], stat[:, 3:4],
                                ALU.add, ALU.mult)
                v.drain()
                v.tensor_mul(t2[:], t1[:], gb_bc[:, 0:1024])
                v.drain()
                v.tensor_add(t2[:], t2[:], gb_bc[:, 1024:2048])
                s_v.inc(v.drain())

            emit_ln_stats(block, t2[:], stat, sq[:], s_v, s_a)

            @block.vector
            def _(v, g_tb=g_tb):
                if g_tb >= 1:
                    v.wait_ge(s_p.h, ep_done[g_tb - 1])
                v.tensor_scalar(t1[:], t2[:], stat[:, 0:1], stat[:, 3:4],
                                ALU.add, ALU.mult)
                s_v.inc(v.drain())
                ep_ln.append(s_v.n)

            @block.vector
            def _(v, g_tb=g_tb):
                sld.wait(v)
                if g_tb >= 2:
                    v.wait_ge(sst.h, st_cnt[g_tb - 2])
                ins = v.tensor_add(out_sb[g_tb % 2][:], t1[:], x3e[g_tb % 2][:])
                s_p.inc(ins)
                ep_done.append(s_p.n)

            @block.sync
            def _(sync, g_tb=g_tb):
                sync.wait_ge(s_p.h, ep_done[g_tb])
                sst.dma(sync.dma_start(out=x4[g_tb * 128:(g_tb + 1) * 128, :],
                                       in_=out_sb[g_tb % 2][:]))
                st_cnt.append(sst.n)

    @block.sync
    def _(sync):
        sst.wait(sync)



def build_fused():
    nc = bass.Bass("TRN2", target_bir_lowering=False)

    def ei(name, shape, dtype=F32):
        return nc.dram_tensor(name, shape, dtype, kind="ExternalInput").ap()

    io = {}
    io["x_seq"] = ei("x_seq", [SEQ, D])
    io["x_own"] = ei("x_own", [TCH, D])
    io["wqt"] = ei("wqt", [8, 8, 128, 128])
    io["wkt"] = ei("wkt", [8, 8, 128, 128])
    io["wvt"] = ei("wvt", [8, 128, 1024])
    io["wot"] = ei("wot", [8, 128, 1024])
    io["maskt"] = ei("maskt", [128, 79 * 128])
    io["ident"] = ei("ident", [128, 128])
    io["w1t"] = ei("w1t", [32, 128, 1024])
    io["w2t"] = ei("w2t", [32, 128, 1024])
    io["h1t"] = ei("h1t", [32, 128, 4096])
    io["h2t"] = ei("h2t", [32, 128, 4096])
    io["w3t"] = ei("w3t", [32, 128, 1024])
    io["w1s"] = ei("w1s", [128, 32])
    io["w2s"] = ei("w2s", [128, 32])
    io["h1s"] = ei("h1s", [128, 32])
    io["h2s"] = ei("h2s", [128, 32])
    io["w3s"] = ei("w3s", [1, 1024])
    io["encq"] = ei("encq", [8, 1024, 128])
    io["encv"] = ei("encv", [8, 1024, 128])
    io["dec"] = ei("dec", [8, 128, 1024])
    io["heb"] = ei("heb", [8, 128, 128])
    io["mw1"] = ei("mw1", [8, 128, 512])
    io["mw2"] = ei("mw2", [4, 128, 1024])
    io["mb1"] = ei("mb1", [1, 512])
    io["mb2"] = ei("mb2", [1, 1024])
    io["base"] = ei("base", [1, 1024])
    io["band"] = ei("band", [128, 256])
    io["wsel"] = ei("wsel", [128, 16])
    io["hsel"] = ei("hsel", [128, 8])
    io["wrt"] = ei("wrt", [8, 128, 8])
    io["br"] = ei("br", [1, 8])
    io["w1e"] = ei("w1e", [8, 32, 128, 1024])
    io["b1l"] = ei("b1l", [128, 8 * 32])
    io["w2e"] = ei("w2e", [8, 32, 128, 1024])
    io["b2i"] = ei("b2i", [8, 1024])
    io["gam"] = ei("gam", [1, 1024])
    io["bet"] = ei("bet", [1, 1024])

    io["x4"] = nc.dram_tensor("x4", [TCH, D], F32, kind="ExternalOutput").ap()

    kind_i = "ExternalOutput" if DUMP_STAGES else "Internal"
    io["x1"] = nc.dram_tensor("x1", [TCH, D], F32, kind=kind_i).ap()
    io["x2"] = nc.dram_tensor("x2", [TCH, D], F32, kind=kind_i).ap()
    io["x3"] = nc.dram_tensor("x3", [TCH, D], F32, kind=kind_i).ap()
    io["k_sc"] = nc.dram_tensor("k_sc", [8, 128, SEQ], BF16, kind="Internal").ap()
    io["v_sc"] = nc.dram_tensor("v_sc", [16, SEQ, 64], BF16, kind="Internal").ap()
    cc_in_h = nc.dram_tensor("cc_in", [128, 1040], F32, kind="Internal")
    cc_out_h = nc.dram_tensor("cc_out", [N_CORES * 128, 1040], F32, kind="Internal")
    io["cc_in"] = cc_in_h.ap()
    io["cc_out"] = cc_out_h.ap()
    io["cc_in_h"] = cc_in_h
    io["cc_out_h"] = cc_out_h
    io["halo_d"] = nc.dram_tensor("halo_d", [128, D], F32, kind=kind_i).ap()
    io["sig_d"] = nc.dram_tensor("sig_d", [128, 8], F32, kind=kind_i).ap()

    es = contextlib.ExitStack()
    with es:
        root = Ctx(nc, es)
        for name in ("sld", "swl", "sw2", "swu0", "swu1", "swu2", "swu3",
                     "swh0", "swh1", "s_t", "s_a", "s_v", "s_p", "sst",
                     "ssc", "scc"):
            root.sem(name)
        block = es.enter_context(nc.Block())
        emitters = [("a", emit_l0attn), ("f", emit_l0ffn), ("s", emit_sig),
                    ("h", emit_l1attn), ("m", emit_l1ffn)]
        for idx, (tag, fn) in enumerate(emitters):
            if idx > 0:
                barrier(block, root)
            with contextlib.ExitStack() as es_sec:
                ctx = Ctx(nc, es_sec, root.sems, tag + "_")
                fn(nc, ctx, block, io)
    return nc



def _chunk_w_ic_oc(w):
    i_n, o_n = w.shape
    return np.ascontiguousarray(
        w.reshape(i_n // 128, 128, o_n // 128, 128).transpose(0, 2, 1, 3))


def _chunk_w_oc_ic(w):
    i_n, o_n = w.shape
    return np.ascontiguousarray(
        w.reshape(i_n // 128, 128, o_n // 128, 128).transpose(2, 0, 1, 3))


def prep_l0attn_shared(inputs):
    d = {}
    d["wqt"] = _chunk_w_ic_oc(np.ascontiguousarray(inputs["L0_attn_Wq"].T))
    d["wkt"] = _chunk_w_ic_oc(np.ascontiguousarray(inputs["L0_attn_Wk"].T))
    d["wvt"] = np.ascontiguousarray(
        np.ascontiguousarray(inputs["L0_attn_Wv"].T).reshape(8, 128, 1024))
    d["wot"] = np.ascontiguousarray(
        np.ascontiguousarray(inputs["L0_attn_Wo"].T).reshape(8, 128, 1024))
    d["ident"] = np.eye(128, dtype=np.float32)
    return d


def make_l0attn_mask(p):
    r = np.arange(128)[None, :]
    c = np.arange(128)[:, None]
    tiles = []
    for mu in range(79):
        delta = 16 * p + 15 - mu
        dd = 128 * delta + r - c
        act = (dd >= 0) & ((dd <= 255) | (dd % 64 == 0))
        tiles.append(act.astype(np.float32))
    return np.ascontiguousarray(np.concatenate(tiles, axis=1))


def _pmaj(w4):
    n_oc, n_ic = w4.shape[0], w4.shape[1]
    return np.ascontiguousarray(
        w4.transpose(0, 2, 1, 3).reshape(n_oc, 128, n_ic * 128))


def prep_l0ffn_shared(inputs):
    d = {}
    d["w1t"] = _pmaj(_chunk_w_oc_ic(np.ascontiguousarray(inputs["L0_ffn_w1_w"].T)))
    d["w2t"] = _pmaj(_chunk_w_oc_ic(np.ascontiguousarray(inputs["L0_ffn_w2_w"].T)))
    d["h1t"] = _pmaj(_chunk_w_oc_ic(np.ascontiguousarray(inputs["L0_ffn_h1_w"].T)))
    d["h2t"] = _pmaj(_chunk_w_oc_ic(np.ascontiguousarray(inputs["L0_ffn_h2_w"].T)))
    d["w3t"] = np.ascontiguousarray(
        np.ascontiguousarray(inputs["L0_ffn_w3_w"].T).reshape(32, 128, 1024))
    for nm, key in [("w1s", "L0_ffn_w1_s"), ("w2s", "L0_ffn_w2_s"),
                    ("h1s", "L0_ffn_h1_s"), ("h2s", "L0_ffn_h2_s")]:
        d[nm] = np.ascontiguousarray(inputs[key].reshape(32, 128).T)
    d["w3s"] = np.ascontiguousarray(inputs["L0_ffn_w3_s"].reshape(1, 1024))
    return d


def prep_l1attn_shared(inputs):
    d = {}
    d["encq"] = np.ascontiguousarray(inputs["L1_attn_enc_q"])
    d["encv"] = np.ascontiguousarray(inputs["L1_attn_enc_v"])
    d["dec"] = np.ascontiguousarray(inputs["L1_attn_decoder"].reshape(8, 128, 1024))
    d["heb"] = np.ascontiguousarray(inputs["L1_attn_hebbian"])
    d["mw1"] = np.ascontiguousarray(inputs["L1_attn_m_w1"].reshape(8, 128, 512))
    d["mw2"] = np.ascontiguousarray(inputs["L1_attn_m_w2"].reshape(4, 128, 1024))
    d["mb1"] = np.ascontiguousarray(inputs["L1_attn_m_b1"].reshape(1, 512))
    d["mb2"] = np.ascontiguousarray(inputs["L1_attn_m_b2"].reshape(1, 1024))
    d["base"] = np.ascontiguousarray(inputs["L1_attn_base_metric"].reshape(1, 1024))
    rr = np.arange(128)[:, None]
    ss = np.arange(256)[None, :]
    act = (ss >= rr + 97) & (ss <= rr + 128)
    d["band"] = np.where(act, 0.0, 1e30).astype(np.float32)
    return d


def prep_l1ffn_shared(inputs):
    d = {}
    d["wrt"] = np.ascontiguousarray(
        np.ascontiguousarray(inputs["L1_ffn_Wr"].T).reshape(8, 128, 8))
    d["br"] = np.ascontiguousarray(inputs["L1_ffn_br"].reshape(1, 8))
    w1 = inputs["L1_ffn_W1"]
    d["w1e"] = np.stack([_pmaj(_chunk_w_oc_ic(np.ascontiguousarray(w1[e].T)))
                         for e in range(8)])
    d["b1l"] = np.ascontiguousarray(
        inputs["L1_ffn_b1"].reshape(8, 32, 128).transpose(2, 0, 1).reshape(128, 256))
    w2 = inputs["L1_ffn_W2"]
    d["w2e"] = np.stack([np.ascontiguousarray(w2[e].T).reshape(32, 128, 1024)
                         for e in range(8)])
    d["b2i"] = np.ascontiguousarray(inputs["L1_ffn_b2"])
    d["gam"] = np.ascontiguousarray(inputs["L1_ffn_gamma"].reshape(1, 1024))
    d["bet"] = np.ascontiguousarray(inputs["L1_ffn_beta"].reshape(1, 1024))
    return d


def make_wsel(core):
    b = core // 4
    w = np.zeros((16,), np.float32)
    for r in range(8):
        if r // 4 == b:
            w[r * 2 + 0] = 1.0 / SEQ
            w[r * 2 + 1] = 0.5 / 1024.0
    return np.ascontiguousarray(np.tile(w[None, :], (128, 1)))


def make_hsel(core):
    h = np.zeros((8,), np.float32)
    if core % 4 != 0:
        h[core - 1] = 1.0
    return np.ascontiguousarray(np.tile(h[None, :], (128, 1)))


def kernel(**inputs):
    x = np.ascontiguousarray(inputs["L0_attn_x"], dtype=np.float32)

    sh = {}
    sh.update(prep_l0attn_shared(inputs))
    sh.update(prep_l0ffn_shared(inputs))
    sh.update(prep_l1attn_shared(inputs))
    sh.update(prep_l1ffn_shared(inputs))

    nc = build_fused()
    in_maps = []
    for c in range(N_CORES):
        b, p = c // 4, c % 4
        m = dict(sh)
        m["x_seq"] = np.ascontiguousarray(x[b])
        m["x_own"] = np.ascontiguousarray(x[b][p * TCH:(p + 1) * TCH])
        m["maskt"] = make_l0attn_mask(p)
        m["wsel"] = make_wsel(c)
        m["hsel"] = make_hsel(c)
        in_maps.append(m)
    res = run_bass_kernel_spmd(nc, in_maps, core_ids=list(range(N_CORES)))
    x4 = np.stack([np.asarray(res.results[c]["x4"]) for c in range(N_CORES)])
    return x4.reshape(2, SEQ, D).astype(np.float32)


# version 06
# speedup vs baseline: 1.0376x; 1.0117x over previous
import contextlib
import numpy as np
import concourse.bass as bass
import concourse.mybir as mybir
from concourse.bass_utils import run_bass_kernel_spmd

F32 = mybir.dt.float32
BF16 = mybir.dt.bfloat16
AF = mybir.ActivationFunctionType
ALU = mybir.AluOpType
AX = mybir.AxisListType

N_CORES = 8
DUMP_STAGES = False
L1FFN_GROUPS = 4
L1FFN_EXPERTS = 8
B = 2
SEQ = 8192
D = 1024
TCH = 2048
NBLK = TCH // 128
LN_EPS = 1e-5



class Sem:

    def __init__(self, nc, es, name):
        self.h = es.enter_context(nc.semaphore(name))
        self.n = 0

    def inc(self, instr, k=1):
        instr.then_inc(self.h, k)
        self.n += k
        return instr

    def dma(self, instr):
        return self.inc(instr, 16)

    def wait(self, eng, n=None):
        eng.wait_ge(self.h, self.n if n is None else n)


class Ctx:
    def __init__(self, nc, es, sems=None, tag=""):
        self.nc = nc
        self.es = es
        self.sems = {} if sems is None else sems
        self.tag = tag

    def sem(self, name):
        if name not in self.sems:
            self.sems[name] = Sem(self.nc, self.es, name)
        return self.sems[name]

    def sub(self, es, subtag):
        return Ctx(self.nc, es, self.sems, self.tag + subtag)

    def sbuf(self, name, shape, dtype):
        return self.es.enter_context(
            self.nc.sbuf_tensor(self.tag + name, shape, dtype))

    def psum(self, name, shape, dtype=F32):
        return self.es.enter_context(
            self.nc.psum_tensor(self.tag + name, shape, dtype))


def fm(ap, c=8):
    return ap[:, :].rearrange("p (c t) -> p c t", c=c)


def barrier(block, ctx, skip=("scc",)):
    snap = [(s.h, s.n) for k, s in ctx.sems.items()
            if s.n > 0 and k not in skip]
    for en in ("sync", "gpsimd", "scalar", "vector", "tensor"):
        @getattr(block, en)
        def _(eng, snap=snap):
            for h, n in snap:
                eng.wait_ge(h, n)


def emit_ln_stats(block, xf, stat, sq, s_v, s_a, n_free=D, eps=LN_EPS):
    @block.vector
    def _(v):
        v.reduce_sum(stat[:, 0:1], xf, axis=AX.X, negate=True)
        v.drain()
        v.tensor_scalar_mul(stat[:, 0:1], stat[:, 0:1], 1.0 / n_free)
        s_v.inc(v.drain())

    @block.scalar
    def _(s):
        s_v.wait(s)
        s.activation(out=sq, in_=xf, func=AF.Square, bias=stat[:, 0:1], scale=1.0)
        s_a.inc(s.drain())

    @block.vector
    def _(v):
        s_a.wait(v)
        v.reduce_sum(stat[:, 1:2], sq, axis=AX.X)
        v.drain()
        v.tensor_scalar(stat[:, 1:2], stat[:, 1:2], 1.0 / n_free, eps,
                        ALU.mult, ALU.add)
        s_v.inc(v.drain())

    @block.scalar
    def _(s):
        s_v.wait(s)
        s.activation(out=stat[:, 2:3], in_=stat[:, 1:2], func=AF.Sqrt,
                     bias=0.0, scale=1.0)
        s_a.inc(s.drain())

    @block.vector
    def _(v):
        s_a.wait(v)
        v.reciprocal(stat[:, 3:4], stat[:, 2:3])
        s_v.inc(v.drain())



def emit_l0attn(nc, ctx, block, io):
    x_own = io["x_own"]; x_seq = io["x_seq"]
    wqt = io["wqt"]; wkt = io["wkt"]; wvt = io["wvt"]; wot = io["wot"]
    maskt = io["maskt"]; ident = io["ident"]
    x1 = io["x1"]; k_sc = io["k_sc"]; v_sc = io["v_sc"]

    q_all = ctx.sbuf("q_all", [128, 8 * TCH], BF16)
    maskm = ctx.sbuf("maskm", [128, 79 * 128], BF16)
    attn_f = ctx.sbuf("attn_f", [128, 8 * TCH], BF16)
    identb = ctx.sbuf("identb", [128, 128], BF16)
    bneg = ctx.sbuf("bneg", [128, 1], F32)
    onesf = ctx.sbuf("onesf", [128, 64], F32)
    stat = ctx.sbuf("stat", [128, 8], F32)
    sq = ctx.sbuf("sq", [128, D], F32)
    xf = [ctx.sbuf(f"xf{i}", [128, D], F32) for i in range(2)]
    xn_t = ctx.sbuf("xn_t", [128, D], BF16)
    t1 = ctx.sbuf("t1", [128, D], F32)
    out_sb = [ctx.sbuf(f"out_sb{i}", [128, D], F32) for i in range(2)]
    x1e = [ctx.sbuf(f"x1e{i}", [128, D], F32) for i in range(2)]

    sld = ctx.sem("sld")
    swl = ctx.sem("swl")
    svx = [ctx.sem(f"svx{i}") for i in range(2)]
    s_t = ctx.sem("s_t")
    s_a = ctx.sem("s_a")
    s_v = ctx.sem("s_v")
    s_p = ctx.sem("s_p")
    sst = ctx.sem("sst")
    ssc = ctx.sem("ssc")

    @block.gpsimd
    def _(g):
        swl.dma(g.dma_start(out=identb[:], in_=ident[:]))
        swl.dma(g.dma_start(out=maskm[:], in_=maskt[:]))
        g.memset(bneg[:], -25.0)
        g.memset(onesf[:], 1.0)
        s_p.inc(g.nop())
    consts_g = s_p.n
    consts_w = swl.n

    with contextlib.ExitStack() as esA:
        ctxA = ctx.sub(esA, "A")
        wq_sb = ctxA.sbuf("wq_sb", [128, 8 * 1024], BF16)
        wk_sb = ctxA.sbuf("wk_sb", [128, 8 * 1024], BF16)
        wv_sb = ctxA.sbuf("wv_sb", [128, 8 * 1024], BF16)
        xnf_g = ctxA.sbuf("xnf_g", [128, 8 * 512], BF16)
        krow = [ctxA.sbuf(f"krow{i}", [128, 512], BF16) for i in range(2)]
        vtok = [ctxA.sbuf(f"vtok{i}", [128, 1024], BF16) for i in range(2)]
        ps_t = ctxA.psum("ps_t", [128, 1024], BF16)
        ps_ab = [ctxA.psum(f"ps_ab{i}", [128, 512], F32) for i in range(2)]

        @block.gpsimd
        def _(g):
            for c in range(8):
                swl.dma(g.dma_start(
                    out=fm(wq_sb)[:, c, :].rearrange("p (o u) -> p o u", o=8),
                    in_=wqt[c].rearrange("o p u -> p o u")))
                swl.dma(g.dma_start(
                    out=fm(wk_sb)[:, c, :].rearrange("p (o u) -> p o u", o=8),
                    in_=wkt[c].rearrange("o p u -> p o u")))
                swl.dma(g.dma_start(out=fm(wv_sb)[:, c, :], in_=wvt[c]))
        wts_loaded = swl.n

        ev_cnt = []
        ln_cnt = []
        kv_t = []
        kv_a = []
        k_store = []
        v_store = []
        q_ev = []

        def ln_block(src_dram, row0, i_par, blkid):
            @block.sync
            def _(sync):
                if len(ln_cnt) >= 2:
                    sync.wait_ge(s_v.h, ln_cnt[-2])
                sld.dma(sync.dma_start(out=xf[blkid % 2][:],
                                       in_=src_dram[row0:row0 + 128, :]))

            @block.vector
            def _(v):
                sld.wait(v)

            emit_ln_stats(block, xf[blkid % 2][:], stat, sq[:], s_v, s_a)

            @block.vector
            def _(v):
                if ev_cnt:
                    v.wait_ge(s_v.h, ev_cnt[-1])
                v.tensor_scalar(xn_t[:], xf[blkid % 2][:], stat[:, 0:1],
                                stat[:, 3:4], ALU.add, ALU.mult)
                s_v.inc(v.drain())
                ln_cnt.append(s_v.n)

            @block.tensor
            def _(t):
                swl.wait(t, consts_w)
                s_v.wait(t)
                for c in range(8):
                    ins = t.transpose(ps_t[:, c * 128:(c + 1) * 128],
                                      xn_t[:, c * 128:(c + 1) * 128], identb[:])
                s_t.inc(ins)

            @block.vector
            def _(v):
                s_t.wait(v)
                v.tensor_copy(fm(xnf_g)[:, :, i_par * 128:(i_par + 1) * 128],
                              fm(ps_t)[:, :, :])
                s_v.inc(v.drain())
                ev_cnt.append(s_v.n)

        blkid = 0
        for g in range(16):
            for i4 in range(4):
                ln_block(x_seq, g * 512 + i4 * 128, i4, blkid)
                blkid += 1
            for oc in range(8):
                kk = len(kv_t)

                @block.tensor
                def _(t, oc=oc, kk=kk):
                    t.wait_ge(swl.h, wts_loaded)
                    s_v.wait(t)
                    if kk >= 2:
                        t.wait_ge(s_v.h, kv_a[kk - 2])
                    for c in range(8):
                        ins = t.matmul(ps_ab[kk % 2][:],
                                       wk_sb[:, c * 1024 + oc * 128:c * 1024 + oc * 128 + 128],
                                       fm(xnf_g)[:, c, :],
                                       start=(c == 0), stop=(c == 7))
                    s_t.inc(ins)
                    kv_t.append(s_t.n)

                @block.vector
                def _(v, oc=oc, kk=kk):
                    s_t.wait(v)
                    if len(k_store) >= 2:
                        v.wait_ge(ssc.h, k_store[-2])
                    v.tensor_copy(krow[kk % 2][:], ps_ab[kk % 2][:])
                    s_v.inc(v.drain())
                    kv_a.append(s_v.n)

                @block.sync
                def _(sync, g=g, oc=oc, kk=kk):
                    sync.wait_ge(s_v.h, kv_a[kk])
                    ssc.dma(sync.dma_start(out=k_sc[oc][:, g * 512:(g + 1) * 512],
                                           in_=krow[kk % 2][:]))
                    k_store.append(ssc.n)

            for tb in range(4):
                for nh in range(2):
                    kk = len(kv_t)

                    @block.tensor
                    def _(t, tb=tb, nh=nh, kk=kk):
                        if kk >= 2:
                            t.wait_ge(s_v.h, kv_a[kk - 2])
                        for c in range(8):
                            ins = t.matmul(
                                ps_ab[kk % 2][:],
                                fm(xnf_g)[:, c, tb * 128:(tb + 1) * 128],
                                wv_sb[:, c * 1024 + nh * 512:c * 1024 + nh * 512 + 512],
                                start=(c == 0), stop=(c == 7))
                        s_t.inc(ins)
                        kv_t.append(s_t.n)

                    @block.vector
                    def _(v, tb=tb, nh=nh, kk=kk):
                        s_t.wait(v)
                        if nh == 0 and len(v_store) >= 2:
                            v.wait_ge(ssc.h, v_store[-2])
                        v.tensor_copy(vtok[tb % 2][:, nh * 512:(nh + 1) * 512],
                                      ps_ab[kk % 2][:])
                        s_v.inc(v.drain())
                        kv_a.append(s_v.n)

                @block.sync
                def _(sync, g=g, tb=tb):
                    sync.wait_ge(s_v.h, kv_a[-1])
                    r0 = g * 512 + tb * 128
                    for hq in range(4):
                        ins = sync.dma_start(
                            out=v_sc[:, r0:r0 + 128, :].rearrange(
                                "h p c -> p h c")[:, 4 * hq:4 * hq + 4, :],
                            in_=vtok[tb % 2][:, :].rearrange(
                                "p (h c) -> p h c", h=16)[:, 4 * hq:4 * hq + 4, :])
                        ssc.dma(ins)
                    v_store.append(ssc.n)

        for g in range(4):
            for i4 in range(4):
                ln_block(x_own, g * 512 + i4 * 128, i4, blkid)
                blkid += 1
            for oc in range(8):
                kk = len(kv_t)

                @block.tensor
                def _(t, oc=oc, kk=kk):
                    s_v.wait(t)
                    if kk >= 2:
                        t.wait_ge(s_v.h, kv_a[kk - 2])
                    for c in range(8):
                        ins = t.matmul(ps_ab[kk % 2][:],
                                       wq_sb[:, c * 1024 + oc * 128:c * 1024 + oc * 128 + 128],
                                       fm(xnf_g)[:, c, :],
                                       start=(c == 0), stop=(c == 7))
                    s_t.inc(ins)
                    kv_t.append(s_t.n)

                @block.vector
                def _(v, g=g, oc=oc, kk=kk):
                    s_t.wait(v)
                    v.tensor_copy(fm(q_all)[:, oc, g * 512:(g + 1) * 512],
                                  ps_ab[kk % 2][:])
                    s_v.inc(v.drain())
                    kv_a.append(s_v.n)
                    q_ev.append(s_v.n)

    with contextlib.ExitStack() as esB:
        ctxB = ctx.sub(esB, "B")
        kcb = [ctxB.sbuf("kcb0", [128, SEQ], BF16)] * 2
        ptb = ctxB.sbuf("ptb", [128, 32 * 512], BF16)
        vex = [ctxB.sbuf(f"vex{i}", [128, 64 * 65], BF16) for i in range(2)]
        wo_sb = ctxB.sbuf("wo_sb", [128, 8 * 1024], BF16)
        dsb = ctxB.sbuf("dsb", [128, 1024], F32)
        araw = ctxB.sbuf("araw", [128, 512], F32)
        ps_s = [ctxB.psum(f"ps_s{i}", [128, 512], F32) for i in range(4)]
        ps_av = ctxB.psum("ps_av", [128, 512], F32)
        ps_bc = ctxB.psum("ps_bc", [128, 512], F32)
        ps_o = ctxB.psum("ps_o", [128, 1024], F32)

        @block.gpsimd
        def _(g):
            g.wait_ge(s_v.h, q_ev[-1])
            g.wait_ge(ssc.h, ssc.n)
            for c in range(8):
                swl.dma(g.dma_start(out=fm(wo_sb)[:, c, :], in_=wot[c]))
        wo_loaded = swl.n

        @block.vector
        def _(v):
            for i in range(2):
                ins = v.memset(vex[i][:, :].rearrange(
                    "p (k c) -> p k c", c=65)[:, :, 64:65], 1.0)
                v.drain()
            s_p.inc(v.nop())
        vex_ones = s_p.n

        kload = []
        vload = []
        exp_a = []
        sc_t = []
        av_t = []
        av_mid = []
        msk_v = []
        den_a = []
        af_v = []

        for kc in range(8):
            @block.sync
            def _(sync, kc=kc):
                if kc == 0:
                    sync.wait_ge(s_v.h, q_ev[-1])
                if kc >= 1:
                    sync.wait_ge(s_t.h, av_t[-1])
                sld.dma(sync.dma_start(out=kcb[kc % 2][:], in_=k_sc[kc]))
                kload.append(sld.n)

            for h2 in range(2):
                h = 2 * kc + h2
                vb = h % 2

                @block.gpsimd
                def _(g, kc=kc, h=h, vb=vb):
                    g.wait_ge(ssc.h, ssc.n)
                    g.wait_ge(s_p.h, vex_ones)
                    if h >= 2:
                        g.wait_ge(s_t.h, av_t[(h - 1) * 4 - 1])
                    for q8 in range(16):
                        svx[vb].dma(g.dma_start(
                            out=vex[vb][:, :].rearrange(
                                "p (k c) -> p k c", c=65)[:, 4 * q8:4 * q8 + 4, 0:64],
                            in_=v_sc[h].rearrange(
                                "(k p) c -> p k c", p=128)[:, 4 * q8:4 * q8 + 4, :]))
                    vload.append((svx[vb].h, svx[vb].n))

                for qg in range(4):
                    hq = len(av_t)
                    for kh in range(2):
                        for kb32 in range(32):
                            kb = 32 * kh + kb32

                            @block.tensor
                            def _(t, kc=kc, h2=h2, qg=qg, kh=kh, kb32=kb32, kb=kb, hq=hq):
                                if kb32 == 0 and kh == 0:
                                    t.wait_ge(sld.h, kload[kc])
                                if kb32 == 0 and qg == 0 and kh == 0 and kc == 0 and h2 == 0:
                                    t.wait_ge(s_v.h, q_ev[-1])
                                if len(exp_a) >= 4:
                                    t.wait_ge(s_a.h, exp_a[-4])
                                ins = t.matmul(
                                    ps_s[len(sc_t) % 4][:],
                                    kcb[kc % 2][h2 * 64:h2 * 64 + 64, kb * 128:kb * 128 + 128],
                                    q_all[h2 * 64:h2 * 64 + 64,
                                          kc * 2048 + qg * 512:kc * 2048 + qg * 512 + 512],
                                    start=True, stop=True)
                                s_t.inc(ins)
                                sc_t.append(s_t.n)

                            @block.scalar
                            def _(s, kb32=kb32, kh=kh, hq=hq):
                                if not exp_a:
                                    s.wait_ge(s_p.h, consts_g)
                                s.wait_ge(s_t.h, sc_t[-1])
                                if kb32 == 0:
                                    if kh == 1:
                                        s.wait_ge(s_v.h, msk_v[-1])
                                        s.wait_ge(s_t.h, av_mid[-1])
                                    elif av_t:
                                        s.wait_ge(s_t.h, av_t[-1])
                                ins = s.activation(
                                    out=ptb[:, kb32 * 512:(kb32 + 1) * 512],
                                    in_=ps_s[(len(sc_t) - 1) % 4][:],
                                    func=AF.Exp, bias=bneg[:, 0:1], scale=0.125)
                                s_a.inc(ins)
                                exp_a.append(s_a.n)

                        @block.vector
                        def _(v, qg=qg, kh=kh):
                            if not msk_v:
                                v.wait_ge(swl.h, consts_w)
                            s_a.wait(v)
                            for j in range(4):
                                mu0 = (15 - 4 * qg - j + 32 * kh)
                                v.tensor_mul(
                                    fm(ptb, c=32)[:, :, j * 128:(j + 1) * 128],
                                    fm(ptb, c=32)[:, :, j * 128:(j + 1) * 128],
                                    fm(maskm, c=79)[:, mu0:mu0 + 32, :])
                                v.drain()
                            s_v.inc(v.nop())
                            msk_v.append(s_v.n)

                        for kb32 in range(32):
                            kb = 32 * kh + kb32

                            @block.tensor
                            def _(t, vb=vb, kh=kh, kb32=kb32, kb=kb, h=h):
                                if kb32 == 0:
                                    t.wait_ge(s_v.h, msk_v[-1])
                                    if kh == 0:
                                        t.wait_ge(*vload[h])
                                        if den_a:
                                            t.wait_ge(s_a.h, den_a[-1])
                                ins = t.matmul(
                                    ps_av[0:65, :],
                                    vex[vb][:, kb * 65:kb * 65 + 65],
                                    ptb[:, kb32 * 512:(kb32 + 1) * 512],
                                    start=(kh == 0 and kb32 == 0),
                                    stop=(kh == 1 and kb32 == 31))
                                s_t.inc(ins)
                                if kb32 == 31 and kh == 0:
                                    av_mid.append(s_t.n)

                    av_t.append(s_t.n)

                    @block.scalar
                    def _(s, hq=hq):
                        s.wait_ge(s_t.h, av_t[hq])
                        if af_v:
                            s.wait_ge(s_v.h, af_v[-1])
                        s.activation(out=dsb[64:65, 0:512], in_=ps_av[64:65, :],
                                     func=AF.Copy, bias=0.0, scale=1.0)
                        s.drain()
                        ins = s.activation(out=araw[0:64, :], in_=ps_av[0:64, :],
                                           func=AF.Copy, bias=0.0, scale=1.0)
                        s_a.inc(ins)
                        den_a.append(s_a.n)

                    @block.vector
                    def _(v):
                        s_a.wait(v)
                        v.reciprocal(dsb[64:65, 512:1024], dsb[64:65, 0:512])
                        s_v.inc(v.drain())

                    @block.tensor
                    def _(t):
                        t.wait_ge(s_p.h, consts_g)
                        s_v.wait(t)
                        ins = t.matmul(ps_bc[0:64, :], onesf[64:65, 0:64],
                                       dsb[64:65, 512:1024], start=True, stop=True)
                        s_t.inc(ins)

                    @block.vector
                    def _(v, kc=kc, h2=h2, qg=qg):
                        s_t.wait(v)
                        v.tensor_mul(
                            attn_f[h2 * 64:h2 * 64 + 64,
                                   kc * 2048 + qg * 512:kc * 2048 + qg * 512 + 512],
                            araw[0:64, :], ps_bc[0:64, :])
                        s_v.inc(v.drain())
                        af_v.append(s_v.n)

        ep_done = []
        ep_ln = []
        st_cnt = []
        for tb in range(NBLK):
            @block.sync
            def _(sync, tb=tb):
                if tb >= 2:
                    sync.wait_ge(s_p.h, ep_done[tb - 2])
                sld.dma(sync.dma_start(out=x1e[tb % 2][:],
                                       in_=x_own[tb * 128:(tb + 1) * 128, :]))

            @block.tensor
            def _(t, tb=tb):
                if tb == 0:
                    t.wait_ge(swl.h, wo_loaded)
                    s_v.wait(t)
                if tb >= 1:
                    t.wait_ge(s_v.h, ep_ln[tb - 1])
                for c in range(8):
                    for nh in range(2):
                        ins = t.matmul(
                            ps_o[:, nh * 512:(nh + 1) * 512],
                            fm(attn_f)[:, c, tb * 128:(tb + 1) * 128],
                            wo_sb[:, c * 1024 + nh * 512:c * 1024 + nh * 512 + 512],
                            start=(c == 0), stop=(c == 7))
                s_t.inc(ins)

            @block.vector
            def _(v):
                s_t.wait(v)

            emit_ln_stats(block, ps_o[:], stat, sq[:], s_v, s_a)

            @block.vector
            def _(v, tb=tb):
                if tb >= 1:
                    v.wait_ge(s_p.h, ep_done[tb - 1])
                v.tensor_scalar(t1[:], ps_o[:], stat[:, 0:1], stat[:, 3:4],
                                ALU.add, ALU.mult)
                s_v.inc(v.drain())
                ep_ln.append(s_v.n)

            @block.vector
            def _(v, tb=tb):
                sld.wait(v)
                if tb >= 2:
                    v.wait_ge(sst.h, st_cnt[tb - 2])
                ins = v.tensor_add(out_sb[tb % 2][:], t1[:], x1e[tb % 2][:])
                s_p.inc(ins)
                ep_done.append(s_p.n)

            @block.sync
            def _(sync, tb=tb):
                sync.wait_ge(s_p.h, ep_done[tb])
                sst.dma(sync.dma_start(out=x1[tb * 128:(tb + 1) * 128, :],
                                       in_=out_sb[tb % 2][:]))
                st_cnt.append(sst.n)

        @block.sync
        def _(sync):
            sst.wait(sync)



def emit_l0ffn(nc, ctx, block, io):
    x1 = io["x1"]; x2 = io["x2"]; ident = io["ident"]
    w1t = io["w1t"]; w2t = io["w2t"]; h1t = io["h1t"]; h2t = io["h2t"]
    w3t = io["w3t"]
    w1s = io["w1s"]; w2s = io["w2s"]; h1s = io["h1s"]; h2s = io["h2s"]
    w3s = io["w3s"]

    xnf = ctx.sbuf("xnf", [128, 8 * TCH], BF16)
    big2 = ctx.sbuf("big2", [128, 32 * 1024], BF16)
    hbuf = ctx.sbuf("hbuf", [128, 32 * 512], BF16)
    w3sb = big2
    h1o = big2[:, 0:32 * 512]
    xf = [ctx.sbuf(f"xf{i}", [128, D], F32) for i in range(2)]
    xn_t = ctx.sbuf("xn_t", [128, D], BF16)
    identb = ctx.sbuf("identb", [128, 128], BF16)
    scl = ctx.sbuf("scl", [128, 4 * 32], F32)
    w3sb_bc = ctx.sbuf("w3sb_bc", [128, D], F32)
    w3s_row = ctx.sbuf("w3s_row", [1, D], BF16)
    ones1 = ctx.sbuf("ones1", [1, 128], BF16)
    stat = ctx.sbuf("stat", [128, 8], F32)
    sq = ctx.sbuf("sq", [128, D], F32)
    usb = [ctx.sbuf(f"usb{i}", [128, 512], F32) for i in range(2)]
    wsl = [ctx.sbuf(f"wsl{i}", [128, 8 * 128], BF16) for i in range(4)]
    hsl = [ctx.sbuf(f"hsl{i}", [128, 32 * 128], BF16) for i in range(2)]
    t1 = ctx.sbuf("t1", [128, D], F32)
    out_sb = [ctx.sbuf(f"out_sb{i}", [128, D], F32) for i in range(2)]
    x1e = [ctx.sbuf(f"x1e{i}", [128, D], F32) for i in range(2)]

    ps_t = ctx.psum("ps_t", [128, 1024], BF16)
    ps_u = [ctx.psum(f"ps_u{i}", [128, 512], F32) for i in range(4)]
    ps_o = ctx.psum("ps_o", [128, 1024], F32)

    sld = ctx.sem("sld")
    swl = ctx.sem("swl")
    sw2 = ctx.sem("sw2")
    swu = [ctx.sem(f"swu{i}") for i in range(4)]
    swh = [ctx.sem(f"swh{i}") for i in range(2)]
    s_t = ctx.sem("s_t")
    s_a = ctx.sem("s_a")
    s_v = ctx.sem("s_v")
    s_p = ctx.sem("s_p")
    sst = ctx.sem("sst")
    sld_base = sld.n

    @block.gpsimd
    def _(g):
        swl.dma(g.dma_start(out=identb[:], in_=ident[:]))
        swl.dma(g.dma_start(out=scl[:, 0:32], in_=w1s[:]))
        swl.dma(g.dma_start(out=scl[:, 32:64], in_=w2s[:]))
        swl.dma(g.dma_start(out=scl[:, 64:96], in_=h1s[:]))
        swl.dma(g.dma_start(out=scl[:, 96:128], in_=h2s[:]))
        swl.dma(g.dma_start(out=w3s_row[:], in_=w3s[:]))
        g.memset(ones1[:], 1.0)
        s_p.inc(g.nop())
    consts_loaded = swl.n
    ones_ready = s_p.n

    @block.tensor
    def _(t):
        swl.wait(t, consts_loaded)
        t.wait_ge(s_p.h, ones_ready)
        t.matmul(ps_o[:, 0:512], ones1[:], w3s_row[:, 0:512], start=True, stop=True)
        ins = t.matmul(ps_o[:, 512:1024], ones1[:], w3s_row[:, 512:1024],
                       start=True, stop=True)
        s_t.inc(ins)
    bc_mm = s_t.n

    @block.vector
    def _(v):
        v.wait_ge(s_t.h, bc_mm)
        v.tensor_copy(w3sb_bc[:], ps_o[:])
        s_v.inc(v.drain())
    bc_done = s_v.n

    ln_done = []
    ev_done = []
    for i in range(NBLK):
        @block.sync
        def _(sync, i=i):
            if i >= 2:
                sync.wait_ge(s_v.h, ln_done[i - 2])
            sld.dma(sync.dma_start(out=xf[i % 2][:], in_=x1[i * 128:(i + 1) * 128, :]))

        @block.vector
        def _(v, i=i):
            sld.wait(v, sld_base + 16 * (i + 1))

        emit_ln_stats(block, xf[i % 2][:], stat, sq[:], s_v, s_a)

        @block.vector
        def _(v, i=i):
            if i >= 1:
                v.wait_ge(s_v.h, ev_done[i - 1])
            v.tensor_scalar(xn_t[:], xf[i % 2][:], stat[:, 0:1], stat[:, 3:4],
                            ALU.add, ALU.mult)
            s_v.inc(v.drain())
            ln_done.append(s_v.n)

        @block.tensor
        def _(t, i=i):
            s_v.wait(t)
            for c in range(8):
                ins = t.transpose(ps_t[:, c * 128:(c + 1) * 128],
                                  xn_t[:, c * 128:(c + 1) * 128], identb[:])
            s_t.inc(ins)

        @block.vector
        def _(v, i=i):
            s_t.wait(v)
            v.tensor_copy(fm(xnf)[:, :, i * 128:(i + 1) * 128],
                          fm(ps_t)[:, :, :])
            s_v.inc(v.drain())
            ev_done.append(s_v.n)

    up_t = []
    up_a = []
    up_p = []
    hp_t = []
    hp_a = []
    ep_done = []
    ep_ln = []
    st_cnt = []
    for grp in range(4):
        tok0 = grp * 512
        for oc in range(32):
            for kind in range(2):
                k = len(up_t)
                slot = 2 * kind + oc % 2
                wslb = wsl[slot]
                wt = w1t if kind == 0 else w2t

                @block.gpsimd
                def _(g, oc=oc, k=k, wslb=wslb, wt=wt, slot=slot):
                    if k >= 4:
                        g.wait_ge(s_t.h, up_t[k - 4])
                    swu[slot].dma(g.dma_start(out=wslb[:, :], in_=wt[oc]))
                ld_sem, ld_cnt = swu[slot].h, swu[slot].n

                @block.tensor
                def _(t, k=k, wslb=wslb, tok0=tok0, ld_sem=ld_sem, ld_cnt=ld_cnt):
                    t.wait_ge(ld_sem, ld_cnt)
                    if k == 0:
                        t.wait_ge(s_v.h, ev_done[-1])
                    if k >= 4:
                        t.wait_ge(s_a.h, up_a[k - 4])
                    for c in range(8):
                        ins = t.matmul(
                            ps_u[k % 4][:],
                            wslb[:, c * 128:(c + 1) * 128],
                            fm(xnf)[:, c, tok0:tok0 + 512],
                            start=(c == 0), stop=(c == 7))
                    s_t.inc(ins)
                    up_t.append(s_t.n)

                @block.scalar
                def _(s, oc=oc, kind=kind, k=k):
                    s_t.wait(s)
                    if len(up_p) >= 1:
                        s.wait_ge(s_p.h, up_p[-1])
                    if kind == 0:
                        ins = s.activation(out=usb[0][:], in_=ps_u[k % 4][:],
                                           func=AF.Silu, bias=0.0,
                                           scale=scl[:, oc:oc + 1])
                    else:
                        ins = s.activation(out=usb[1][:], in_=ps_u[k % 4][:],
                                           func=AF.Copy, bias=0.0,
                                           scale=scl[:, 32 + oc:33 + oc])
                    s_a.inc(ins)
                    up_a.append(s_a.n)

            @block.vector
            def _(v, oc=oc):
                v.wait_ge(s_a.h, up_a[-1])
                ins = v.tensor_mul(hbuf[:, oc * 512:(oc + 1) * 512],
                                   usb[0][:], usb[1][:])
                s_p.inc(ins)
                up_p.append(s_p.n)

        for phase, (srcb, dst, wt, soff) in enumerate(
                [(hbuf[:], h1o, h1t, 64), (h1o, hbuf[:], h2t, 96)]):
            for oc in range(32):
                hk = len(hp_t)

                @block.gpsimd
                def _(g, oc=oc, wt=wt, hk=hk):
                    if hk >= 2:
                        g.wait_ge(s_t.h, hp_t[hk - 2])
                    swh[oc % 2].dma(g.dma_start(out=hsl[oc % 2][:, :], in_=wt[oc]))
                ld_sem, ld_cnt = swh[oc % 2].h, swh[oc % 2].n

                @block.tensor
                def _(t, oc=oc, phase=phase, srcb=srcb, hk=hk,
                      ld_sem=ld_sem, ld_cnt=ld_cnt):
                    t.wait_ge(ld_sem, ld_cnt)
                    if oc == 0:
                        if phase == 0:
                            s_p.wait(t)
                        else:
                            s_a.wait(t)
                    if hk >= 4:
                        t.wait_ge(s_a.h, hp_a[hk - 4])
                    for c in range(32):
                        ins = t.matmul(
                            ps_u[hk % 4][:],
                            hsl[oc % 2][:, c * 128:(c + 1) * 128],
                            srcb[:, c * 512:(c + 1) * 512],
                            start=(c == 0), stop=(c == 31))
                    s_t.inc(ins)
                    hp_t.append(s_t.n)

                @block.scalar
                def _(s, oc=oc, dst=dst, soff=soff, hk=hk):
                    s_t.wait(s)
                    ins = s.activation(
                        out=dst[:, oc * 512:(oc + 1) * 512],
                        in_=ps_u[hk % 4][:], func=AF.Gelu,
                        bias=0.0, scale=scl[:, soff + oc:soff + oc + 1])
                    s_a.inc(ins)
                    hp_a.append(s_a.n)

        @block.gpsimd
        def _(g):
            s_a.wait(g)
            for c0 in range(0, 32, 8):
                sw2.dma(g.dma_start(
                    out=w3sb[:, c0 * 1024:(c0 + 8) * 1024].rearrange(
                        "p (c o) -> p c o", c=8),
                    in_=w3t[c0:c0 + 8].rearrange("c p o -> p c o")))
        w3_loaded = sw2.n

        @block.vector
        def _(v):
            sw2.wait(v, w3_loaded)
            v.wait_ge(s_v.h, bc_done)
            for c in range(32):
                v.tensor_mul(w3sb[:, c * 1024:(c + 1) * 1024],
                             w3sb[:, c * 1024:(c + 1) * 1024], w3sb_bc[:])
                v.drain()
            s_v.inc(v.nop())
        w3_folded = s_v.n

        for tb in range(4):
            g_tb = grp * 4 + tb
            r0 = grp * 512 + tb * 128

            @block.sync
            def _(sync, r0=r0, g_tb=g_tb):
                if g_tb >= 2:
                    sync.wait_ge(s_p.h, ep_done[g_tb - 2])
                sld.dma(sync.dma_start(out=x1e[g_tb % 2][:],
                                       in_=x1[r0:r0 + 128, :]))

            @block.tensor
            def _(t, tb=tb, g_tb=g_tb, w3_folded=w3_folded):
                t.wait_ge(s_v.h, w3_folded)
                if tb == 0:
                    s_a.wait(t)
                if g_tb >= 1:
                    t.wait_ge(s_v.h, ep_ln[g_tb - 1])
                for c in range(32):
                    for nh in range(2):
                        ins = t.matmul(
                            ps_o[:, nh * 512:(nh + 1) * 512],
                            hbuf[:, c * 512 + tb * 128:c * 512 + tb * 128 + 128],
                            w3sb[:, c * 1024 + nh * 512:c * 1024 + nh * 512 + 512],
                            start=(c == 0), stop=(c == 31))
                s_t.inc(ins)

            @block.vector
            def _(v):
                s_t.wait(v)

            emit_ln_stats(block, ps_o[:], stat, sq[:], s_v, s_a)

            @block.vector
            def _(v, g_tb=g_tb):
                if g_tb >= 1:
                    v.wait_ge(s_p.h, ep_done[g_tb - 1])
                v.tensor_scalar(t1[:], ps_o[:], stat[:, 0:1], stat[:, 3:4],
                                ALU.add, ALU.mult)
                s_v.inc(v.drain())
                ep_ln.append(s_v.n)

            @block.vector
            def _(v, g_tb=g_tb):
                sld.wait(v)
                if g_tb >= 2:
                    v.wait_ge(sst.h, st_cnt[g_tb - 2])
                ins = v.tensor_add(out_sb[g_tb % 2][:], t1[:], x1e[g_tb % 2][:])
                s_p.inc(ins)
                ep_done.append(s_p.n)

            @block.sync
            def _(sync, r0=r0, g_tb=g_tb):
                sync.wait_ge(s_p.h, ep_done[g_tb])
                sst.dma(sync.dma_start(out=x2[r0:r0 + 128, :],
                                       in_=out_sb[g_tb % 2][:]))
                st_cnt.append(sst.n)

    @block.sync
    def _(sync):
        sst.wait(sync)



def emit_sig(nc, ctx, block, io):
    x2 = io["x2"]; encq = io["encq"]; ident = io["ident"]
    cc_in = io["cc_in"]; cc_out = io["cc_out"]
    halo_d = io["halo_d"]; sig_d = io["sig_d"]
    wsel = io["wsel"]; hsel = io["hsel"]

    xnf = ctx.sbuf("xnf", [128, 8 * TCH], BF16)
    eq_sb = ctx.sbuf("eq_sb", [128, 8 * 1024], BF16)
    qbuf = ctx.sbuf("qbuf", [128, TCH], BF16)
    xf = [ctx.sbuf(f"xf{i}", [128, D], F32) for i in range(2)]
    xn_t = ctx.sbuf("xn_t", [128, D], BF16)
    identb = ctx.sbuf("identb", [128, 128], BF16)
    stat = ctx.sbuf("stat", [128, 8], F32)
    sq = ctx.sbuf("sq", [128, D], F32)
    parts_sb = ctx.sbuf("parts_sb", [128, 16], F32)
    pg = ctx.sbuf("pg", [128, 8 * 16], F32)
    hg = ctx.sbuf("hg", [128, 8 * 1024], F32)
    halo_sb = ctx.sbuf("halo_sb", [128, D], F32)
    sig_col = ctx.sbuf("sig_col", [128, 8], F32)
    wsel_sb = ctx.sbuf("wsel_sb", [128, 16], F32)
    hsel_sb = ctx.sbuf("hsel_sb", [128, 8], F32)
    ps_t = ctx.psum("ps_t", [128, 1024], BF16)
    ps_a = [ctx.psum(f"ps_a{i}", [128, 512], F32) for i in range(2)]

    sld = ctx.sem("sld")
    swl = ctx.sem("swl")
    s_t = ctx.sem("s_t")
    s_a = ctx.sem("s_a")
    s_v = ctx.sem("s_v")
    ssc = ctx.sem("ssc")
    scc = ctx.sem("scc")
    swl_base = swl.n

    @block.gpsimd
    def _(g):
        swl.dma(g.dma_start(out=identb[:], in_=ident[:]))
        swl.dma(g.dma_start(out=wsel_sb[:], in_=wsel[:]))
        swl.dma(g.dma_start(out=hsel_sb[:], in_=hsel[:]))
        for n in range(8):
            swl.dma(g.dma_start(
                out=eq_sb[:, :].rearrange("p (c u) -> p c u", c=8)[:, :, n * 128:(n + 1) * 128],
                in_=encq[n].rearrange("(c p) u -> p c u", p=128)))
    consts_w = swl.n

    ln_done = []
    ev_done = []
    for i in range(NBLK):
        @block.sync
        def _(sync, i=i):
            if i >= 2:
                sync.wait_ge(s_v.h, ln_done[i - 2])
            sld.dma(sync.dma_start(out=xf[i % 2][:], in_=x2[i * 128:(i + 1) * 128, :]))

        @block.vector
        def _(v, i=i):
            sld.wait(v)

        emit_ln_stats(block, xf[i % 2][:], stat, sq[:], s_v, s_a)

        @block.vector
        def _(v, i=i):
            if i >= 1:
                v.wait_ge(s_v.h, ev_done[i - 1])
            v.tensor_scalar(xn_t[:], xf[i % 2][:], stat[:, 0:1], stat[:, 3:4],
                            ALU.add, ALU.mult)
            s_v.inc(v.drain())
            ln_done.append(s_v.n)

        @block.tensor
        def _(t, i=i):
            swl.wait(t, swl_base + 16)
            s_v.wait(t)
            for c in range(8):
                ins = t.transpose(ps_t[:, c * 128:(c + 1) * 128],
                                  xn_t[:, c * 128:(c + 1) * 128], identb[:])
            s_t.inc(ins)

        @block.vector
        def _(v, i=i):
            s_t.wait(v)
            v.tensor_copy(fm(xnf)[:, :, i * 128:(i + 1) * 128], fm(ps_t)[:, :, :])
            s_v.inc(v.drain())
            ev_done.append(s_v.n)

    @block.sync
    def _(sync):
        sync.wait_ge(s_v.h, ln_done[15])
        ssc.dma(sync.dma_start(out=cc_in[:, 16:1040], in_=xf[1][:]))

    qp_t = []
    qp_a = []
    qr_done = []
    for n in range(8):
        for g in range(4):
            kk = len(qp_t)

            @block.tensor
            def _(t, n=n, g=g, kk=kk):
                swl.wait(t, consts_w)
                if kk == 0:
                    t.wait_ge(s_v.h, ev_done[-1])
                if kk >= 2:
                    t.wait_ge(s_a.h, qp_a[kk - 2])
                for c in range(8):
                    ins = t.matmul(ps_a[kk % 2][:],
                                   eq_sb[:, c * 1024 + n * 128:c * 1024 + (n + 1) * 128],
                                   fm(xnf)[:, c, g * 512:(g + 1) * 512],
                                   start=(c == 0), stop=(c == 7))
                s_t.inc(ins)
                qp_t.append(s_t.n)

            @block.scalar
            def _(s, n=n, g=g, kk=kk):
                s_t.wait(s)
                if g == 0 and qr_done:
                    s.wait_ge(s_v.h, qr_done[-1])
                ins = s.activation(out=qbuf[:, g * 512:(g + 1) * 512],
                                   in_=ps_a[kk % 2][:], func=AF.Relu,
                                   bias=0.0, scale=1.0)
                s_a.inc(ins)
                qp_a.append(s_a.n)

        @block.vector
        def _(v, n=n):
            s_a.wait(v)
            v.reduce_sum(parts_sb[:, n:n + 1], qbuf[:, :], axis=AX.X)
            v.drain()
            v.reduce_sum(parts_sb[:, 8 + n:9 + n],
                         qbuf[:, :].rearrange("p (a b) -> p b a", b=8)[:, 0:1, :],
                         axis=AX.X)
            s_v.inc(v.drain())
            qr_done.append(s_v.n)

    @block.sync
    def _(sync):
        sync.wait_ge(s_v.h, qr_done[-1])
        ssc.dma(sync.dma_start(out=cc_in[:, 0:16], in_=parts_sb[:]))

    @block.gpsimd
    def _(g):
        g.wait_ge(ssc.h, ssc.n)
        ins = g.collective_compute(
            "AllGather", ALU.bypass,
            replica_groups=[list(range(N_CORES))],
            ins=[io["cc_in_h"].ap().opt()],
            outs=[io["cc_out_h"].ap().opt()])
        scc.inc(ins, 1)

    @block.gpsimd
    def _(g):
        g.wait_ge(scc.h, 1)
        swl.dma(g.dma_start(
            out=pg[:, :].rearrange("p (r j) -> p r j", r=8),
            in_=cc_out.rearrange("(r p) c -> p r c", p=128)[:, :, 0:16]))
        swl.dma(g.dma_start(
            out=hg[:, :].rearrange("p (r j) -> p r j", r=8),
            in_=cc_out.rearrange("(r p) c -> p r c", p=128)[:, :, 16:1040]))
    cc_loaded = swl.n

    @block.vector
    def _(v):
        v.wait_ge(swl.h, cc_loaded)
        v.memset(halo_sb[:], 0.0)
        v.drain()
        for r in range(8):
            v.scalar_tensor_tensor(
                halo_sb[:], hg[:, r * 1024:(r + 1) * 1024],
                hsel_sb[:, r:r + 1], halo_sb[:],
                op0=ALU.mult, op1=ALU.add)
            v.drain()
        v.memset(sig_col[:], 0.0)
        v.drain()
        for r in range(8):
            for k in range(2):
                j0 = r * 16 + k * 8
                v.scalar_tensor_tensor(
                    sig_col[:], pg[:, j0:j0 + 8],
                    wsel_sb[:, r * 2 + k:r * 2 + k + 1], sig_col[:],
                    op0=ALU.mult, op1=ALU.add)
                v.drain()
        s_v.inc(v.nop())
    comb_done = s_v.n

    @block.sync
    def _(sync):
        sync.wait_ge(s_v.h, comb_done)
        ssc.dma(sync.dma_start(out=halo_d[:], in_=halo_sb[:]))
        ssc.dma(sync.dma_start(out=sig_d[:], in_=sig_col[:]))
        ssc.wait(sync)



def emit_l1attn(nc, ctx, block, io):
    x2 = io["x2"]; x3 = io["x3"]
    encq = io["encq"]; encv = io["encv"]; dec = io["dec"]; heb = io["heb"]
    mw1 = io["mw1"]; mw2 = io["mw2"]; mb1 = io["mb1"]; mb2 = io["mb2"]
    base = io["base"]; band = io["band"]; ident = io["ident"]
    halo_d = io["halo_d"]; sig_d = io["sig_d"]

    TH = TCH + 128
    xnf = ctx.sbuf("xnf", [128, 8 * TH], BF16)
    zf = ctx.sbuf("zf", [128, 8 * TCH], BF16)
    onesf = ctx.sbuf("onesf", [128, 128], F32)
    onesc = ctx.sbuf("onesc", [128, 1], F32)
    identb = ctx.sbuf("identb", [128, 128], BF16)
    stat = ctx.sbuf("stat", [128, 8], F32)
    sq = ctx.sbuf("sq", [128, D], F32)
    xf = [ctx.sbuf(f"xf{i}", [128, D], F32) for i in range(2)]
    xn_t = ctx.sbuf("xn_t", [128, D], BF16)
    t1 = ctx.sbuf("t1", [128, D], F32)
    out_sb = [ctx.sbuf(f"out_sb{i}", [128, D], F32) for i in range(2)]
    x2e = [ctx.sbuf(f"x2e{i}", [128, D], F32) for i in range(2)]
    es_S = contextlib.ExitStack()
    ctxS = ctx.sub(es_S, "S")
    eq_sb = ctxS.sbuf("eq_sb", [128, 8 * 1024], BF16)
    ev_sb = ctxS.sbuf("ev_sb", [128, 8 * 1024], BF16)
    heb_sb = ctxS.sbuf("heb_sb", [128, 8 * 128], BF16)
    qb_buf = ctxS.sbuf("qb_buf", [128, TH], BF16)
    qt_buf = ctxS.sbuf("qt_buf", [128, TH], BF16)
    v_buf = ctxS.sbuf("v_buf", [128, TH], BF16)
    sq512 = ctxS.sbuf("sq512", [128, 512], F32)
    pt_sb = ctxS.sbuf("pt_sb", [128, 256], BF16)
    band_sb = ctxS.sbuf("band_sb", [128, 256], F32)
    rows_t = ctxS.sbuf("rows_t", [128, 6144], F32)
    sig_col = ctxS.sbuf("sig_col", [128, 16], F32)
    sig_colb = ctxS.sbuf("sig_colb", [128, 16], BF16)
    sqm_pp = ctxS.sbuf("sqm_pp", [128, 8], F32)
    wtp = ctxS.sbuf("wtp", [128, 16], F32)
    denb = ctxS.sbuf("denb", [128, 16], F32)
    rden = ctxS.sbuf("rden", [128, 16], F32)
    hb_sb = ctxS.sbuf("hb_sb", [128, 128], F32)
    t2_sb = ctxS.sbuf("t2_sb", [128, 128], F32)
    t3_sb = ctxS.sbuf("t3_sb", [128, 128], F32)

    ps_t = ctx.psum("ps_t", [128, 1024], BF16)
    ps_a = ctx.psum("ps_a", [128, 512], F32)
    ps_b = ctx.psum("ps_b", [128, 512], F32)
    ps_tp = ctx.psum("ps_tp", [128, 256], BF16)
    ps_y = ctx.psum("ps_y", [128, 128], F32)
    ps_ctx = ctx.psum("ps_ctx", [128, 128], F32)
    ps_o = ctx.psum("ps_o", [128, 1024], F32)

    sld = ctx.sem("sld")
    swl = ctx.sem("swl")
    s_t = ctx.sem("s_t")
    s_a = ctx.sem("s_a")
    s_v = ctx.sem("s_v")
    s_p = ctx.sem("s_p")
    sst = ctx.sem("sst")
    scc = ctx.sem("scc")
    swl_base = swl.n

    es_mw = contextlib.ExitStack()
    ctxM = ctx.sub(es_mw, "M")
    mw1_sb = ctxM.sbuf("mw1_sb", [128, 8 * 512], BF16)
    mw2_sb = ctxM.sbuf("mw2_sb", [128, 4 * 1024], BF16)

    @block.gpsimd
    def _(g):
        swl.dma(g.dma_start(out=identb[:], in_=ident[:]))
        swl.dma(g.dma_start(out=band_sb[:], in_=band[:]))
        swl.dma(g.dma_start(out=rows_t[64:65, 2560:3072], in_=mb1[:]))
        swl.dma(g.dma_start(out=rows_t[64:65, 3072:4096], in_=mb2[:]))
        swl.dma(g.dma_start(out=rows_t[64:65, 4096:5120], in_=base[:]))
        swl.dma(g.dma_start(out=sig_col[:, 0:8], in_=sig_d[:]))
        swl.dma(g.dma_start(out=sig_colb[:, 0:8], in_=sig_d[:]))
        for n in range(8):
            swl.dma(g.dma_start(
                out=eq_sb[:, :].rearrange("p (c u) -> p c u", c=8)[:, :, n * 128:(n + 1) * 128],
                in_=encq[n].rearrange("(c p) u -> p c u", p=128)))
            swl.dma(g.dma_start(
                out=ev_sb[:, :].rearrange("p (c u) -> p c u", c=8)[:, :, n * 128:(n + 1) * 128],
                in_=encv[n].rearrange("(c p) u -> p c u", p=128)))
            swl.dma(g.dma_start(out=heb_sb[:, n * 128:(n + 1) * 128], in_=heb[n]))
            swl.dma(g.dma_start(out=fm(mw1_sb)[:, n, :], in_=mw1[n]))
        for c in range(4):
            swl.dma(g.dma_start(out=fm(mw2_sb, c=4)[:, c, :], in_=mw2[c]))
    consts_w = swl.n

    @block.vector
    def _(v):
        v.memset(onesf[:], 1.0)
        v.drain()
        v.memset(onesc[:], 1.0)
        s_v.inc(v.drain())

    ln_done = []
    ev_done = []
    for i in range(17):
        @block.sync
        def _(sync, i=i):
            if i >= 2:
                sync.wait_ge(s_v.h, ln_done[i - 2])
            if i == 0:
                sld.dma(sync.dma_start(out=xf[0][:], in_=halo_d[:]))
            else:
                sld.dma(sync.dma_start(out=xf[i % 2][:],
                                       in_=x2[(i - 1) * 128:i * 128, :]))

        @block.vector
        def _(v):
            sld.wait(v)

        emit_ln_stats(block, xf[i % 2][:], stat, sq[:], s_v, s_a)

        @block.vector
        def _(v, i=i):
            if i >= 1:
                v.wait_ge(s_v.h, ev_done[i - 1])
            v.tensor_scalar(xn_t[:], xf[i % 2][:], stat[:, 0:1], stat[:, 3:4],
                            ALU.add, ALU.mult)
            s_v.inc(v.drain())
            ln_done.append(s_v.n)

        @block.tensor
        def _(t, i=i):
            swl.wait(t, swl_base + 16)
            s_v.wait(t)
            for c in range(8):
                ins = t.transpose(ps_t[:, c * 128:(c + 1) * 128],
                                  xn_t[:, c * 128:(c + 1) * 128], identb[:])
            s_t.inc(ins)

        @block.vector
        def _(v, i=i):
            s_t.wait(v)
            v.tensor_copy(fm(xnf)[:, :, i * 128:(i + 1) * 128], fm(ps_t)[:, :, :])
            s_v.inc(v.drain())
            ev_done.append(s_v.n)

    @block.tensor
    def _(t):
        swl.wait(t, consts_w)
        for c in range(8):
            ins = t.matmul(ps_b[0:1, :], sig_colb[:, c:c + 1],
                           mw1_sb[:, c * 512:(c + 1) * 512],
                           start=(c == 0), stop=(c == 7))
        s_t.inc(ins)

    @block.vector
    def _(v):
        s_t.wait(v)
        v.tensor_add(rows_t[64:65, 1024:1536], ps_b[0:1, :], rows_t[64:65, 2560:3072])
        s_v.inc(v.drain())

    @block.scalar
    def _(s):
        s_v.wait(s)
        s.activation(out=rows_t[64:65, 1024:1536], in_=rows_t[64:65, 1024:1536],
                     func=AF.Gelu, bias=0.0, scale=1.0)
        s_a.inc(s.drain())

    @block.tensor
    def _(t):
        s_a.wait(t)
        for c in range(4):
            ins = t.transpose(ps_y[:, 8 + c:9 + c],
                              rows_t[64:65, 1024 + c * 128:1024 + (c + 1) * 128],
                              onesf[64:65, 64:65])
        s_t.inc(ins)

    @block.vector
    def _(v):
        s_t.wait(v)
        v.tensor_copy(sig_col[:, 8:12], ps_y[:, 8:12])
        v.drain()
        v.tensor_copy(sig_colb[:, 8:12], sig_col[:, 8:12])
        s_v.inc(v.drain())

    @block.tensor
    def _(t):
        s_v.wait(t)
        for nh in range(2):
            for c in range(4):
                ins = t.matmul(ps_o[0:1, nh * 512:(nh + 1) * 512],
                               sig_colb[:, 8 + c:9 + c],
                               mw2_sb[:, c * 1024 + nh * 512:c * 1024 + nh * 512 + 512],
                               start=(c == 0), stop=(c == 3))
        s_t.inc(ins)

    @block.vector
    def _(v):
        s_t.wait(v)
        v.tensor_add(rows_t[64:65, 1536:2560], ps_o[0:1, :], rows_t[64:65, 3072:4096])
        v.drain()
        v.scalar_tensor_tensor(rows_t[64:65, 1536:2560], rows_t[64:65, 1536:2560],
                               0.1, rows_t[64:65, 4096:5120], op0=ALU.mult, op1=ALU.add)
        s_v.inc(v.drain())

    @block.scalar
    def _(s):
        s_v.wait(s)
        s.activation(out=rows_t[64:65, 5120:6144], in_=rows_t[64:65, 1536:2560],
                     func=AF.Abs, bias=0.0, scale=1.0)
        s.drain()
        s.activation(out=rows_t[64:65, 5120:6144], in_=rows_t[64:65, 5120:6144],
                     func=AF.Exp, bias=0.0, scale=-1.0)
        s.drain()
        s.activation(out=rows_t[64:65, 5120:6144], in_=rows_t[64:65, 5120:6144],
                     func=AF.Ln, bias=onesf[64:65, 64:65], scale=1.0)
        s_a.inc(s.drain())

    @block.vector
    def _(v):
        s_a.wait(v)
        v.tensor_scalar_max(rows_t[64:65, 1536:2560], rows_t[64:65, 1536:2560], 0.0)
        v.drain()
        v.tensor_add(rows_t[64:65, 1536:2560], rows_t[64:65, 1536:2560],
                     rows_t[64:65, 5120:6144])
        v.drain()
        v.tensor_scalar_add(rows_t[64:65, 1536:2560], rows_t[64:65, 1536:2560], 1e-6)
        s_v.inc(v.drain())

    @block.scalar
    def _(s):
        s_v.wait(s)
        s.activation(out=rows_t[64:65, 1536:2560], in_=rows_t[64:65, 1536:2560],
                     func=AF.Sqrt, bias=0.0, scale=1.0)
        s_a.inc(s.drain())

    @block.tensor
    def _(t):
        s_a.wait(t)
        for c in range(8):
            ins = t.transpose(ps_y[:, c:c + 1],
                              rows_t[64:65, 1536 + c * 128:1536 + (c + 1) * 128],
                              onesf[64:65, 64:65])
        s_t.inc(ins)

    @block.vector
    def _(v):
        s_t.wait(v)
        v.tensor_copy(sqm_pp[:, 0:8], ps_y[:, 0:8])
        s_v.inc(v.drain())
    sqm_done = s_v.n

    es_mw.close()
    dpre = ctxS.sbuf("dpre", [128, 16 * 256], F32)
    pbuf = ctxS.sbuf("pbuf", [128, 16 * 256], BF16)
    gbounds = [(0, 512), (512, 512), (1024, 512), (1536, 512), (2048, 128)]
    nq_t = []
    nq_a = []
    n_zf = []

    def ones_row(n, src_sbuf, dst_p):
        for gi, (t0_, tw) in enumerate(gbounds):
            kk = len(nq_t)

            @block.scalar
            def _(s, t0_=t0_, tw=tw, kk=kk):
                if nq_t:
                    s.wait_ge(s_t.h, nq_t[-1])
                ins = s.activation(out=sq512[:, 0:tw], in_=src_sbuf[:, t0_:t0_ + tw],
                                   func=AF.Square, bias=0.0, scale=1.0)
                s_a.inc(ins)
                sq_a = s_a.n

                @block.tensor
                def _(t, kk=kk, tw=tw, sq_a=sq_a):
                    t.wait_ge(s_a.h, sq_a)
                    if kk >= 2:
                        t.wait_ge(s_a.h, nq_a[kk - 2])
                    psx = [ps_a, ps_b][kk % 2]
                    ins = t.matmul(psx[0:1, 0:tw], onesc[:], sq512[:, 0:tw],
                                   start=True, stop=True)
                    s_t.inc(ins)
                    nq_t.append(s_t.n)

            @block.scalar
            def _(s, t0_=t0_, tw=tw, kk=kk, dst_p=dst_p):
                s.wait_ge(s_t.h, nq_t[kk])
                psx = [ps_a, ps_b][kk % 2]
                ins = s.activation(out=rows_t[dst_p:dst_p + 1, t0_:t0_ + tw],
                                   in_=psx[0:1, 0:tw], func=AF.Copy,
                                   bias=0.0, scale=1.0)
                s_a.inc(ins)
                nq_a.append(s_a.n)

    for n in range(8):
        for gi, (t0_, tw) in enumerate(gbounds):
            kk = len(nq_t)

            @block.tensor
            def _(t, n=n, t0_=t0_, tw=tw, kk=kk):
                if kk == 0:
                    t.wait_ge(s_v.h, ev_done[-1])
                    swl.wait(t, consts_w)
                if kk >= 2:
                    t.wait_ge(s_a.h, nq_a[kk - 2])
                psx = [ps_a, ps_b][kk % 2]
                for c in range(8):
                    ins = t.matmul(psx[:, 0:tw],
                                   eq_sb[:, c * 1024 + n * 128:c * 1024 + (n + 1) * 128],
                                   fm(xnf)[:, c, t0_:t0_ + tw],
                                   start=(c == 0), stop=(c == 7))
                s_t.inc(ins)
                nq_t.append(s_t.n)

            @block.scalar
            def _(s, n=n, gi=gi, t0_=t0_, tw=tw, kk=kk):
                s.wait_ge(s_t.h, nq_t[kk])
                if gi == 0 and n_zf:
                    s.wait_ge(s_v.h, n_zf[-1])
                psx = [ps_a, ps_b][kk % 2]
                ins = s.activation(out=qb_buf[:, t0_:t0_ + tw], in_=psx[:, 0:tw],
                                   func=AF.Relu, bias=0.0, scale=1.0)
                s_a.inc(ins)
                nq_a.append(s_a.n)

        for tb in range(17):
            kk = len(nq_t)

            @block.tensor
            def _(t, n=n, tb=tb, kk=kk):
                if kk >= 2:
                    t.wait_ge(s_a.h, nq_a[kk - 2])
                psx = [ps_a, ps_b][kk % 2]
                for c in range(8):
                    ins = t.matmul(psx[:, 0:128],
                                   fm(xnf)[:, c, tb * 128:(tb + 1) * 128],
                                   ev_sb[:, c * 1024 + n * 128:c * 1024 + (n + 1) * 128],
                                   start=(c == 0), stop=(c == 7))
                s_t.inc(ins)
                nq_t.append(s_t.n)

            @block.scalar
            def _(s, n=n, tb=tb, kk=kk):
                s.wait_ge(s_t.h, nq_t[kk])
                psx = [ps_a, ps_b][kk % 2]
                ins = s.activation(out=v_buf[:, tb * 128:(tb + 1) * 128],
                                   in_=psx[:, 0:128],
                                   func=AF.Relu, bias=0.0, scale=1.0)
                s_a.inc(ins)
                nq_a.append(s_a.n)

        @block.vector
        def _(v, n=n):
            s_a.wait(v)
            v.wait_ge(s_v.h, sqm_done)
            v.tensor_scalar_mul(qt_buf[:], qb_buf[:], sqm_pp[:, n:n + 1])
            s_v.inc(v.drain())

        @block.scalar
        def _(s, n=n):
            s_v.wait(s)
        ones_row(n, qt_buf[:, :], 0)
        ones_row(n, qb_buf[:, :], 32)

        @block.scalar
        def _(s):
            s.wait_ge(s_a.h, nq_a[-1])
            s.activation(out=rows_t[32:33, 0:TH],
                         in_=rows_t[32:33, 0:TH], func=AF.Sqrt,
                         bias=0.0, scale=1.0)
            s_a.inc(s.drain())

        @block.vector
        def _(v):
            s_a.wait(v)
            v.tensor_scalar_max(rows_t[32:33, 0:TH],
                                rows_t[32:33, 0:TH], 1e-12)
            v.drain()
            v.reciprocal(rows_t[32:33, 0:TH], rows_t[32:33, 0:TH])
            s_v.inc(v.drain())
        rnorm_done = s_v.n

        @block.tensor
        def _(t, n=n):
            s_v.wait(t)
            for qb in range(16):
                ins = t.transpose(ps_y[:, qb:qb + 1],
                                  rows_t[0:1, (qb + 1) * 128:(qb + 2) * 128],
                                  onesf[0:1, 0:1])
            s_t.inc(ins)

        @block.vector
        def _(v):
            s_t.wait(v)
            v.tensor_scalar_add(wtp[:, 0:16], ps_y[:, 0:16], 1e-8)
            s_v.inc(v.drain())
        wtp_done = s_v.n

        al_t = []
        al_v = []
        for qb in range(16):
            @block.tensor
            def _(t, n=n, qb=qb):
                t.wait_ge(s_v.h, wtp_done)
                if qb >= 2:
                    t.wait_ge(s_v.h, al_v[qb - 2])
                psw = [ps_a, ps_b][qb % 2]
                t.matmul(psw[0:128, 0:256], onesf[0:1, 0:128],
                         rows_t[0:1, qb * 128:qb * 128 + 256],
                         start=True, stop=True)
                ins = t.matmul(psw[0:128, 256:512],
                               qt_buf[:, (qb + 1) * 128:(qb + 2) * 128],
                               qt_buf[:, qb * 128:qb * 128 + 256],
                               start=True, stop=True)
                s_t.inc(ins)
                al_t.append(s_t.n)

            @block.vector
            def _(v, n=n, qb=qb):
                v.wait_ge(s_t.h, al_t[qb])
                psw = [ps_a, ps_b][qb % 2]
                v.tensor_add(dpre[:, qb * 256:(qb + 1) * 256],
                             psw[0:128, 0:256], band_sb[:])
                v.drain()
                v.scalar_tensor_tensor(dpre[:, qb * 256:(qb + 1) * 256],
                                       psw[0:128, 256:512], -2.0,
                                       dpre[:, qb * 256:(qb + 1) * 256],
                                       op0=ALU.mult, op1=ALU.add)
                v.drain()
                v.tensor_scalar(dpre[:, qb * 256:(qb + 1) * 256],
                                dpre[:, qb * 256:(qb + 1) * 256],
                                wtp[:, qb:qb + 1], 1e-8, ALU.add, ALU.max)
                s_v.inc(v.drain())
                al_v.append(s_v.n)

        @block.scalar
        def _(s, n=n):
            s_v.wait(s)
            for qb in range(16):
                ins = s.activation(out=dpre[:, qb * 256:(qb + 1) * 256],
                                   in_=dpre[:, qb * 256:(qb + 1) * 256],
                                   func=AF.Sqrt, bias=0.0, scale=1.0)
            s.drain()
            for qb in range(16):
                ins = s.activation(out=pbuf[:, qb * 256:(qb + 1) * 256],
                                   in_=dpre[:, qb * 256:(qb + 1) * 256],
                                   func=AF.Exp, bias=0.0, scale=-1.0,
                                   accum_out=denb[:, qb:qb + 1])
            s_a.inc(s.drain())

        @block.vector
        def _(v, n=n):
            s_a.wait(v)
            v.tensor_scalar_add(rden[:, 0:16], denb[:, 0:16], 1e-8)
            v.drain()
            v.reciprocal(rden[:, 0:16], rden[:, 0:16])
            s_v.inc(v.drain())
        rden_done = s_v.n

        for qb in range(16):
            @block.vector
            def _(v, n=n, qb=qb):
                v.wait_ge(s_v.h, rden_done)
                v.tensor_scalar_mul(pbuf[:, qb * 256:(qb + 1) * 256],
                                    pbuf[:, qb * 256:(qb + 1) * 256],
                                    rden[:, qb:qb + 1])
                s_v.inc(v.drain())
            pn_done = s_v.n

            @block.tensor
            def _(t, n=n, qb=qb):
                t.wait_ge(s_v.h, pn_done)
                for j in range(2):
                    ins = t.transpose(ps_tp[:, j * 128:(j + 1) * 128],
                                      pbuf[:, qb * 256 + j * 128:qb * 256 + (j + 1) * 128],
                                      identb[:])
                s_t.inc(ins)
            tp_done2 = s_t.n

            @block.vector
            def _(v, n=n, qb=qb):
                v.wait_ge(s_t.h, tp_done2)
                v.tensor_copy(pt_sb[:], ps_tp[:])
                s_v.inc(v.drain())
            pt_done = s_v.n

            @block.tensor
            def _(t, n=n, qb=qb):
                t.wait_ge(s_v.h, pt_done)
                for j in range(2):
                    t.matmul(ps_ctx[:], v_buf[:, (qb + j) * 128:(qb + j + 1) * 128],
                             pt_sb[:, j * 128:(j + 1) * 128],
                             start=(j == 0), stop=(j == 1))
                ins = t.matmul(ps_y[:], heb_sb[:, n * 128:(n + 1) * 128],
                               qb_buf[:, (qb + 1) * 128:(qb + 2) * 128],
                               start=True, stop=True)
                s_t.inc(ins)
            hc_done = s_t.n

            @block.scalar
            def _(s, n=n, qb=qb):
                s.wait_ge(s_t.h, hc_done)
                s.activation(out=hb_sb[:], in_=ps_y[:], func=AF.Copy,
                             bias=0.0, scale=1.0)
                s_a.inc(s.drain())
            hb_done = s_a.n

            @block.tensor
            def _(t, n=n, qb=qb):
                t.wait_ge(s_a.h, hb_done)
                t.wait_ge(s_v.h, rnorm_done)
                ins = t.matmul(ps_y[:], onesf[32:33, 0:128],
                               rows_t[32:33, (qb + 1) * 128:(qb + 2) * 128],
                               start=True, stop=True)
                s_t.inc(ins)
            rb_done = s_t.n

            @block.vector
            def _(v, n=n, qb=qb):
                v.wait_ge(s_t.h, rb_done)
                v.tensor_mul(t2_sb[:], hb_sb[:], ps_y[:])
                v.drain()
                v.tensor_add(t3_sb[:], t2_sb[:], ps_ctx[:])
                v.drain()
                v.tensor_mul(zf[:, n * TCH + qb * 128:n * TCH + (qb + 1) * 128],
                             t3_sb[:], qb_buf[:, (qb + 1) * 128:(qb + 2) * 128])
                s_v.inc(v.drain())
                n_zf.append(s_v.n)

    es_S.close()
    dec_sb = ctx.sbuf("dec_sb", [128, 8 * 1024], BF16)

    @block.gpsimd
    def _(g):
        g.wait_ge(s_v.h, n_zf[-1])
        for c in range(8):
            swl.dma(g.dma_start(out=fm(dec_sb)[:, c, :], in_=dec[c]))
    dec_loaded = swl.n

    ep_done = []
    ep_ln = []
    st_cnt = []
    for tb in range(NBLK):
        @block.sync
        def _(sync, tb=tb):
            if tb >= 2:
                sync.wait_ge(s_p.h, ep_done[tb - 2])
            sld.dma(sync.dma_start(out=x2e[tb % 2][:],
                                   in_=x2[tb * 128:(tb + 1) * 128, :]))

        @block.tensor
        def _(t, tb=tb):
            if tb == 0:
                t.wait_ge(s_v.h, n_zf[-1])
                t.wait_ge(swl.h, dec_loaded)
            if tb >= 1:
                t.wait_ge(s_v.h, ep_ln[tb - 1])
            for c in range(8):
                for nh in range(2):
                    ins = t.matmul(
                        ps_o[:, nh * 512:(nh + 1) * 512],
                        zf[:, c * TCH + tb * 128:c * TCH + (tb + 1) * 128],
                        dec_sb[:, c * 1024 + nh * 512:c * 1024 + nh * 512 + 512],
                        start=(c == 0), stop=(c == 7))
            s_t.inc(ins)

        @block.vector
        def _(v):
            s_t.wait(v)

        emit_ln_stats(block, ps_o[:], stat, sq[:], s_v, s_a)

        @block.vector
        def _(v, tb=tb):
            if tb >= 1:
                v.wait_ge(s_p.h, ep_done[tb - 1])
            v.tensor_scalar(t1[:], ps_o[:], stat[:, 0:1], stat[:, 3:4],
                            ALU.add, ALU.mult)
            s_v.inc(v.drain())
            ep_ln.append(s_v.n)

        @block.vector
        def _(v, tb=tb):
            sld.wait(v)
            if tb >= 2:
                v.wait_ge(sst.h, st_cnt[tb - 2])
            ins = v.tensor_add(out_sb[tb % 2][:], t1[:], x2e[tb % 2][:])
            s_p.inc(ins)
            ep_done.append(s_p.n)

        @block.sync
        def _(sync, tb=tb):
            sync.wait_ge(s_p.h, ep_done[tb])
            sst.dma(sync.dma_start(out=x3[tb * 128:(tb + 1) * 128, :],
                                   in_=out_sb[tb % 2][:]))
            st_cnt.append(sst.n)

    @block.sync
    def _(sync):
        sst.wait(sync)



def emit_l1ffn(nc, ctx, block, io):
    x3 = io["x3"]; x4 = io["x4"]; ident = io["ident"]
    wrt = io["wrt"]; br = io["br"]; w1e = io["w1e"]; b1l = io["b1l"]
    w2e = io["w2e"]; b2i = io["b2i"]; gam = io["gam"]; bet = io["bet"]

    xnf = ctx.sbuf("xnf", [128, 8 * TCH], BF16)
    ubuf = ctx.sbuf("ubuf", [128, 32 * 512], BF16)
    w2sb = ctx.sbuf("w2sb", [128, 32 * 1024], BF16)
    acc = ctx.sbuf("acc", [128, 4 * 1024], F32)
    stats_all = ctx.sbuf("stats_all", [128, 32], F32)
    rw_all = ctx.sbuf("rw_all", [128, 8 * 16], F32)
    rwT_all = ctx.sbuf("rwT_all", [8, 16 * 128], BF16)
    rwb = ctx.sbuf("rwb", [128, 8], BF16)
    wr_sb = ctx.sbuf("wr_sb", [128, 8 * 8], BF16)
    br_sb = ctx.sbuf("br_sb", [1, 8], BF16)
    b1_sb = ctx.sbuf("b1_sb", [128, 8 * 32], F32)
    b2_sb = ctx.sbuf("b2_sb", [8, 1024], BF16)
    gb_bc = ctx.sbuf("gb_bc", [128, 2048], F32)
    g_row = ctx.sbuf("g_row", [1, 1024], BF16)
    b_row = ctx.sbuf("b_row", [1, 1024], BF16)
    onesb = ctx.sbuf("onesb", [1, 128], BF16)
    wsl = [ctx.sbuf(f"wsl{i}", [128, 8 * 128], BF16) for i in range(2)]
    logit_sb = ctx.sbuf("logit_sb", [128, 8], F32)
    top_sb = ctx.sbuf("top_sb", [128, 8], F32)
    vv_sb = ctx.sbuf("vv_sb", [128, 4], F32)
    identb = ctx.sbuf("identb", [128, 128], BF16)
    stat = ctx.sbuf("stat", [128, 8], F32)
    sq = ctx.sbuf("sq", [128, D], F32)
    xf = [ctx.sbuf(f"xf{i}", [128, D], F32) for i in range(2)]
    xn_t = ctx.sbuf("xn_t", [128, D], BF16)
    t1 = ctx.sbuf("t1", [128, D], F32)
    t2 = ctx.sbuf("t2", [128, D], F32)
    inner = ctx.sbuf("inner", [128, D], F32)
    out_sb = [ctx.sbuf(f"out_sb{i}", [128, D], F32) for i in range(2)]
    x3e = xf

    ps_t = ctx.psum("ps_t", [128, 1024], BF16)
    ps_u = [ctx.psum(f"ps_u{i}", [128, 512], F32) for i in range(2)]
    ps_o = ctx.psum("ps_o", [128, 1024], F32)
    ps_r = ctx.psum("ps_r", [128, 8], F32)
    ps_rt = ctx.psum("ps_rt", [8, 128], BF16)

    sld = ctx.sem("sld")
    swl = ctx.sem("swl")
    sw2 = ctx.sem("sw2")
    swu = [ctx.sem(f"swu{i}") for i in range(2)]
    s_t = ctx.sem("s_t")
    s_a = ctx.sem("s_a")
    s_v = ctx.sem("s_v")
    s_p = ctx.sem("s_p")
    sst = ctx.sem("sst")
    swl_base = swl.n

    @block.gpsimd
    def _(g):
        swl.dma(g.dma_start(out=identb[:], in_=ident[:]))
        for c in range(8):
            swl.dma(g.dma_start(out=wr_sb[:, c * 8:(c + 1) * 8], in_=wrt[c]))
        swl.dma(g.dma_start(out=br_sb[:], in_=br[:]))
        swl.dma(g.dma_start(out=b1_sb[:], in_=b1l[:]))
        swl.dma(g.dma_start(out=b2_sb[:], in_=b2i[:]))
        swl.dma(g.dma_start(out=g_row[:], in_=gam[:]))
        swl.dma(g.dma_start(out=b_row[:], in_=bet[:]))
    consts_w = swl.n

    @block.vector
    def _(v):
        v.memset(onesb[:], 1.0)
        s_v.inc(v.drain())
    ones_done = s_v.n

    @block.tensor
    def _(t):
        swl.wait(t, consts_w)
        t.wait_ge(s_v.h, ones_done)
        for nh in range(2):
            ins = t.matmul(ps_o[:, nh * 512:(nh + 1) * 512], onesb[:],
                           g_row[0:1, nh * 512:(nh + 1) * 512],
                           start=True, stop=True)
        s_t.inc(ins)

    @block.vector
    def _(v):
        s_t.wait(v)
        v.tensor_copy(gb_bc[:, 0:1024], ps_o[:])
        s_v.inc(v.drain())

    @block.tensor
    def _(t):
        s_v.wait(t)
        for nh in range(2):
            ins = t.matmul(ps_o[:, nh * 512:(nh + 1) * 512], onesb[:],
                           b_row[0:1, nh * 512:(nh + 1) * 512],
                           start=True, stop=True)
        s_t.inc(ins)

    @block.vector
    def _(v):
        s_t.wait(v)
        v.tensor_copy(gb_bc[:, 1024:2048], ps_o[:])
        s_v.inc(v.drain())
    gb_done = s_v.n

    ln_done = []
    ev_done = []
    for i in range(NBLK):
        @block.sync
        def _(sync, i=i):
            if i >= 2:
                sync.wait_ge(s_v.h, ln_done[i - 2])
            sld.dma(sync.dma_start(out=xf[i % 2][:], in_=x3[i * 128:(i + 1) * 128, :]))

        @block.vector
        def _(v, i=i):
            sld.wait(v)

        emit_ln_stats(block, xf[i % 2][:], stat, sq[:], s_v, s_a)

        @block.vector
        def _(v, i=i):
            if i >= 1:
                v.wait_ge(s_v.h, ev_done[i - 1])
            v.tensor_copy(stats_all[:, 2 * i:2 * i + 1], stat[:, 0:1])
            v.drain()
            v.tensor_copy(stats_all[:, 2 * i + 1:2 * i + 2], stat[:, 3:4])
            v.drain()
            v.tensor_scalar(xn_t[:], xf[i % 2][:], stat[:, 0:1], stat[:, 3:4],
                            ALU.add, ALU.mult)
            s_v.inc(v.drain())
            ln_done.append(s_v.n)

        @block.tensor
        def _(t, i=i):
            swl.wait(t, swl_base + 16)
            s_v.wait(t)
            for c in range(8):
                ins = t.transpose(ps_t[:, c * 128:(c + 1) * 128],
                                  xn_t[:, c * 128:(c + 1) * 128], identb[:])
            s_t.inc(ins)

        @block.vector
        def _(v, i=i):
            s_t.wait(v)
            v.tensor_copy(fm(xnf)[:, :, i * 128:(i + 1) * 128], fm(ps_t)[:, :, :])
            s_v.inc(v.drain())
            ev_done.append(s_v.n)

    rt_done = []
    for i in range(NBLK):
        @block.tensor
        def _(t, i=i):
            if i == 0:
                t.wait_ge(s_v.h, ev_done[-1])
            if i >= 1:
                t.wait_ge(s_a.h, s_a.n)
            for c in range(8):
                t.matmul(ps_r[:], fm(xnf)[:, c, i * 128:(i + 1) * 128],
                         wr_sb[:, c * 8:(c + 1) * 8],
                         start=(c == 0), stop=False)
            ins = t.matmul(ps_r[:], onesb[:], br_sb[:], start=False, stop=True)
            s_t.inc(ins)

        @block.scalar
        def _(s, i=i):
            s_t.wait(s)
            ins = s.activation(out=logit_sb[:], in_=ps_r[:], func=AF.Copy,
                               bias=0.0, scale=1.0)
            s_a.inc(ins)

        @block.vector
        def _(v, i=i):
            s_a.wait(v)
            v.reduce_max(vv_sb[:, 0:1], logit_sb[:], axis=AX.X)
            v.drain()
            v.tensor_scalar(top_sb[:], logit_sb[:], vv_sb[:, 0:1], None,
                            ALU.is_equal)
            v.drain()
            v.scalar_tensor_tensor(top_sb[:], top_sb[:], -1e30, logit_sb[:],
                                   op0=ALU.mult, op1=ALU.add)
            v.drain()
            v.reduce_max(vv_sb[:, 1:2], top_sb[:], axis=AX.X)
            v.drain()
            v.tensor_sub(vv_sb[:, 2:3], vv_sb[:, 0:1], vv_sb[:, 1:2])
            s_v.inc(v.drain())

        @block.scalar
        def _(s, i=i):
            s_v.wait(s)
            s.activation(out=vv_sb[:, 3:4], in_=vv_sb[:, 2:3], func=AF.Sigmoid,
                         bias=0.0, scale=1.0)
            s_a.inc(s.drain())

        @block.vector
        def _(v, i=i):
            s_a.wait(v)
            rw = rw_all[:, i * 8:(i + 1) * 8]
            v.tensor_scalar(sq[:, 0:8], top_sb[:], vv_sb[:, 1:2], None,
                            ALU.is_equal)
            v.drain()
            v.tensor_scalar(sq[:, 8:16], logit_sb[:], vv_sb[:, 0:1], None,
                            ALU.is_equal)
            v.drain()
            v.tensor_scalar(vv_sb[:, 2:3], vv_sb[:, 3:4], -1.0, 1.0,
                            ALU.mult, ALU.add)
            v.drain()
            v.tensor_scalar_mul(rw, sq[:, 8:16], vv_sb[:, 3:4])
            v.drain()
            v.tensor_scalar_mul(sq[:, 0:8], sq[:, 0:8], vv_sb[:, 2:3])
            v.drain()
            v.tensor_add(rw, rw, sq[:, 0:8])
            v.drain()
            v.tensor_copy(rwb[:], rw)
            s_v.inc(v.drain())

        @block.tensor
        def _(t, i=i):
            s_v.wait(t)
            ins = t.transpose(ps_rt[:], rwb[:], identb[:])
            s_t.inc(ins)

        @block.vector
        def _(v, i=i):
            s_t.wait(v)
            v.tensor_copy(rwT_all[:, i * 128:(i + 1) * 128], ps_rt[0:8, :])
            s_v.inc(v.drain())
            rt_done.append(s_v.n)

    up_t = []
    up_a = []
    ob2_a = []
    oev_v = []
    ep_in = []
    ep_done = []
    ep_ln = []
    st_cnt = []
    w2l_cnt = []

    for grp in range(L1FFN_GROUPS):
        for tb in range(4):
            g_tb = grp * 4 + tb

            @block.tensor
            def _(t, grp=grp, tb=tb, g_tb=g_tb):
                if g_tb == 0:
                    t.wait_ge(s_v.h, rt_done[-1])
                if tb == 0 and oev_v:
                    t.wait_ge(s_v.h, oev_v[-1])
                if tb >= 1:
                    t.wait_ge(s_a.h, ob2_a[-1])
                for nh in range(2):
                    ins = t.matmul(
                        ps_o[:, nh * 512:(nh + 1) * 512],
                        rwT_all[:, g_tb * 128:(g_tb + 1) * 128],
                        b2_sb[:, nh * 512:(nh + 1) * 512],
                        start=True, stop=True)
                s_t.inc(ins)

            @block.scalar
            def _(s, grp=grp, tb=tb, g_tb=g_tb):
                s_t.wait(s)
                if len(ep_in) >= (grp - 1) * 4 + tb + 1 and grp >= 1:
                    s.wait_ge(s_v.h, ep_in[(grp - 1) * 4 + tb])
                ins = s.activation(out=acc[:, tb * 1024:(tb + 1) * 1024],
                                   in_=ps_o[:], func=AF.Copy, bias=0.0, scale=1.0)
                s_a.inc(ins)
                ob2_a.append(s_a.n)

        for e in range(L1FFN_EXPERTS):
            @block.gpsimd
            def _(g, grp=grp, e=e):
                if w2l_cnt:
                    g.wait_ge(s_t.h, w2l_cnt[-1])
                for c0 in range(0, 32, 8):
                    sw2.dma(g.dma_start(
                        out=w2sb[:, c0 * 1024:(c0 + 8) * 1024].rearrange(
                            "p (c o) -> p c o", c=8),
                        in_=w2e[e, c0:c0 + 8].rearrange("c p o -> p c o")))
            w2_loaded = sw2.n

            for oc in range(32):
                kk = len(up_t)

                @block.gpsimd
                def _(g, e=e, oc=oc, kk=kk):
                    if kk >= 2:
                        g.wait_ge(s_t.h, up_t[kk - 2])
                    swu[kk % 2].dma(g.dma_start(out=wsl[kk % 2][:, :], in_=w1e[e, oc]))
                ld_sem, ld_cnt = swu[kk % 2].h, swu[kk % 2].n

                @block.tensor
                def _(t, grp=grp, e=e, oc=oc, kk=kk, ld_sem=ld_sem, ld_cnt=ld_cnt):
                    t.wait_ge(ld_sem, ld_cnt)
                    if kk >= 2:
                        t.wait_ge(s_a.h, up_a[kk - 2])
                    for c in range(8):
                        ins = t.matmul(
                            ps_u[kk % 2][:],
                            wsl[kk % 2][:, c * 128:(c + 1) * 128],
                            fm(xnf)[:, c, grp * 512:(grp + 1) * 512],
                            start=(c == 0), stop=(c == 7))
                    s_t.inc(ins)
                    up_t.append(s_t.n)

                @block.scalar
                def _(s, grp=grp, e=e, oc=oc, kk=kk):
                    s.wait_ge(s_t.h, up_t[kk])
                    if oc == 0 and oev_v:
                        s.wait_ge(s_v.h, oev_v[-1])
                    ins = s.activation(out=ubuf[:, oc * 512:(oc + 1) * 512],
                                       in_=ps_u[kk % 2][:], func=AF.Relu,
                                       bias=b1_sb[:, e * 32 + oc:e * 32 + oc + 1],
                                       scale=1.0)
                    s_a.inc(ins)
                    up_a.append(s_a.n)

            for tb in range(4):
                @block.tensor
                def _(t, grp=grp, e=e, tb=tb, w2_loaded=w2_loaded):
                    t.wait_ge(sw2.h, w2_loaded)
                    s_a.wait(t)
                    if oev_v:
                        t.wait_ge(s_v.h, oev_v[-1])
                    for c in range(32):
                        for nh in range(2):
                            ins = t.matmul(
                                ps_o[:, nh * 512:(nh + 1) * 512],
                                ubuf[:, c * 512 + tb * 128:c * 512 + tb * 128 + 128],
                                w2sb[:, c * 1024 + nh * 512:c * 1024 + nh * 512 + 512],
                                start=(c == 0), stop=(c == 31))
                    s_t.inc(ins)
                o_t = s_t.n

                @block.vector
                def _(v, grp=grp, e=e, tb=tb, o_t=o_t):
                    v.wait_ge(s_t.h, o_t)
                    if e == 0:
                        v.wait_ge(s_a.h, ob2_a[grp * 4 + tb])
                    g_tb = grp * 4 + tb
                    v.scalar_tensor_tensor(
                        acc[:, tb * 1024:(tb + 1) * 1024], ps_o[:],
                        rw_all[:, g_tb * 8 + e:g_tb * 8 + e + 1],
                        acc[:, tb * 1024:(tb + 1) * 1024],
                        op0=ALU.mult, op1=ALU.add)
                    s_v.inc(v.drain())
                    oev_v.append(s_v.n)

            w2l_cnt.append(s_t.n)

        for tb in range(4):
            g_tb = grp * 4 + tb

            @block.sync
            def _(sync, g_tb=g_tb):
                if g_tb < 2:
                    sync.wait_ge(s_v.h, ln_done[-1])
                if g_tb >= 2:
                    sync.wait_ge(s_p.h, ep_done[g_tb - 2])
                sld.dma(sync.dma_start(out=x3e[g_tb % 2][:],
                                       in_=x3[g_tb * 128:(g_tb + 1) * 128, :]))

            @block.vector
            def _(v, grp=grp, tb=tb, g_tb=g_tb):
                s_v.wait(v)
                sld.wait(v)
                v.wait_ge(s_v.h, gb_done)
                v.tensor_scalar(t1[:], x3e[g_tb % 2][:],
                                stats_all[:, 2 * g_tb:2 * g_tb + 1],
                                stats_all[:, 2 * g_tb + 1:2 * g_tb + 2],
                                ALU.add, ALU.mult)
                v.drain()
                v.tensor_add(inner[:], t1[:], acc[:, tb * 1024:(tb + 1) * 1024])
                s_v.inc(v.drain())
                ep_in.append(s_v.n)

            emit_ln_stats(block, inner[:], stat, sq[:], s_v, s_a)

            @block.vector
            def _(v, g_tb=g_tb):
                v.tensor_scalar(t1[:], inner[:], stat[:, 0:1], stat[:, 3:4],
                                ALU.add, ALU.mult)
                v.drain()
                v.tensor_mul(t2[:], t1[:], gb_bc[:, 0:1024])
                v.drain()
                v.tensor_add(t2[:], t2[:], gb_bc[:, 1024:2048])
                s_v.inc(v.drain())

            emit_ln_stats(block, t2[:], stat, sq[:], s_v, s_a)

            @block.vector
            def _(v, g_tb=g_tb):
                if g_tb >= 1:
                    v.wait_ge(s_p.h, ep_done[g_tb - 1])
                v.tensor_scalar(t1[:], t2[:], stat[:, 0:1], stat[:, 3:4],
                                ALU.add, ALU.mult)
                s_v.inc(v.drain())
                ep_ln.append(s_v.n)

            @block.vector
            def _(v, g_tb=g_tb):
                sld.wait(v)
                if g_tb >= 2:
                    v.wait_ge(sst.h, st_cnt[g_tb - 2])
                ins = v.tensor_add(out_sb[g_tb % 2][:], t1[:], x3e[g_tb % 2][:])
                s_p.inc(ins)
                ep_done.append(s_p.n)

            @block.sync
            def _(sync, g_tb=g_tb):
                sync.wait_ge(s_p.h, ep_done[g_tb])
                sst.dma(sync.dma_start(out=x4[g_tb * 128:(g_tb + 1) * 128, :],
                                       in_=out_sb[g_tb % 2][:]))
                st_cnt.append(sst.n)

    @block.sync
    def _(sync):
        sst.wait(sync)



def build_fused():
    nc = bass.Bass("TRN2", target_bir_lowering=False)

    def ei(name, shape, dtype=F32):
        return nc.dram_tensor(name, shape, dtype, kind="ExternalInput").ap()

    io = {}
    io["x_seq"] = ei("x_seq", [SEQ, D])
    io["x_own"] = ei("x_own", [TCH, D])
    io["wqt"] = ei("wqt", [8, 8, 128, 128])
    io["wkt"] = ei("wkt", [8, 8, 128, 128])
    io["wvt"] = ei("wvt", [8, 128, 1024])
    io["wot"] = ei("wot", [8, 128, 1024])
    io["maskt"] = ei("maskt", [128, 79 * 128])
    io["ident"] = ei("ident", [128, 128])
    io["w1t"] = ei("w1t", [32, 128, 1024])
    io["w2t"] = ei("w2t", [32, 128, 1024])
    io["h1t"] = ei("h1t", [32, 128, 4096])
    io["h2t"] = ei("h2t", [32, 128, 4096])
    io["w3t"] = ei("w3t", [32, 128, 1024])
    io["w1s"] = ei("w1s", [128, 32])
    io["w2s"] = ei("w2s", [128, 32])
    io["h1s"] = ei("h1s", [128, 32])
    io["h2s"] = ei("h2s", [128, 32])
    io["w3s"] = ei("w3s", [1, 1024])
    io["encq"] = ei("encq", [8, 1024, 128])
    io["encv"] = ei("encv", [8, 1024, 128])
    io["dec"] = ei("dec", [8, 128, 1024])
    io["heb"] = ei("heb", [8, 128, 128])
    io["mw1"] = ei("mw1", [8, 128, 512])
    io["mw2"] = ei("mw2", [4, 128, 1024])
    io["mb1"] = ei("mb1", [1, 512])
    io["mb2"] = ei("mb2", [1, 1024])
    io["base"] = ei("base", [1, 1024])
    io["band"] = ei("band", [128, 256])
    io["wsel"] = ei("wsel", [128, 16])
    io["hsel"] = ei("hsel", [128, 8])
    io["wrt"] = ei("wrt", [8, 128, 8])
    io["br"] = ei("br", [1, 8])
    io["w1e"] = ei("w1e", [8, 32, 128, 1024])
    io["b1l"] = ei("b1l", [128, 8 * 32])
    io["w2e"] = ei("w2e", [8, 32, 128, 1024])
    io["b2i"] = ei("b2i", [8, 1024])
    io["gam"] = ei("gam", [1, 1024])
    io["bet"] = ei("bet", [1, 1024])

    io["x4"] = nc.dram_tensor("x4", [TCH, D], F32, kind="ExternalOutput").ap()

    kind_i = "ExternalOutput" if DUMP_STAGES else "Internal"
    io["x1"] = nc.dram_tensor("x1", [TCH, D], F32, kind=kind_i).ap()
    io["x2"] = nc.dram_tensor("x2", [TCH, D], F32, kind=kind_i).ap()
    io["x3"] = nc.dram_tensor("x3", [TCH, D], F32, kind=kind_i).ap()
    io["k_sc"] = nc.dram_tensor("k_sc", [8, 128, SEQ], BF16, kind="Internal").ap()
    io["v_sc"] = nc.dram_tensor("v_sc", [16, SEQ, 64], BF16, kind="Internal").ap()
    cc_in_h = nc.dram_tensor("cc_in", [128, 1040], F32, kind="Internal")
    cc_out_h = nc.dram_tensor("cc_out", [N_CORES * 128, 1040], F32, kind="Internal")
    io["cc_in"] = cc_in_h.ap()
    io["cc_out"] = cc_out_h.ap()
    io["cc_in_h"] = cc_in_h
    io["cc_out_h"] = cc_out_h
    io["halo_d"] = nc.dram_tensor("halo_d", [128, D], F32, kind=kind_i).ap()
    io["sig_d"] = nc.dram_tensor("sig_d", [128, 8], F32, kind=kind_i).ap()

    es = contextlib.ExitStack()
    with es:
        root = Ctx(nc, es)
        for name in ("sld", "swl", "sw2", "swu0", "swu1", "swu2", "swu3",
                     "swh0", "swh1", "svx0", "svx1", "s_t", "s_a", "s_v",
                     "s_p", "sst", "ssc", "scc"):
            root.sem(name)
        block = es.enter_context(nc.Block())
        emitters = [("a", emit_l0attn), ("f", emit_l0ffn), ("s", emit_sig),
                    ("h", emit_l1attn), ("m", emit_l1ffn)]
        for idx, (tag, fn) in enumerate(emitters):
            if idx > 0:
                barrier(block, root)
            with contextlib.ExitStack() as es_sec:
                ctx = Ctx(nc, es_sec, root.sems, tag + "_")
                fn(nc, ctx, block, io)
    return nc



def _chunk_w_ic_oc(w):
    i_n, o_n = w.shape
    return np.ascontiguousarray(
        w.reshape(i_n // 128, 128, o_n // 128, 128).transpose(0, 2, 1, 3))


def _chunk_w_oc_ic(w):
    i_n, o_n = w.shape
    return np.ascontiguousarray(
        w.reshape(i_n // 128, 128, o_n // 128, 128).transpose(2, 0, 1, 3))


def prep_l0attn_shared(inputs):
    d = {}
    d["wqt"] = _chunk_w_ic_oc(np.ascontiguousarray(inputs["L0_attn_Wq"].T))
    d["wkt"] = _chunk_w_ic_oc(np.ascontiguousarray(inputs["L0_attn_Wk"].T))
    d["wvt"] = np.ascontiguousarray(
        np.ascontiguousarray(inputs["L0_attn_Wv"].T).reshape(8, 128, 1024))
    d["wot"] = np.ascontiguousarray(
        np.ascontiguousarray(inputs["L0_attn_Wo"].T).reshape(8, 128, 1024))
    d["ident"] = np.eye(128, dtype=np.float32)
    return d


def make_l0attn_mask(p):
    r = np.arange(128)[None, :]
    c = np.arange(128)[:, None]
    tiles = []
    for mu in range(79):
        delta = 16 * p + 15 - mu
        dd = 128 * delta + r - c
        act = (dd >= 0) & ((dd <= 255) | (dd % 64 == 0))
        tiles.append(act.astype(np.float32))
    return np.ascontiguousarray(np.concatenate(tiles, axis=1))


def _pmaj(w4):
    n_oc, n_ic = w4.shape[0], w4.shape[1]
    return np.ascontiguousarray(
        w4.transpose(0, 2, 1, 3).reshape(n_oc, 128, n_ic * 128))


def prep_l0ffn_shared(inputs):
    d = {}
    d["w1t"] = _pmaj(_chunk_w_oc_ic(np.ascontiguousarray(inputs["L0_ffn_w1_w"].T)))
    d["w2t"] = _pmaj(_chunk_w_oc_ic(np.ascontiguousarray(inputs["L0_ffn_w2_w"].T)))
    d["h1t"] = _pmaj(_chunk_w_oc_ic(np.ascontiguousarray(inputs["L0_ffn_h1_w"].T)))
    d["h2t"] = _pmaj(_chunk_w_oc_ic(np.ascontiguousarray(inputs["L0_ffn_h2_w"].T)))
    d["w3t"] = np.ascontiguousarray(
        np.ascontiguousarray(inputs["L0_ffn_w3_w"].T).reshape(32, 128, 1024))
    for nm, key in [("w1s", "L0_ffn_w1_s"), ("w2s", "L0_ffn_w2_s"),
                    ("h1s", "L0_ffn_h1_s"), ("h2s", "L0_ffn_h2_s")]:
        d[nm] = np.ascontiguousarray(inputs[key].reshape(32, 128).T)
    d["w3s"] = np.ascontiguousarray(inputs["L0_ffn_w3_s"].reshape(1, 1024))
    return d


def prep_l1attn_shared(inputs):
    d = {}
    d["encq"] = np.ascontiguousarray(inputs["L1_attn_enc_q"])
    d["encv"] = np.ascontiguousarray(inputs["L1_attn_enc_v"])
    d["dec"] = np.ascontiguousarray(inputs["L1_attn_decoder"].reshape(8, 128, 1024))
    d["heb"] = np.ascontiguousarray(inputs["L1_attn_hebbian"])
    d["mw1"] = np.ascontiguousarray(inputs["L1_attn_m_w1"].reshape(8, 128, 512))
    d["mw2"] = np.ascontiguousarray(inputs["L1_attn_m_w2"].reshape(4, 128, 1024))
    d["mb1"] = np.ascontiguousarray(inputs["L1_attn_m_b1"].reshape(1, 512))
    d["mb2"] = np.ascontiguousarray(inputs["L1_attn_m_b2"].reshape(1, 1024))
    d["base"] = np.ascontiguousarray(inputs["L1_attn_base_metric"].reshape(1, 1024))
    rr = np.arange(128)[:, None]
    ss = np.arange(256)[None, :]
    act = (ss >= rr + 97) & (ss <= rr + 128)
    d["band"] = np.where(act, 0.0, 1e30).astype(np.float32)
    return d


def prep_l1ffn_shared(inputs):
    d = {}
    d["wrt"] = np.ascontiguousarray(
        np.ascontiguousarray(inputs["L1_ffn_Wr"].T).reshape(8, 128, 8))
    d["br"] = np.ascontiguousarray(inputs["L1_ffn_br"].reshape(1, 8))
    w1 = inputs["L1_ffn_W1"]
    d["w1e"] = np.stack([_pmaj(_chunk_w_oc_ic(np.ascontiguousarray(w1[e].T)))
                         for e in range(8)])
    d["b1l"] = np.ascontiguousarray(
        inputs["L1_ffn_b1"].reshape(8, 32, 128).transpose(2, 0, 1).reshape(128, 256))
    w2 = inputs["L1_ffn_W2"]
    d["w2e"] = np.stack([np.ascontiguousarray(w2[e].T).reshape(32, 128, 1024)
                         for e in range(8)])
    d["b2i"] = np.ascontiguousarray(inputs["L1_ffn_b2"])
    d["gam"] = np.ascontiguousarray(inputs["L1_ffn_gamma"].reshape(1, 1024))
    d["bet"] = np.ascontiguousarray(inputs["L1_ffn_beta"].reshape(1, 1024))
    return d


def make_wsel(core):
    b = core // 4
    w = np.zeros((16,), np.float32)
    for r in range(8):
        if r // 4 == b:
            w[r * 2 + 0] = 1.0 / SEQ
            w[r * 2 + 1] = 0.5 / 1024.0
    return np.ascontiguousarray(np.tile(w[None, :], (128, 1)))


def make_hsel(core):
    h = np.zeros((8,), np.float32)
    if core % 4 != 0:
        h[core - 1] = 1.0
    return np.ascontiguousarray(np.tile(h[None, :], (128, 1)))


def kernel(**inputs):
    x = np.ascontiguousarray(inputs["L0_attn_x"], dtype=np.float32)

    sh = {}
    sh.update(prep_l0attn_shared(inputs))
    sh.update(prep_l0ffn_shared(inputs))
    sh.update(prep_l1attn_shared(inputs))
    sh.update(prep_l1ffn_shared(inputs))

    nc = build_fused()
    in_maps = []
    for c in range(N_CORES):
        b, p = c // 4, c % 4
        m = dict(sh)
        m["x_seq"] = np.ascontiguousarray(x[b])
        m["x_own"] = np.ascontiguousarray(x[b][p * TCH:(p + 1) * TCH])
        m["maskt"] = make_l0attn_mask(p)
        m["wsel"] = make_wsel(c)
        m["hsel"] = make_hsel(c)
        in_maps.append(m)
    res = run_bass_kernel_spmd(nc, in_maps, core_ids=list(range(N_CORES)))
    x4 = np.stack([np.asarray(res.results[c]["x4"]) for c in range(N_CORES)])
    return x4.reshape(2, SEQ, D).astype(np.float32)


# version 07 (final)
# speedup vs baseline: 1.0621x; 1.0236x over previous
import contextlib
import numpy as np
import concourse.bass as bass
import concourse.mybir as mybir
from concourse.bass_utils import run_bass_kernel_spmd

F32 = mybir.dt.float32
BF16 = mybir.dt.bfloat16
AF = mybir.ActivationFunctionType
ALU = mybir.AluOpType
AX = mybir.AxisListType

N_CORES = 8
DUMP_STAGES = False
L1FFN_GROUPS = 4
L1FFN_EXPERTS = 8
B = 2
SEQ = 8192
D = 1024
TCH = 2048
NBLK = TCH // 128
LN_EPS = 1e-5



class Sem:

    def __init__(self, nc, es, name):
        self.h = es.enter_context(nc.semaphore(name))
        self.n = 0

    def inc(self, instr, k=1):
        instr.then_inc(self.h, k)
        self.n += k
        return instr

    def dma(self, instr):
        return self.inc(instr, 16)

    def wait(self, eng, n=None):
        eng.wait_ge(self.h, self.n if n is None else n)


class Ctx:
    def __init__(self, nc, es, sems=None, tag=""):
        self.nc = nc
        self.es = es
        self.sems = {} if sems is None else sems
        self.tag = tag

    def sem(self, name):
        if name not in self.sems:
            self.sems[name] = Sem(self.nc, self.es, name)
        return self.sems[name]

    def sub(self, es, subtag):
        return Ctx(self.nc, es, self.sems, self.tag + subtag)

    def sbuf(self, name, shape, dtype):
        return self.es.enter_context(
            self.nc.sbuf_tensor(self.tag + name, shape, dtype))

    def psum(self, name, shape, dtype=F32):
        return self.es.enter_context(
            self.nc.psum_tensor(self.tag + name, shape, dtype))


def fm(ap, c=8):
    return ap[:, :].rearrange("p (c t) -> p c t", c=c)


def barrier(block, ctx, skip=("scc",)):
    snap = [(s.h, s.n) for k, s in ctx.sems.items()
            if s.n > 0 and k not in skip]
    for en in ("sync", "gpsimd", "scalar", "vector", "tensor"):
        @getattr(block, en)
        def _(eng, snap=snap):
            for h, n in snap:
                eng.wait_ge(h, n)


def emit_ln_stats(block, xf, stat, sq, s_v, s_a, n_free=D, eps=LN_EPS):
    @block.vector
    def _(v):
        v.reduce_sum(stat[:, 0:1], xf, axis=AX.X, negate=True)
        v.drain()
        v.tensor_scalar_mul(stat[:, 0:1], stat[:, 0:1], 1.0 / n_free)
        s_v.inc(v.drain())

    @block.scalar
    def _(s):
        s_v.wait(s)
        s.activation(out=sq, in_=xf, func=AF.Square, bias=stat[:, 0:1], scale=1.0)
        s_a.inc(s.drain())

    @block.vector
    def _(v):
        s_a.wait(v)
        v.reduce_sum(stat[:, 1:2], sq, axis=AX.X)
        v.drain()
        v.tensor_scalar(stat[:, 1:2], stat[:, 1:2], 1.0 / n_free, eps,
                        ALU.mult, ALU.add)
        s_v.inc(v.drain())

    @block.scalar
    def _(s):
        s_v.wait(s)
        s.activation(out=stat[:, 2:3], in_=stat[:, 1:2], func=AF.Sqrt,
                     bias=0.0, scale=1.0)
        s_a.inc(s.drain())

    @block.vector
    def _(v):
        s_a.wait(v)
        v.reciprocal(stat[:, 3:4], stat[:, 2:3])
        s_v.inc(v.drain())



def emit_l0attn(nc, ctx, block, io):
    x_own = io["x_own"]; x_seq = io["x_seq"]
    wqt = io["wqt"]; wkt = io["wkt"]; wvt = io["wvt"]; wot = io["wot"]
    maskt = io["maskt"]; ident = io["ident"]
    x1 = io["x1"]; k_sc = io["k_sc"]; v_sc = io["v_sc"]

    q_all = ctx.sbuf("q_all", [128, 8 * TCH], BF16)
    maskm = ctx.sbuf("maskm", [128, 79 * 128], BF16)
    attn_f = ctx.sbuf("attn_f", [128, 8 * TCH], BF16)
    identb = ctx.sbuf("identb", [128, 128], BF16)
    bneg = ctx.sbuf("bneg", [128, 1], F32)
    onesf = ctx.sbuf("onesf", [128, 64], F32)
    stat = ctx.sbuf("stat", [128, 8], F32)
    sq = ctx.sbuf("sq", [128, D], F32)
    xf = [ctx.sbuf(f"xf{i}", [128, D], F32) for i in range(2)]
    xn_t = ctx.sbuf("xn_t", [128, D], BF16)
    t1 = ctx.sbuf("t1", [128, D], F32)
    out_sb = [ctx.sbuf(f"out_sb{i}", [128, D], F32) for i in range(2)]
    x1e = [ctx.sbuf(f"x1e{i}", [128, D], F32) for i in range(2)]

    sld = ctx.sem("sld")
    swl = ctx.sem("swl")
    svx = [ctx.sem(f"svx{i}") for i in range(2)]
    s_t = ctx.sem("s_t")
    s_a = ctx.sem("s_a")
    s_v = ctx.sem("s_v")
    s_p = ctx.sem("s_p")
    sst = ctx.sem("sst")
    ssc = ctx.sem("ssc")

    @block.gpsimd
    def _(g):
        swl.dma(g.dma_start(out=identb[:], in_=ident[:]))
        swl.dma(g.dma_start(out=maskm[:], in_=maskt[:]))
        g.memset(bneg[:], -25.0)
        g.memset(onesf[:], 1.0)
        s_p.inc(g.nop())
    consts_g = s_p.n
    consts_w = swl.n

    with contextlib.ExitStack() as esA:
        ctxA = ctx.sub(esA, "A")
        wq_sb = ctxA.sbuf("wq_sb", [128, 8 * 1024], BF16)
        wk_sb = ctxA.sbuf("wk_sb", [128, 8 * 1024], BF16)
        wv_sb = ctxA.sbuf("wv_sb", [128, 8 * 1024], BF16)
        xnf_g = ctxA.sbuf("xnf_g", [128, 8 * 512], BF16)
        krow = [ctxA.sbuf(f"krow{i}", [128, 512], BF16) for i in range(2)]
        vtok = [ctxA.sbuf(f"vtok{i}", [128, 1024], BF16) for i in range(2)]
        ps_t = ctxA.psum("ps_t", [128, 1024], BF16)
        ps_ab = [ctxA.psum(f"ps_ab{i}", [128, 512], F32) for i in range(2)]

        @block.gpsimd
        def _(g):
            for c in range(8):
                swl.dma(g.dma_start(
                    out=fm(wq_sb)[:, c, :].rearrange("p (o u) -> p o u", o=8),
                    in_=wqt[c].rearrange("o p u -> p o u")))
                swl.dma(g.dma_start(
                    out=fm(wk_sb)[:, c, :].rearrange("p (o u) -> p o u", o=8),
                    in_=wkt[c].rearrange("o p u -> p o u")))
                swl.dma(g.dma_start(out=fm(wv_sb)[:, c, :], in_=wvt[c]))
        wts_loaded = swl.n

        ev_cnt = []
        ln_cnt = []
        kv_t = []
        kv_a = []
        k_store = []
        v_store = []
        q_ev = []

        def ln_block(src_dram, row0, i_par, blkid):
            @block.sync
            def _(sync):
                if len(ln_cnt) >= 2:
                    sync.wait_ge(s_v.h, ln_cnt[-2])
                sld.dma(sync.dma_start(out=xf[blkid % 2][:],
                                       in_=src_dram[row0:row0 + 128, :]))

            @block.vector
            def _(v):
                sld.wait(v)

            emit_ln_stats(block, xf[blkid % 2][:], stat, sq[:], s_v, s_a)

            @block.vector
            def _(v):
                if ev_cnt:
                    v.wait_ge(s_v.h, ev_cnt[-1])
                v.tensor_scalar(xn_t[:], xf[blkid % 2][:], stat[:, 0:1],
                                stat[:, 3:4], ALU.add, ALU.mult)
                s_v.inc(v.drain())
                ln_cnt.append(s_v.n)

            @block.tensor
            def _(t):
                swl.wait(t, consts_w)
                s_v.wait(t)
                for c in range(8):
                    ins = t.transpose(ps_t[:, c * 128:(c + 1) * 128],
                                      xn_t[:, c * 128:(c + 1) * 128], identb[:])
                s_t.inc(ins)

            @block.vector
            def _(v):
                s_t.wait(v)
                v.tensor_copy(fm(xnf_g)[:, :, i_par * 128:(i_par + 1) * 128],
                              fm(ps_t)[:, :, :])
                s_v.inc(v.drain())
                ev_cnt.append(s_v.n)

        blkid = 0
        for g in range(16):
            for i4 in range(4):
                ln_block(x_seq, g * 512 + i4 * 128, i4, blkid)
                blkid += 1
            for oc in range(8):
                kk = len(kv_t)

                @block.tensor
                def _(t, oc=oc, kk=kk):
                    t.wait_ge(swl.h, wts_loaded)
                    s_v.wait(t)
                    if kk >= 2:
                        t.wait_ge(s_v.h, kv_a[kk - 2])
                    for c in range(8):
                        ins = t.matmul(ps_ab[kk % 2][:],
                                       wk_sb[:, c * 1024 + oc * 128:c * 1024 + oc * 128 + 128],
                                       fm(xnf_g)[:, c, :],
                                       start=(c == 0), stop=(c == 7))
                    s_t.inc(ins)
                    kv_t.append(s_t.n)

                @block.vector
                def _(v, oc=oc, kk=kk):
                    s_t.wait(v)
                    if len(k_store) >= 2:
                        v.wait_ge(ssc.h, k_store[-2])
                    v.tensor_copy(krow[kk % 2][:], ps_ab[kk % 2][:])
                    s_v.inc(v.drain())
                    kv_a.append(s_v.n)

                @block.sync
                def _(sync, g=g, oc=oc, kk=kk):
                    sync.wait_ge(s_v.h, kv_a[kk])
                    ssc.dma(sync.dma_start(out=k_sc[oc][:, g * 512:(g + 1) * 512],
                                           in_=krow[kk % 2][:]))
                    k_store.append(ssc.n)

            for tb in range(4):
                for nh in range(2):
                    kk = len(kv_t)

                    @block.tensor
                    def _(t, tb=tb, nh=nh, kk=kk):
                        if kk >= 2:
                            t.wait_ge(s_v.h, kv_a[kk - 2])
                        for c in range(8):
                            ins = t.matmul(
                                ps_ab[kk % 2][:],
                                fm(xnf_g)[:, c, tb * 128:(tb + 1) * 128],
                                wv_sb[:, c * 1024 + nh * 512:c * 1024 + nh * 512 + 512],
                                start=(c == 0), stop=(c == 7))
                        s_t.inc(ins)
                        kv_t.append(s_t.n)

                    @block.vector
                    def _(v, tb=tb, nh=nh, kk=kk):
                        s_t.wait(v)
                        if nh == 0 and len(v_store) >= 2:
                            v.wait_ge(ssc.h, v_store[-2])
                        v.tensor_copy(vtok[tb % 2][:, nh * 512:(nh + 1) * 512],
                                      ps_ab[kk % 2][:])
                        s_v.inc(v.drain())
                        kv_a.append(s_v.n)

                @block.sync
                def _(sync, g=g, tb=tb):
                    sync.wait_ge(s_v.h, kv_a[-1])
                    r0 = g * 512 + tb * 128
                    for hq in range(4):
                        ins = sync.dma_start(
                            out=v_sc[:, r0:r0 + 128, :].rearrange(
                                "h p c -> p h c")[:, 4 * hq:4 * hq + 4, :],
                            in_=vtok[tb % 2][:, :].rearrange(
                                "p (h c) -> p h c", h=16)[:, 4 * hq:4 * hq + 4, :])
                        ssc.dma(ins)
                    v_store.append(ssc.n)

        for g in range(4):
            for i4 in range(4):
                ln_block(x_own, g * 512 + i4 * 128, i4, blkid)
                blkid += 1
            for oc in range(8):
                kk = len(kv_t)

                @block.tensor
                def _(t, oc=oc, kk=kk):
                    s_v.wait(t)
                    if kk >= 2:
                        t.wait_ge(s_v.h, kv_a[kk - 2])
                    for c in range(8):
                        ins = t.matmul(ps_ab[kk % 2][:],
                                       wq_sb[:, c * 1024 + oc * 128:c * 1024 + oc * 128 + 128],
                                       fm(xnf_g)[:, c, :],
                                       start=(c == 0), stop=(c == 7))
                    s_t.inc(ins)
                    kv_t.append(s_t.n)

                @block.vector
                def _(v, g=g, oc=oc, kk=kk):
                    s_t.wait(v)
                    v.tensor_copy(fm(q_all)[:, oc, g * 512:(g + 1) * 512],
                                  ps_ab[kk % 2][:])
                    s_v.inc(v.drain())
                    kv_a.append(s_v.n)
                    q_ev.append(s_v.n)

    with contextlib.ExitStack() as esB:
        ctxB = ctx.sub(esB, "B")
        kcb = [ctxB.sbuf("kcb0", [128, SEQ], BF16)] * 2
        ptb = ctxB.sbuf("ptb", [128, 32 * 512], BF16)
        vex = [ctxB.sbuf(f"vex{i}", [128, 64 * 65], BF16) for i in range(2)]
        wo_sb = ctxB.sbuf("wo_sb", [128, 8 * 1024], BF16)
        dsb = ctxB.sbuf("dsb", [128, 1024], F32)
        araw = ctxB.sbuf("araw", [128, 512], F32)
        ps_s = [ctxB.psum(f"ps_s{i}", [128, 512], F32) for i in range(4)]
        ps_av = ctxB.psum("ps_av", [128, 512], F32)
        ps_bc = ctxB.psum("ps_bc", [128, 512], F32)
        ps_o = ctxB.psum("ps_o", [128, 1024], F32)

        @block.gpsimd
        def _(g):
            g.wait_ge(s_v.h, q_ev[-1])
            g.wait_ge(ssc.h, ssc.n)
            for c in range(8):
                swl.dma(g.dma_start(out=fm(wo_sb)[:, c, :], in_=wot[c]))
        wo_loaded = swl.n

        @block.vector
        def _(v):
            for i in range(2):
                ins = v.memset(vex[i][:, :].rearrange(
                    "p (k c) -> p k c", c=65)[:, :, 64:65], 1.0)
                v.drain()
            s_p.inc(v.nop())
        vex_ones = s_p.n

        kload = []
        vload = []
        exp_a = []
        sc_t = []
        av_t = []
        av_mid = []
        msk_v = []
        den_a = []
        af_v = []

        for kc in range(8):
            @block.sync
            def _(sync, kc=kc):
                if kc == 0:
                    sync.wait_ge(s_v.h, q_ev[-1])
                if kc >= 1:
                    sync.wait_ge(s_t.h, av_t[-1])
                sld.dma(sync.dma_start(out=kcb[kc % 2][:], in_=k_sc[kc]))
                kload.append(sld.n)

            for h2 in range(2):
                h = 2 * kc + h2
                vb = h % 2

                @block.gpsimd
                def _(g, kc=kc, h=h, vb=vb):
                    g.wait_ge(ssc.h, ssc.n)
                    g.wait_ge(s_p.h, vex_ones)
                    if h >= 2:
                        g.wait_ge(s_t.h, av_t[(h - 1) * 4 - 1])
                    for q8 in range(16):
                        svx[vb].dma(g.dma_start(
                            out=vex[vb][:, :].rearrange(
                                "p (k c) -> p k c", c=65)[:, 4 * q8:4 * q8 + 4, 0:64],
                            in_=v_sc[h].rearrange(
                                "(k p) c -> p k c", p=128)[:, 4 * q8:4 * q8 + 4, :]))
                    vload.append((svx[vb].h, svx[vb].n))

                for qg in range(4):
                    hq = len(av_t)
                    for kh in range(2):
                        for kb32 in range(32):
                            kb = 32 * kh + kb32

                            @block.tensor
                            def _(t, kc=kc, h2=h2, qg=qg, kh=kh, kb32=kb32, kb=kb, hq=hq):
                                if kb32 == 0 and kh == 0:
                                    t.wait_ge(sld.h, kload[kc])
                                if kb32 == 0 and qg == 0 and kh == 0 and kc == 0 and h2 == 0:
                                    t.wait_ge(s_v.h, q_ev[-1])
                                if len(exp_a) >= 4:
                                    t.wait_ge(s_a.h, exp_a[-4])
                                ins = t.matmul(
                                    ps_s[len(sc_t) % 4][:],
                                    kcb[kc % 2][h2 * 64:h2 * 64 + 64, kb * 128:kb * 128 + 128],
                                    q_all[h2 * 64:h2 * 64 + 64,
                                          kc * 2048 + qg * 512:kc * 2048 + qg * 512 + 512],
                                    start=True, stop=True)
                                s_t.inc(ins)
                                sc_t.append(s_t.n)

                            @block.scalar
                            def _(s, kb32=kb32, kh=kh, hq=hq):
                                if not exp_a:
                                    s.wait_ge(s_p.h, consts_g)
                                s.wait_ge(s_t.h, sc_t[-1])
                                if kb32 == 0:
                                    if kh == 1:
                                        s.wait_ge(s_v.h, msk_v[-1])
                                        s.wait_ge(s_t.h, av_mid[-1])
                                    elif av_t:
                                        s.wait_ge(s_t.h, av_t[-1])
                                ins = s.activation(
                                    out=ptb[:, kb32 * 512:(kb32 + 1) * 512],
                                    in_=ps_s[(len(sc_t) - 1) % 4][:],
                                    func=AF.Exp, bias=bneg[:, 0:1], scale=0.125)
                                s_a.inc(ins)
                                exp_a.append(s_a.n)

                            if kb32 in (15, 31):
                                s_half = kb32 // 16

                                @block.vector
                                def _(v, qg=qg, kh=kh, s_half=s_half):
                                    if not msk_v:
                                        v.wait_ge(swl.h, consts_w)
                                    s_a.wait(v)
                                    for j in range(4):
                                        mu0 = (15 - 4 * qg - j + 32 * kh)
                                        v.tensor_mul(
                                            fm(ptb, c=32)[:, 16 * s_half:16 * s_half + 16,
                                                          j * 128:(j + 1) * 128],
                                            fm(ptb, c=32)[:, 16 * s_half:16 * s_half + 16,
                                                          j * 128:(j + 1) * 128],
                                            fm(maskm, c=79)[:, mu0 + 16 * s_half:
                                                            mu0 + 16 * s_half + 16, :])
                                        v.drain()
                                    s_v.inc(v.nop())
                                    msk_v.append(s_v.n)

                        for kb32 in range(32):
                            kb = 32 * kh + kb32

                            @block.tensor
                            def _(t, vb=vb, kh=kh, kb32=kb32, kb=kb, h=h):
                                if kb32 == 0:
                                    t.wait_ge(s_v.h, msk_v[-2])
                                    if kh == 0:
                                        t.wait_ge(*vload[h])
                                        if den_a:
                                            t.wait_ge(s_a.h, den_a[-1])
                                if kb32 == 16:
                                    t.wait_ge(s_v.h, msk_v[-1])
                                ins = t.matmul(
                                    ps_av[0:65, :],
                                    vex[vb][:, kb * 65:kb * 65 + 65],
                                    ptb[:, kb32 * 512:(kb32 + 1) * 512],
                                    start=(kh == 0 and kb32 == 0),
                                    stop=(kh == 1 and kb32 == 31))
                                s_t.inc(ins)
                                if kb32 == 31 and kh == 0:
                                    av_mid.append(s_t.n)

                    av_t.append(s_t.n)

                    @block.scalar
                    def _(s, hq=hq):
                        s.wait_ge(s_t.h, av_t[hq])
                        if af_v:
                            s.wait_ge(s_v.h, af_v[-1])
                        s.activation(out=dsb[64:65, 0:512], in_=ps_av[64:65, :],
                                     func=AF.Copy, bias=0.0, scale=1.0)
                        s.drain()
                        ins = s.activation(out=araw[0:64, :], in_=ps_av[0:64, :],
                                           func=AF.Copy, bias=0.0, scale=1.0)
                        s_a.inc(ins)
                        den_a.append(s_a.n)

                    @block.vector
                    def _(v):
                        s_a.wait(v)
                        v.reciprocal(dsb[64:65, 512:1024], dsb[64:65, 0:512])
                        s_v.inc(v.drain())

                    @block.tensor
                    def _(t):
                        t.wait_ge(s_p.h, consts_g)
                        s_v.wait(t)
                        ins = t.matmul(ps_bc[0:64, :], onesf[64:65, 0:64],
                                       dsb[64:65, 512:1024], start=True, stop=True)
                        s_t.inc(ins)

                    @block.vector
                    def _(v, kc=kc, h2=h2, qg=qg):
                        s_t.wait(v)
                        v.tensor_mul(
                            attn_f[h2 * 64:h2 * 64 + 64,
                                   kc * 2048 + qg * 512:kc * 2048 + qg * 512 + 512],
                            araw[0:64, :], ps_bc[0:64, :])
                        s_v.inc(v.drain())
                        af_v.append(s_v.n)

        ep_done = []
        ep_ln = []
        st_cnt = []
        for tb in range(NBLK):
            @block.sync
            def _(sync, tb=tb):
                if tb >= 2:
                    sync.wait_ge(s_p.h, ep_done[tb - 2])
                sld.dma(sync.dma_start(out=x1e[tb % 2][:],
                                       in_=x_own[tb * 128:(tb + 1) * 128, :]))

            @block.tensor
            def _(t, tb=tb):
                if tb == 0:
                    t.wait_ge(swl.h, wo_loaded)
                    s_v.wait(t)
                if tb >= 1:
                    t.wait_ge(s_v.h, ep_ln[tb - 1])
                for c in range(8):
                    for nh in range(2):
                        ins = t.matmul(
                            ps_o[:, nh * 512:(nh + 1) * 512],
                            fm(attn_f)[:, c, tb * 128:(tb + 1) * 128],
                            wo_sb[:, c * 1024 + nh * 512:c * 1024 + nh * 512 + 512],
                            start=(c == 0), stop=(c == 7))
                s_t.inc(ins)

            @block.vector
            def _(v):
                s_t.wait(v)

            emit_ln_stats(block, ps_o[:], stat, sq[:], s_v, s_a)

            @block.vector
            def _(v, tb=tb):
                if tb >= 1:
                    v.wait_ge(s_p.h, ep_done[tb - 1])
                v.tensor_scalar(t1[:], ps_o[:], stat[:, 0:1], stat[:, 3:4],
                                ALU.add, ALU.mult)
                s_v.inc(v.drain())
                ep_ln.append(s_v.n)

            @block.vector
            def _(v, tb=tb):
                sld.wait(v)
                if tb >= 2:
                    v.wait_ge(sst.h, st_cnt[tb - 2])
                ins = v.tensor_add(out_sb[tb % 2][:], t1[:], x1e[tb % 2][:])
                s_p.inc(ins)
                ep_done.append(s_p.n)

            @block.sync
            def _(sync, tb=tb):
                sync.wait_ge(s_p.h, ep_done[tb])
                sst.dma(sync.dma_start(out=x1[tb * 128:(tb + 1) * 128, :],
                                       in_=out_sb[tb % 2][:]))
                st_cnt.append(sst.n)

        @block.sync
        def _(sync):
            sst.wait(sync)



def emit_l0ffn(nc, ctx, block, io):
    x1 = io["x1"]; x2 = io["x2"]; ident = io["ident"]
    w1t = io["w1t"]; w2t = io["w2t"]; h1t = io["h1t"]; h2t = io["h2t"]
    w3t = io["w3t"]
    w1s = io["w1s"]; w2s = io["w2s"]; h1s = io["h1s"]; h2s = io["h2s"]
    w3s = io["w3s"]

    xnf = ctx.sbuf("xnf", [128, 8 * TCH], BF16)
    big2 = ctx.sbuf("big2", [128, 32 * 1024], BF16)
    hbuf = ctx.sbuf("hbuf", [128, 32 * 512], BF16)
    w3sb = big2
    h1o = big2[:, 0:32 * 512]
    xf = [ctx.sbuf(f"xf{i}", [128, D], F32) for i in range(2)]
    xn_t = ctx.sbuf("xn_t", [128, D], BF16)
    identb = ctx.sbuf("identb", [128, 128], BF16)
    scl = ctx.sbuf("scl", [128, 4 * 32], F32)
    w3sb_bc = ctx.sbuf("w3sb_bc", [128, D], F32)
    w3s_row = ctx.sbuf("w3s_row", [1, D], BF16)
    ones1 = ctx.sbuf("ones1", [1, 128], BF16)
    stat = ctx.sbuf("stat", [128, 8], F32)
    sq = ctx.sbuf("sq", [128, D], F32)
    usb = [ctx.sbuf(f"usb{i}", [128, 512], F32) for i in range(2)]
    wsl = [ctx.sbuf(f"wsl{i}", [128, 8 * 128], BF16) for i in range(4)]
    hsl = [ctx.sbuf(f"hsl{i}", [128, 32 * 128], BF16) for i in range(2)]
    t1 = ctx.sbuf("t1", [128, D], F32)
    out_sb = [ctx.sbuf(f"out_sb{i}", [128, D], F32) for i in range(2)]
    x1e = [ctx.sbuf(f"x1e{i}", [128, D], F32) for i in range(2)]

    ps_t = ctx.psum("ps_t", [128, 1024], BF16)
    ps_u = [ctx.psum(f"ps_u{i}", [128, 512], F32) for i in range(4)]
    ps_o = ctx.psum("ps_o", [128, 1024], F32)

    sld = ctx.sem("sld")
    swl = ctx.sem("swl")
    sw2 = ctx.sem("sw2")
    swu = [ctx.sem(f"swu{i}") for i in range(4)]
    swh = [ctx.sem(f"swh{i}") for i in range(2)]
    s_t = ctx.sem("s_t")
    s_a = ctx.sem("s_a")
    s_v = ctx.sem("s_v")
    s_p = ctx.sem("s_p")
    sst = ctx.sem("sst")
    sld_base = sld.n

    @block.gpsimd
    def _(g):
        swl.dma(g.dma_start(out=identb[:], in_=ident[:]))
        swl.dma(g.dma_start(out=scl[:, 0:32], in_=w1s[:]))
        swl.dma(g.dma_start(out=scl[:, 32:64], in_=w2s[:]))
        swl.dma(g.dma_start(out=scl[:, 64:96], in_=h1s[:]))
        swl.dma(g.dma_start(out=scl[:, 96:128], in_=h2s[:]))
        swl.dma(g.dma_start(out=w3s_row[:], in_=w3s[:]))
        g.memset(ones1[:], 1.0)
        s_p.inc(g.nop())
    consts_loaded = swl.n
    ones_ready = s_p.n

    @block.tensor
    def _(t):
        swl.wait(t, consts_loaded)
        t.wait_ge(s_p.h, ones_ready)
        t.matmul(ps_o[:, 0:512], ones1[:], w3s_row[:, 0:512], start=True, stop=True)
        ins = t.matmul(ps_o[:, 512:1024], ones1[:], w3s_row[:, 512:1024],
                       start=True, stop=True)
        s_t.inc(ins)
    bc_mm = s_t.n

    @block.vector
    def _(v):
        v.wait_ge(s_t.h, bc_mm)
        v.tensor_copy(w3sb_bc[:], ps_o[:])
        s_v.inc(v.drain())
    bc_done = s_v.n

    ln_done = []
    ev_done = []
    for i in range(NBLK):
        @block.sync
        def _(sync, i=i):
            if i >= 2:
                sync.wait_ge(s_v.h, ln_done[i - 2])
            sld.dma(sync.dma_start(out=xf[i % 2][:], in_=x1[i * 128:(i + 1) * 128, :]))

        @block.vector
        def _(v, i=i):
            sld.wait(v, sld_base + 16 * (i + 1))

        emit_ln_stats(block, xf[i % 2][:], stat, sq[:], s_v, s_a)

        @block.vector
        def _(v, i=i):
            if i >= 1:
                v.wait_ge(s_v.h, ev_done[i - 1])
            v.tensor_scalar(xn_t[:], xf[i % 2][:], stat[:, 0:1], stat[:, 3:4],
                            ALU.add, ALU.mult)
            s_v.inc(v.drain())
            ln_done.append(s_v.n)

        @block.tensor
        def _(t, i=i):
            s_v.wait(t)
            for c in range(8):
                ins = t.transpose(ps_t[:, c * 128:(c + 1) * 128],
                                  xn_t[:, c * 128:(c + 1) * 128], identb[:])
            s_t.inc(ins)

        @block.vector
        def _(v, i=i):
            s_t.wait(v)
            v.tensor_copy(fm(xnf)[:, :, i * 128:(i + 1) * 128],
                          fm(ps_t)[:, :, :])
            s_v.inc(v.drain())
            ev_done.append(s_v.n)

    up_t = []
    up_a = []
    up_p = []
    hp_t = []
    hp_a = []
    ep_done = []
    ep_ln = []
    st_cnt = []
    for grp in range(4):
        tok0 = grp * 512
        for oc in range(32):
            for kind in range(2):
                k = len(up_t)
                slot = 2 * kind + oc % 2
                wslb = wsl[slot]
                wt = w1t if kind == 0 else w2t

                @block.gpsimd
                def _(g, oc=oc, k=k, wslb=wslb, wt=wt, slot=slot):
                    if k >= 4:
                        g.wait_ge(s_t.h, up_t[k - 4])
                    swu[slot].dma(g.dma_start(out=wslb[:, :], in_=wt[oc]))
                ld_sem, ld_cnt = swu[slot].h, swu[slot].n

                @block.tensor
                def _(t, k=k, wslb=wslb, tok0=tok0, ld_sem=ld_sem, ld_cnt=ld_cnt):
                    t.wait_ge(ld_sem, ld_cnt)
                    if k == 0:
                        t.wait_ge(s_v.h, ev_done[-1])
                    if k >= 4:
                        t.wait_ge(s_a.h, up_a[k - 4])
                    for c in range(8):
                        ins = t.matmul(
                            ps_u[k % 4][:],
                            wslb[:, c * 128:(c + 1) * 128],
                            fm(xnf)[:, c, tok0:tok0 + 512],
                            start=(c == 0), stop=(c == 7))
                    s_t.inc(ins)
                    up_t.append(s_t.n)

                @block.scalar
                def _(s, oc=oc, kind=kind, k=k):
                    s_t.wait(s)
                    if len(up_p) >= 1:
                        s.wait_ge(s_p.h, up_p[-1])
                    if kind == 0:
                        ins = s.activation(out=usb[0][:], in_=ps_u[k % 4][:],
                                           func=AF.Silu, bias=0.0,
                                           scale=scl[:, oc:oc + 1])
                    else:
                        ins = s.activation(out=usb[1][:], in_=ps_u[k % 4][:],
                                           func=AF.Copy, bias=0.0,
                                           scale=scl[:, 32 + oc:33 + oc])
                    s_a.inc(ins)
                    up_a.append(s_a.n)

            @block.vector
            def _(v, oc=oc):
                v.wait_ge(s_a.h, up_a[-1])
                ins = v.tensor_mul(hbuf[:, oc * 512:(oc + 1) * 512],
                                   usb[0][:], usb[1][:])
                s_p.inc(ins)
                up_p.append(s_p.n)

        for phase, (srcb, dst, wt, soff) in enumerate(
                [(hbuf[:], h1o, h1t, 64), (h1o, hbuf[:], h2t, 96)]):
            for oc in range(32):
                hk = len(hp_t)

                @block.gpsimd
                def _(g, oc=oc, wt=wt, hk=hk):
                    if hk >= 2:
                        g.wait_ge(s_t.h, hp_t[hk - 2])
                    swh[oc % 2].dma(g.dma_start(out=hsl[oc % 2][:, :], in_=wt[oc]))
                ld_sem, ld_cnt = swh[oc % 2].h, swh[oc % 2].n

                @block.tensor
                def _(t, oc=oc, phase=phase, srcb=srcb, hk=hk,
                      ld_sem=ld_sem, ld_cnt=ld_cnt):
                    t.wait_ge(ld_sem, ld_cnt)
                    if oc == 0:
                        if phase == 0:
                            s_p.wait(t)
                        else:
                            s_a.wait(t)
                    if hk >= 4:
                        t.wait_ge(s_a.h, hp_a[hk - 4])
                    for c in range(32):
                        ins = t.matmul(
                            ps_u[hk % 4][:],
                            hsl[oc % 2][:, c * 128:(c + 1) * 128],
                            srcb[:, c * 512:(c + 1) * 512],
                            start=(c == 0), stop=(c == 31))
                    s_t.inc(ins)
                    hp_t.append(s_t.n)

                @block.scalar
                def _(s, oc=oc, dst=dst, soff=soff, hk=hk):
                    s_t.wait(s)
                    ins = s.activation(
                        out=dst[:, oc * 512:(oc + 1) * 512],
                        in_=ps_u[hk % 4][:], func=AF.Gelu,
                        bias=0.0, scale=scl[:, soff + oc:soff + oc + 1])
                    s_a.inc(ins)
                    hp_a.append(s_a.n)

        @block.gpsimd
        def _(g):
            s_a.wait(g)
            for c0 in range(0, 32, 8):
                sw2.dma(g.dma_start(
                    out=w3sb[:, c0 * 1024:(c0 + 8) * 1024].rearrange(
                        "p (c o) -> p c o", c=8),
                    in_=w3t[c0:c0 + 8].rearrange("c p o -> p c o")))
        w3_loaded = sw2.n

        @block.vector
        def _(v):
            sw2.wait(v, w3_loaded)
            v.wait_ge(s_v.h, bc_done)
            for c in range(32):
                v.tensor_mul(w3sb[:, c * 1024:(c + 1) * 1024],
                             w3sb[:, c * 1024:(c + 1) * 1024], w3sb_bc[:])
                v.drain()
            s_v.inc(v.nop())
        w3_folded = s_v.n

        for tb in range(4):
            g_tb = grp * 4 + tb
            r0 = grp * 512 + tb * 128

            @block.sync
            def _(sync, r0=r0, g_tb=g_tb):
                if g_tb >= 2:
                    sync.wait_ge(s_p.h, ep_done[g_tb - 2])
                sld.dma(sync.dma_start(out=x1e[g_tb % 2][:],
                                       in_=x1[r0:r0 + 128, :]))

            @block.tensor
            def _(t, tb=tb, g_tb=g_tb, w3_folded=w3_folded):
                t.wait_ge(s_v.h, w3_folded)
                if tb == 0:
                    s_a.wait(t)
                if g_tb >= 1:
                    t.wait_ge(s_v.h, ep_ln[g_tb - 1])
                for c in range(32):
                    for nh in range(2):
                        ins = t.matmul(
                            ps_o[:, nh * 512:(nh + 1) * 512],
                            hbuf[:, c * 512 + tb * 128:c * 512 + tb * 128 + 128],
                            w3sb[:, c * 1024 + nh * 512:c * 1024 + nh * 512 + 512],
                            start=(c == 0), stop=(c == 31))
                s_t.inc(ins)

            @block.vector
            def _(v):
                s_t.wait(v)

            emit_ln_stats(block, ps_o[:], stat, sq[:], s_v, s_a)

            @block.vector
            def _(v, g_tb=g_tb):
                if g_tb >= 1:
                    v.wait_ge(s_p.h, ep_done[g_tb - 1])
                v.tensor_scalar(t1[:], ps_o[:], stat[:, 0:1], stat[:, 3:4],
                                ALU.add, ALU.mult)
                s_v.inc(v.drain())
                ep_ln.append(s_v.n)

            @block.vector
            def _(v, g_tb=g_tb):
                sld.wait(v)
                if g_tb >= 2:
                    v.wait_ge(sst.h, st_cnt[g_tb - 2])
                ins = v.tensor_add(out_sb[g_tb % 2][:], t1[:], x1e[g_tb % 2][:])
                s_p.inc(ins)
                ep_done.append(s_p.n)

            @block.sync
            def _(sync, r0=r0, g_tb=g_tb):
                sync.wait_ge(s_p.h, ep_done[g_tb])
                sst.dma(sync.dma_start(out=x2[r0:r0 + 128, :],
                                       in_=out_sb[g_tb % 2][:]))
                st_cnt.append(sst.n)

    @block.sync
    def _(sync):
        sst.wait(sync)



def emit_sig(nc, ctx, block, io):
    x2 = io["x2"]; encq = io["encq"]; ident = io["ident"]
    cc_in = io["cc_in"]; cc_out = io["cc_out"]
    halo_d = io["halo_d"]; sig_d = io["sig_d"]
    wsel = io["wsel"]; hsel = io["hsel"]

    xnf = ctx.sbuf("xnf", [128, 8 * TCH], BF16)
    eq_sb = ctx.sbuf("eq_sb", [128, 8 * 1024], BF16)
    qbuf = ctx.sbuf("qbuf", [128, TCH], BF16)
    xf = [ctx.sbuf(f"xf{i}", [128, D], F32) for i in range(2)]
    xn_t = ctx.sbuf("xn_t", [128, D], BF16)
    identb = ctx.sbuf("identb", [128, 128], BF16)
    stat = ctx.sbuf("stat", [128, 8], F32)
    sq = ctx.sbuf("sq", [128, D], F32)
    parts_sb = ctx.sbuf("parts_sb", [128, 16], F32)
    pg = ctx.sbuf("pg", [128, 8 * 16], F32)
    hg = ctx.sbuf("hg", [128, 8 * 1024], F32)
    halo_sb = ctx.sbuf("halo_sb", [128, D], F32)
    sig_col = ctx.sbuf("sig_col", [128, 8], F32)
    wsel_sb = ctx.sbuf("wsel_sb", [128, 16], F32)
    hsel_sb = ctx.sbuf("hsel_sb", [128, 8], F32)
    ps_t = ctx.psum("ps_t", [128, 1024], BF16)
    ps_a = [ctx.psum(f"ps_a{i}", [128, 512], F32) for i in range(2)]

    sld = ctx.sem("sld")
    swl = ctx.sem("swl")
    s_t = ctx.sem("s_t")
    s_a = ctx.sem("s_a")
    s_v = ctx.sem("s_v")
    ssc = ctx.sem("ssc")
    scc = ctx.sem("scc")
    swl_base = swl.n

    @block.gpsimd
    def _(g):
        swl.dma(g.dma_start(out=identb[:], in_=ident[:]))
        swl.dma(g.dma_start(out=wsel_sb[:], in_=wsel[:]))
        swl.dma(g.dma_start(out=hsel_sb[:], in_=hsel[:]))
        for n in range(8):
            swl.dma(g.dma_start(
                out=eq_sb[:, :].rearrange("p (c u) -> p c u", c=8)[:, :, n * 128:(n + 1) * 128],
                in_=encq[n].rearrange("(c p) u -> p c u", p=128)))
    consts_w = swl.n

    ln_done = []
    ev_done = []
    for i in range(NBLK):
        @block.sync
        def _(sync, i=i):
            if i >= 2:
                sync.wait_ge(s_v.h, ln_done[i - 2])
            sld.dma(sync.dma_start(out=xf[i % 2][:], in_=x2[i * 128:(i + 1) * 128, :]))

        @block.vector
        def _(v, i=i):
            sld.wait(v)

        emit_ln_stats(block, xf[i % 2][:], stat, sq[:], s_v, s_a)

        @block.vector
        def _(v, i=i):
            if i >= 1:
                v.wait_ge(s_v.h, ev_done[i - 1])
            v.tensor_scalar(xn_t[:], xf[i % 2][:], stat[:, 0:1], stat[:, 3:4],
                            ALU.add, ALU.mult)
            s_v.inc(v.drain())
            ln_done.append(s_v.n)

        @block.tensor
        def _(t, i=i):
            swl.wait(t, swl_base + 16)
            s_v.wait(t)
            for c in range(8):
                ins = t.transpose(ps_t[:, c * 128:(c + 1) * 128],
                                  xn_t[:, c * 128:(c + 1) * 128], identb[:])
            s_t.inc(ins)

        @block.vector
        def _(v, i=i):
            s_t.wait(v)
            v.tensor_copy(fm(xnf)[:, :, i * 128:(i + 1) * 128], fm(ps_t)[:, :, :])
            s_v.inc(v.drain())
            ev_done.append(s_v.n)

    @block.sync
    def _(sync):
        sync.wait_ge(s_v.h, ln_done[15])
        ssc.dma(sync.dma_start(out=cc_in[:, 16:1040], in_=xf[1][:]))

    qp_t = []
    qp_a = []
    qr_done = []
    for n in range(8):
        for g in range(4):
            kk = len(qp_t)

            @block.tensor
            def _(t, n=n, g=g, kk=kk):
                swl.wait(t, consts_w)
                if kk == 0:
                    t.wait_ge(s_v.h, ev_done[-1])
                if kk >= 2:
                    t.wait_ge(s_a.h, qp_a[kk - 2])
                for c in range(8):
                    ins = t.matmul(ps_a[kk % 2][:],
                                   eq_sb[:, c * 1024 + n * 128:c * 1024 + (n + 1) * 128],
                                   fm(xnf)[:, c, g * 512:(g + 1) * 512],
                                   start=(c == 0), stop=(c == 7))
                s_t.inc(ins)
                qp_t.append(s_t.n)

            @block.scalar
            def _(s, n=n, g=g, kk=kk):
                s_t.wait(s)
                if g == 0 and qr_done:
                    s.wait_ge(s_v.h, qr_done[-1])
                ins = s.activation(out=qbuf[:, g * 512:(g + 1) * 512],
                                   in_=ps_a[kk % 2][:], func=AF.Relu,
                                   bias=0.0, scale=1.0)
                s_a.inc(ins)
                qp_a.append(s_a.n)

        @block.vector
        def _(v, n=n):
            s_a.wait(v)
            v.reduce_sum(parts_sb[:, n:n + 1], qbuf[:, :], axis=AX.X)
            v.drain()
            v.reduce_sum(parts_sb[:, 8 + n:9 + n],
                         qbuf[:, :].rearrange("p (a b) -> p b a", b=8)[:, 0:1, :],
                         axis=AX.X)
            s_v.inc(v.drain())
            qr_done.append(s_v.n)

    @block.sync
    def _(sync):
        sync.wait_ge(s_v.h, qr_done[-1])
        ssc.dma(sync.dma_start(out=cc_in[:, 0:16], in_=parts_sb[:]))

    @block.gpsimd
    def _(g):
        g.wait_ge(ssc.h, ssc.n)
        ins = g.collective_compute(
            "AllGather", ALU.bypass,
            replica_groups=[list(range(N_CORES))],
            ins=[io["cc_in_h"].ap().opt()],
            outs=[io["cc_out_h"].ap().opt()])
        scc.inc(ins, 1)

    @block.gpsimd
    def _(g):
        g.wait_ge(scc.h, 1)
        swl.dma(g.dma_start(
            out=pg[:, :].rearrange("p (r j) -> p r j", r=8),
            in_=cc_out.rearrange("(r p) c -> p r c", p=128)[:, :, 0:16]))
        swl.dma(g.dma_start(
            out=hg[:, :].rearrange("p (r j) -> p r j", r=8),
            in_=cc_out.rearrange("(r p) c -> p r c", p=128)[:, :, 16:1040]))
    cc_loaded = swl.n

    @block.vector
    def _(v):
        v.wait_ge(swl.h, cc_loaded)
        v.memset(halo_sb[:], 0.0)
        v.drain()
        for r in range(8):
            v.scalar_tensor_tensor(
                halo_sb[:], hg[:, r * 1024:(r + 1) * 1024],
                hsel_sb[:, r:r + 1], halo_sb[:],
                op0=ALU.mult, op1=ALU.add)
            v.drain()
        v.memset(sig_col[:], 0.0)
        v.drain()
        for r in range(8):
            for k in range(2):
                j0 = r * 16 + k * 8
                v.scalar_tensor_tensor(
                    sig_col[:], pg[:, j0:j0 + 8],
                    wsel_sb[:, r * 2 + k:r * 2 + k + 1], sig_col[:],
                    op0=ALU.mult, op1=ALU.add)
                v.drain()
        s_v.inc(v.nop())
    comb_done = s_v.n

    @block.sync
    def _(sync):
        sync.wait_ge(s_v.h, comb_done)
        ssc.dma(sync.dma_start(out=halo_d[:], in_=halo_sb[:]))
        ssc.dma(sync.dma_start(out=sig_d[:], in_=sig_col[:]))
        ssc.wait(sync)



def emit_l1attn(nc, ctx, block, io):
    x2 = io["x2"]; x3 = io["x3"]
    encq = io["encq"]; encv = io["encv"]; dec = io["dec"]; heb = io["heb"]
    mw1 = io["mw1"]; mw2 = io["mw2"]; mb1 = io["mb1"]; mb2 = io["mb2"]
    base = io["base"]; band = io["band"]; ident = io["ident"]
    halo_d = io["halo_d"]; sig_d = io["sig_d"]

    TH = TCH + 128
    xnf = ctx.sbuf("xnf", [128, 8 * TH], BF16)
    zf = ctx.sbuf("zf", [128, 8 * TCH], BF16)
    onesf = ctx.sbuf("onesf", [128, 128], F32)
    onesc = ctx.sbuf("onesc", [128, 1], F32)
    identb = ctx.sbuf("identb", [128, 128], BF16)
    stat = ctx.sbuf("stat", [128, 8], F32)
    sq = ctx.sbuf("sq", [128, D], F32)
    xf = [ctx.sbuf(f"xf{i}", [128, D], F32) for i in range(2)]
    xn_t = ctx.sbuf("xn_t", [128, D], BF16)
    t1 = ctx.sbuf("t1", [128, D], F32)
    out_sb = [ctx.sbuf(f"out_sb{i}", [128, D], F32) for i in range(2)]
    x2e = [ctx.sbuf(f"x2e{i}", [128, D], F32) for i in range(2)]
    es_S = contextlib.ExitStack()
    ctxS = ctx.sub(es_S, "S")
    eq_sb = ctxS.sbuf("eq_sb", [128, 8 * 1024], BF16)
    ev_sb = ctxS.sbuf("ev_sb", [128, 8 * 1024], BF16)
    heb_sb = ctxS.sbuf("heb_sb", [128, 8 * 128], BF16)
    qb_buf = ctxS.sbuf("qb_buf", [128, TH], BF16)
    qt_buf = ctxS.sbuf("qt_buf", [128, TH], BF16)
    v_buf = ctxS.sbuf("v_buf", [128, TH], BF16)
    sq512 = ctxS.sbuf("sq512", [128, 512], F32)
    pt_sb = ctxS.sbuf("pt_sb", [128, 256], BF16)
    band_sb = ctxS.sbuf("band_sb", [128, 256], F32)
    rows_t = ctxS.sbuf("rows_t", [128, 6144], F32)
    sig_col = ctxS.sbuf("sig_col", [128, 16], F32)
    sig_colb = ctxS.sbuf("sig_colb", [128, 16], BF16)
    sqm_pp = ctxS.sbuf("sqm_pp", [128, 8], F32)
    wtp = ctxS.sbuf("wtp", [128, 16], F32)
    denb = ctxS.sbuf("denb", [128, 16], F32)
    rden = ctxS.sbuf("rden", [128, 16], F32)
    hb_sb = ctxS.sbuf("hb_sb", [128, 128], F32)
    t2_sb = ctxS.sbuf("t2_sb", [128, 128], F32)
    t3_sb = ctxS.sbuf("t3_sb", [128, 128], F32)

    ps_t = ctx.psum("ps_t", [128, 1024], BF16)
    ps_a = ctx.psum("ps_a", [128, 512], F32)
    ps_b = ctx.psum("ps_b", [128, 512], F32)
    ps_tp = ctx.psum("ps_tp", [128, 256], BF16)
    ps_y = ctx.psum("ps_y", [128, 128], F32)
    ps_ctx = ctx.psum("ps_ctx", [128, 128], F32)
    ps_o = ctx.psum("ps_o", [128, 1024], F32)

    sld = ctx.sem("sld")
    swl = ctx.sem("swl")
    s_t = ctx.sem("s_t")
    s_a = ctx.sem("s_a")
    s_v = ctx.sem("s_v")
    s_p = ctx.sem("s_p")
    sst = ctx.sem("sst")
    scc = ctx.sem("scc")
    swl_base = swl.n

    es_mw = contextlib.ExitStack()
    ctxM = ctx.sub(es_mw, "M")
    mw1_sb = ctxM.sbuf("mw1_sb", [128, 8 * 512], BF16)
    mw2_sb = ctxM.sbuf("mw2_sb", [128, 4 * 1024], BF16)

    @block.gpsimd
    def _(g):
        swl.dma(g.dma_start(out=identb[:], in_=ident[:]))
        swl.dma(g.dma_start(out=band_sb[:], in_=band[:]))
        swl.dma(g.dma_start(out=rows_t[64:65, 2560:3072], in_=mb1[:]))
        swl.dma(g.dma_start(out=rows_t[64:65, 3072:4096], in_=mb2[:]))
        swl.dma(g.dma_start(out=rows_t[64:65, 4096:5120], in_=base[:]))
        swl.dma(g.dma_start(out=sig_col[:, 0:8], in_=sig_d[:]))
        swl.dma(g.dma_start(out=sig_colb[:, 0:8], in_=sig_d[:]))
        for n in range(8):
            swl.dma(g.dma_start(
                out=eq_sb[:, :].rearrange("p (c u) -> p c u", c=8)[:, :, n * 128:(n + 1) * 128],
                in_=encq[n].rearrange("(c p) u -> p c u", p=128)))
            swl.dma(g.dma_start(
                out=ev_sb[:, :].rearrange("p (c u) -> p c u", c=8)[:, :, n * 128:(n + 1) * 128],
                in_=encv[n].rearrange("(c p) u -> p c u", p=128)))
            swl.dma(g.dma_start(out=heb_sb[:, n * 128:(n + 1) * 128], in_=heb[n]))
            swl.dma(g.dma_start(out=fm(mw1_sb)[:, n, :], in_=mw1[n]))
        for c in range(4):
            swl.dma(g.dma_start(out=fm(mw2_sb, c=4)[:, c, :], in_=mw2[c]))
    consts_w = swl.n

    @block.vector
    def _(v):
        v.memset(onesf[:], 1.0)
        v.drain()
        v.memset(onesc[:], 1.0)
        s_v.inc(v.drain())

    ln_done = []
    ev_done = []
    for i in range(17):
        @block.sync
        def _(sync, i=i):
            if i >= 2:
                sync.wait_ge(s_v.h, ln_done[i - 2])
            if i == 0:
                sld.dma(sync.dma_start(out=xf[0][:], in_=halo_d[:]))
            else:
                sld.dma(sync.dma_start(out=xf[i % 2][:],
                                       in_=x2[(i - 1) * 128:i * 128, :]))

        @block.vector
        def _(v):
            sld.wait(v)

        emit_ln_stats(block, xf[i % 2][:], stat, sq[:], s_v, s_a)

        @block.vector
        def _(v, i=i):
            if i >= 1:
                v.wait_ge(s_v.h, ev_done[i - 1])
            v.tensor_scalar(xn_t[:], xf[i % 2][:], stat[:, 0:1], stat[:, 3:4],
                            ALU.add, ALU.mult)
            s_v.inc(v.drain())
            ln_done.append(s_v.n)

        @block.tensor
        def _(t, i=i):
            swl.wait(t, swl_base + 16)
            s_v.wait(t)
            for c in range(8):
                ins = t.transpose(ps_t[:, c * 128:(c + 1) * 128],
                                  xn_t[:, c * 128:(c + 1) * 128], identb[:])
            s_t.inc(ins)

        @block.vector
        def _(v, i=i):
            s_t.wait(v)
            v.tensor_copy(fm(xnf)[:, :, i * 128:(i + 1) * 128], fm(ps_t)[:, :, :])
            s_v.inc(v.drain())
            ev_done.append(s_v.n)

    @block.tensor
    def _(t):
        swl.wait(t, consts_w)
        for c in range(8):
            ins = t.matmul(ps_b[0:1, :], sig_colb[:, c:c + 1],
                           mw1_sb[:, c * 512:(c + 1) * 512],
                           start=(c == 0), stop=(c == 7))
        s_t.inc(ins)

    @block.vector
    def _(v):
        s_t.wait(v)
        v.tensor_add(rows_t[64:65, 1024:1536], ps_b[0:1, :], rows_t[64:65, 2560:3072])
        s_v.inc(v.drain())

    @block.scalar
    def _(s):
        s_v.wait(s)
        s.activation(out=rows_t[64:65, 1024:1536], in_=rows_t[64:65, 1024:1536],
                     func=AF.Gelu, bias=0.0, scale=1.0)
        s_a.inc(s.drain())

    @block.tensor
    def _(t):
        s_a.wait(t)
        for c in range(4):
            ins = t.transpose(ps_y[:, 8 + c:9 + c],
                              rows_t[64:65, 1024 + c * 128:1024 + (c + 1) * 128],
                              onesf[64:65, 64:65])
        s_t.inc(ins)

    @block.vector
    def _(v):
        s_t.wait(v)
        v.tensor_copy(sig_col[:, 8:12], ps_y[:, 8:12])
        v.drain()
        v.tensor_copy(sig_colb[:, 8:12], sig_col[:, 8:12])
        s_v.inc(v.drain())

    @block.tensor
    def _(t):
        s_v.wait(t)
        for nh in range(2):
            for c in range(4):
                ins = t.matmul(ps_o[0:1, nh * 512:(nh + 1) * 512],
                               sig_colb[:, 8 + c:9 + c],
                               mw2_sb[:, c * 1024 + nh * 512:c * 1024 + nh * 512 + 512],
                               start=(c == 0), stop=(c == 3))
        s_t.inc(ins)

    @block.vector
    def _(v):
        s_t.wait(v)
        v.tensor_add(rows_t[64:65, 1536:2560], ps_o[0:1, :], rows_t[64:65, 3072:4096])
        v.drain()
        v.scalar_tensor_tensor(rows_t[64:65, 1536:2560], rows_t[64:65, 1536:2560],
                               0.1, rows_t[64:65, 4096:5120], op0=ALU.mult, op1=ALU.add)
        s_v.inc(v.drain())

    @block.scalar
    def _(s):
        s_v.wait(s)
        s.activation(out=rows_t[64:65, 5120:6144], in_=rows_t[64:65, 1536:2560],
                     func=AF.Abs, bias=0.0, scale=1.0)
        s.drain()
        s.activation(out=rows_t[64:65, 5120:6144], in_=rows_t[64:65, 5120:6144],
                     func=AF.Exp, bias=0.0, scale=-1.0)
        s.drain()
        s.activation(out=rows_t[64:65, 5120:6144], in_=rows_t[64:65, 5120:6144],
                     func=AF.Ln, bias=onesf[64:65, 64:65], scale=1.0)
        s_a.inc(s.drain())

    @block.vector
    def _(v):
        s_a.wait(v)
        v.tensor_scalar_max(rows_t[64:65, 1536:2560], rows_t[64:65, 1536:2560], 0.0)
        v.drain()
        v.tensor_add(rows_t[64:65, 1536:2560], rows_t[64:65, 1536:2560],
                     rows_t[64:65, 5120:6144])
        v.drain()
        v.tensor_scalar_add(rows_t[64:65, 1536:2560], rows_t[64:65, 1536:2560], 1e-6)
        s_v.inc(v.drain())

    @block.scalar
    def _(s):
        s_v.wait(s)
        s.activation(out=rows_t[64:65, 1536:2560], in_=rows_t[64:65, 1536:2560],
                     func=AF.Sqrt, bias=0.0, scale=1.0)
        s_a.inc(s.drain())

    @block.tensor
    def _(t):
        s_a.wait(t)
        for c in range(8):
            ins = t.transpose(ps_y[:, c:c + 1],
                              rows_t[64:65, 1536 + c * 128:1536 + (c + 1) * 128],
                              onesf[64:65, 64:65])
        s_t.inc(ins)

    @block.vector
    def _(v):
        s_t.wait(v)
        v.tensor_copy(sqm_pp[:, 0:8], ps_y[:, 0:8])
        s_v.inc(v.drain())
    sqm_done = s_v.n

    es_mw.close()
    dpre = ctxS.sbuf("dpre", [128, 16 * 256], F32)
    pbuf = ctxS.sbuf("pbuf", [128, 16 * 256], BF16)
    gbounds = [(0, 512), (512, 512), (1024, 512), (1536, 512), (2048, 128)]
    nq_t = []
    nq_a = []
    n_zf = []

    def ones_row(n, src_sbuf, dst_p):
        for gi, (t0_, tw) in enumerate(gbounds):
            kk = len(nq_t)

            @block.scalar
            def _(s, t0_=t0_, tw=tw, kk=kk):
                if nq_t:
                    s.wait_ge(s_t.h, nq_t[-1])
                ins = s.activation(out=sq512[:, 0:tw], in_=src_sbuf[:, t0_:t0_ + tw],
                                   func=AF.Square, bias=0.0, scale=1.0)
                s_a.inc(ins)
                sq_a = s_a.n

                @block.tensor
                def _(t, kk=kk, tw=tw, sq_a=sq_a):
                    t.wait_ge(s_a.h, sq_a)
                    if kk >= 2:
                        t.wait_ge(s_a.h, nq_a[kk - 2])
                    psx = [ps_a, ps_b][kk % 2]
                    ins = t.matmul(psx[0:1, 0:tw], onesc[:], sq512[:, 0:tw],
                                   start=True, stop=True)
                    s_t.inc(ins)
                    nq_t.append(s_t.n)

            @block.scalar
            def _(s, t0_=t0_, tw=tw, kk=kk, dst_p=dst_p):
                s.wait_ge(s_t.h, nq_t[kk])
                psx = [ps_a, ps_b][kk % 2]
                ins = s.activation(out=rows_t[dst_p:dst_p + 1, t0_:t0_ + tw],
                                   in_=psx[0:1, 0:tw], func=AF.Copy,
                                   bias=0.0, scale=1.0)
                s_a.inc(ins)
                nq_a.append(s_a.n)

    for n in range(8):
        for gi, (t0_, tw) in enumerate(gbounds):
            kk = len(nq_t)

            @block.tensor
            def _(t, n=n, t0_=t0_, tw=tw, kk=kk):
                if kk == 0:
                    t.wait_ge(s_v.h, ev_done[-1])
                    swl.wait(t, consts_w)
                if kk >= 2:
                    t.wait_ge(s_a.h, nq_a[kk - 2])
                psx = [ps_a, ps_b][kk % 2]
                for c in range(8):
                    ins = t.matmul(psx[:, 0:tw],
                                   eq_sb[:, c * 1024 + n * 128:c * 1024 + (n + 1) * 128],
                                   fm(xnf)[:, c, t0_:t0_ + tw],
                                   start=(c == 0), stop=(c == 7))
                s_t.inc(ins)
                nq_t.append(s_t.n)

            @block.scalar
            def _(s, n=n, gi=gi, t0_=t0_, tw=tw, kk=kk):
                s.wait_ge(s_t.h, nq_t[kk])
                if gi == 0 and n_zf:
                    s.wait_ge(s_v.h, n_zf[-1])
                psx = [ps_a, ps_b][kk % 2]
                ins = s.activation(out=qb_buf[:, t0_:t0_ + tw], in_=psx[:, 0:tw],
                                   func=AF.Relu, bias=0.0, scale=1.0)
                s_a.inc(ins)
                nq_a.append(s_a.n)

        for tb in range(17):
            kk = len(nq_t)

            @block.tensor
            def _(t, n=n, tb=tb, kk=kk):
                if kk >= 2:
                    t.wait_ge(s_a.h, nq_a[kk - 2])
                psx = [ps_a, ps_b][kk % 2]
                for c in range(8):
                    ins = t.matmul(psx[:, 0:128],
                                   fm(xnf)[:, c, tb * 128:(tb + 1) * 128],
                                   ev_sb[:, c * 1024 + n * 128:c * 1024 + (n + 1) * 128],
                                   start=(c == 0), stop=(c == 7))
                s_t.inc(ins)
                nq_t.append(s_t.n)

            @block.scalar
            def _(s, n=n, tb=tb, kk=kk):
                s.wait_ge(s_t.h, nq_t[kk])
                psx = [ps_a, ps_b][kk % 2]
                ins = s.activation(out=v_buf[:, tb * 128:(tb + 1) * 128],
                                   in_=psx[:, 0:128],
                                   func=AF.Relu, bias=0.0, scale=1.0)
                s_a.inc(ins)
                nq_a.append(s_a.n)

        @block.vector
        def _(v, n=n):
            s_a.wait(v)
            v.wait_ge(s_v.h, sqm_done)
            v.tensor_scalar_mul(qt_buf[:], qb_buf[:], sqm_pp[:, n:n + 1])
            s_v.inc(v.drain())

        @block.scalar
        def _(s, n=n):
            s_v.wait(s)
        ones_row(n, qt_buf[:, :], 0)
        ones_row(n, qb_buf[:, :], 32)

        @block.scalar
        def _(s):
            s.wait_ge(s_a.h, nq_a[-1])
            s.activation(out=rows_t[32:33, 0:TH],
                         in_=rows_t[32:33, 0:TH], func=AF.Sqrt,
                         bias=0.0, scale=1.0)
            s_a.inc(s.drain())

        @block.vector
        def _(v):
            s_a.wait(v)
            v.tensor_scalar_max(rows_t[32:33, 0:TH],
                                rows_t[32:33, 0:TH], 1e-12)
            v.drain()
            v.reciprocal(rows_t[32:33, 0:TH], rows_t[32:33, 0:TH])
            s_v.inc(v.drain())
        rnorm_done = s_v.n

        @block.tensor
        def _(t, n=n):
            s_v.wait(t)
            for qb in range(16):
                ins = t.transpose(ps_y[:, qb:qb + 1],
                                  rows_t[0:1, (qb + 1) * 128:(qb + 2) * 128],
                                  onesf[0:1, 0:1])
            s_t.inc(ins)

        @block.vector
        def _(v):
            s_t.wait(v)
            v.tensor_scalar_add(wtp[:, 0:16], ps_y[:, 0:16], 1e-8)
            s_v.inc(v.drain())
        wtp_done = s_v.n

        al_t = []
        al_v = []
        for qb in range(16):
            @block.tensor
            def _(t, n=n, qb=qb):
                t.wait_ge(s_v.h, wtp_done)
                if qb >= 2:
                    t.wait_ge(s_v.h, al_v[qb - 2])
                psw = [ps_a, ps_b][qb % 2]
                t.matmul(psw[0:128, 0:256], onesf[0:1, 0:128],
                         rows_t[0:1, qb * 128:qb * 128 + 256],
                         start=True, stop=True)
                ins = t.matmul(psw[0:128, 256:512],
                               qt_buf[:, (qb + 1) * 128:(qb + 2) * 128],
                               qt_buf[:, qb * 128:qb * 128 + 256],
                               start=True, stop=True)
                s_t.inc(ins)
                al_t.append(s_t.n)

            @block.vector
            def _(v, n=n, qb=qb):
                v.wait_ge(s_t.h, al_t[qb])
                psw = [ps_a, ps_b][qb % 2]
                v.tensor_add(dpre[:, qb * 256:(qb + 1) * 256],
                             psw[0:128, 0:256], band_sb[:])
                v.drain()
                v.scalar_tensor_tensor(dpre[:, qb * 256:(qb + 1) * 256],
                                       psw[0:128, 256:512], -2.0,
                                       dpre[:, qb * 256:(qb + 1) * 256],
                                       op0=ALU.mult, op1=ALU.add)
                v.drain()
                v.tensor_scalar(dpre[:, qb * 256:(qb + 1) * 256],
                                dpre[:, qb * 256:(qb + 1) * 256],
                                wtp[:, qb:qb + 1], 1e-8, ALU.add, ALU.max)
                s_v.inc(v.drain())
                al_v.append(s_v.n)

        @block.scalar
        def _(s, n=n):
            s_v.wait(s)
            for qb in range(16):
                ins = s.activation(out=dpre[:, qb * 256:(qb + 1) * 256],
                                   in_=dpre[:, qb * 256:(qb + 1) * 256],
                                   func=AF.Sqrt, bias=0.0, scale=1.0)
            s.drain()
            for qb in range(16):
                ins = s.activation(out=pbuf[:, qb * 256:(qb + 1) * 256],
                                   in_=dpre[:, qb * 256:(qb + 1) * 256],
                                   func=AF.Exp, bias=0.0, scale=-1.0,
                                   accum_out=denb[:, qb:qb + 1])
            s_a.inc(s.drain())

        @block.vector
        def _(v, n=n):
            s_a.wait(v)
            v.tensor_scalar_add(rden[:, 0:16], denb[:, 0:16], 1e-8)
            v.drain()
            v.reciprocal(rden[:, 0:16], rden[:, 0:16])
            s_v.inc(v.drain())
        rden_done = s_v.n

        for qb in range(16):
            @block.vector
            def _(v, n=n, qb=qb):
                v.wait_ge(s_v.h, rden_done)
                v.tensor_scalar_mul(pbuf[:, qb * 256:(qb + 1) * 256],
                                    pbuf[:, qb * 256:(qb + 1) * 256],
                                    rden[:, qb:qb + 1])
                s_v.inc(v.drain())
            pn_done = s_v.n

            @block.tensor
            def _(t, n=n, qb=qb):
                t.wait_ge(s_v.h, pn_done)
                for j in range(2):
                    ins = t.transpose(ps_tp[:, j * 128:(j + 1) * 128],
                                      pbuf[:, qb * 256 + j * 128:qb * 256 + (j + 1) * 128],
                                      identb[:])
                s_t.inc(ins)
            tp_done2 = s_t.n

            @block.vector
            def _(v, n=n, qb=qb):
                v.wait_ge(s_t.h, tp_done2)
                v.tensor_copy(pt_sb[:], ps_tp[:])
                s_v.inc(v.drain())
            pt_done = s_v.n

            @block.tensor
            def _(t, n=n, qb=qb):
                t.wait_ge(s_v.h, pt_done)
                for j in range(2):
                    t.matmul(ps_ctx[:], v_buf[:, (qb + j) * 128:(qb + j + 1) * 128],
                             pt_sb[:, j * 128:(j + 1) * 128],
                             start=(j == 0), stop=(j == 1))
                ins = t.matmul(ps_y[:], heb_sb[:, n * 128:(n + 1) * 128],
                               qb_buf[:, (qb + 1) * 128:(qb + 2) * 128],
                               start=True, stop=True)
                s_t.inc(ins)
            hc_done = s_t.n

            @block.scalar
            def _(s, n=n, qb=qb):
                s.wait_ge(s_t.h, hc_done)
                s.activation(out=hb_sb[:], in_=ps_y[:], func=AF.Copy,
                             bias=0.0, scale=1.0)
                s_a.inc(s.drain())
            hb_done = s_a.n

            @block.tensor
            def _(t, n=n, qb=qb):
                t.wait_ge(s_a.h, hb_done)
                t.wait_ge(s_v.h, rnorm_done)
                ins = t.matmul(ps_y[:], onesf[32:33, 0:128],
                               rows_t[32:33, (qb + 1) * 128:(qb + 2) * 128],
                               start=True, stop=True)
                s_t.inc(ins)
            rb_done = s_t.n

            @block.vector
            def _(v, n=n, qb=qb):
                v.wait_ge(s_t.h, rb_done)
                v.tensor_mul(t2_sb[:], hb_sb[:], ps_y[:])
                v.drain()
                v.tensor_add(t3_sb[:], t2_sb[:], ps_ctx[:])
                v.drain()
                v.tensor_mul(zf[:, n * TCH + qb * 128:n * TCH + (qb + 1) * 128],
                             t3_sb[:], qb_buf[:, (qb + 1) * 128:(qb + 2) * 128])
                s_v.inc(v.drain())
                n_zf.append(s_v.n)

    es_S.close()
    dec_sb = ctx.sbuf("dec_sb", [128, 8 * 1024], BF16)

    @block.gpsimd
    def _(g):
        g.wait_ge(s_v.h, n_zf[-1])
        for c in range(8):
            swl.dma(g.dma_start(out=fm(dec_sb)[:, c, :], in_=dec[c]))
    dec_loaded = swl.n

    ep_done = []
    ep_ln = []
    st_cnt = []
    for tb in range(NBLK):
        @block.sync
        def _(sync, tb=tb):
            if tb >= 2:
                sync.wait_ge(s_p.h, ep_done[tb - 2])
            sld.dma(sync.dma_start(out=x2e[tb % 2][:],
                                   in_=x2[tb * 128:(tb + 1) * 128, :]))

        @block.tensor
        def _(t, tb=tb):
            if tb == 0:
                t.wait_ge(s_v.h, n_zf[-1])
                t.wait_ge(swl.h, dec_loaded)
            if tb >= 1:
                t.wait_ge(s_v.h, ep_ln[tb - 1])
            for c in range(8):
                for nh in range(2):
                    ins = t.matmul(
                        ps_o[:, nh * 512:(nh + 1) * 512],
                        zf[:, c * TCH + tb * 128:c * TCH + (tb + 1) * 128],
                        dec_sb[:, c * 1024 + nh * 512:c * 1024 + nh * 512 + 512],
                        start=(c == 0), stop=(c == 7))
            s_t.inc(ins)

        @block.vector
        def _(v):
            s_t.wait(v)

        emit_ln_stats(block, ps_o[:], stat, sq[:], s_v, s_a)

        @block.vector
        def _(v, tb=tb):
            if tb >= 1:
                v.wait_ge(s_p.h, ep_done[tb - 1])
            v.tensor_scalar(t1[:], ps_o[:], stat[:, 0:1], stat[:, 3:4],
                            ALU.add, ALU.mult)
            s_v.inc(v.drain())
            ep_ln.append(s_v.n)

        @block.vector
        def _(v, tb=tb):
            sld.wait(v)
            if tb >= 2:
                v.wait_ge(sst.h, st_cnt[tb - 2])
            ins = v.tensor_add(out_sb[tb % 2][:], t1[:], x2e[tb % 2][:])
            s_p.inc(ins)
            ep_done.append(s_p.n)

        @block.sync
        def _(sync, tb=tb):
            sync.wait_ge(s_p.h, ep_done[tb])
            sst.dma(sync.dma_start(out=x3[tb * 128:(tb + 1) * 128, :],
                                   in_=out_sb[tb % 2][:]))
            st_cnt.append(sst.n)

    @block.sync
    def _(sync):
        sst.wait(sync)



def emit_l1ffn(nc, ctx, block, io):
    x3 = io["x3"]; x4 = io["x4"]; ident = io["ident"]
    wrt = io["wrt"]; br = io["br"]; w1e = io["w1e"]; b1l = io["b1l"]
    w2e = io["w2e"]; b2i = io["b2i"]; gam = io["gam"]; bet = io["bet"]

    xnf = ctx.sbuf("xnf", [128, 8 * TCH], BF16)
    ubuf = ctx.sbuf("ubuf", [128, 32 * 512], BF16)
    w2sb = ctx.sbuf("w2sb", [128, 32 * 1024], BF16)
    acc = ctx.sbuf("acc", [128, 4 * 1024], F32)
    stats_all = ctx.sbuf("stats_all", [128, 32], F32)
    rw_all = ctx.sbuf("rw_all", [128, 8 * 16], F32)
    rwT_all = ctx.sbuf("rwT_all", [8, 16 * 128], BF16)
    rwb = ctx.sbuf("rwb", [128, 8], BF16)
    wr_sb = ctx.sbuf("wr_sb", [128, 8 * 8], BF16)
    br_sb = ctx.sbuf("br_sb", [1, 8], BF16)
    b1_sb = ctx.sbuf("b1_sb", [128, 8 * 32], F32)
    b2_sb = ctx.sbuf("b2_sb", [8, 1024], BF16)
    gb_bc = ctx.sbuf("gb_bc", [128, 2048], F32)
    g_row = ctx.sbuf("g_row", [1, 1024], BF16)
    b_row = ctx.sbuf("b_row", [1, 1024], BF16)
    onesb = ctx.sbuf("onesb", [1, 128], BF16)
    wsl = [ctx.sbuf(f"wsl{i}", [128, 8 * 128], BF16) for i in range(2)]
    logit_sb = ctx.sbuf("logit_sb", [128, 8], F32)
    top_sb = ctx.sbuf("top_sb", [128, 8], F32)
    vv_sb = ctx.sbuf("vv_sb", [128, 4], F32)
    identb = ctx.sbuf("identb", [128, 128], BF16)
    stat = ctx.sbuf("stat", [128, 8], F32)
    sq = ctx.sbuf("sq", [128, D], F32)
    xf = [ctx.sbuf(f"xf{i}", [128, D], F32) for i in range(2)]
    xn_t = ctx.sbuf("xn_t", [128, D], BF16)
    t1 = ctx.sbuf("t1", [128, D], F32)
    t2 = ctx.sbuf("t2", [128, D], F32)
    inner = ctx.sbuf("inner", [128, D], F32)
    out_sb = [ctx.sbuf(f"out_sb{i}", [128, D], F32) for i in range(2)]
    x3e = xf

    ps_t = ctx.psum("ps_t", [128, 1024], BF16)
    ps_u = [ctx.psum(f"ps_u{i}", [128, 512], F32) for i in range(2)]
    ps_o = ctx.psum("ps_o", [128, 1024], F32)
    ps_r = ctx.psum("ps_r", [128, 8], F32)
    ps_rt = ctx.psum("ps_rt", [8, 128], BF16)

    sld = ctx.sem("sld")
    swl = ctx.sem("swl")
    sw2 = ctx.sem("sw2")
    swu = [ctx.sem(f"swu{i}") for i in range(2)]
    s_t = ctx.sem("s_t")
    s_a = ctx.sem("s_a")
    s_v = ctx.sem("s_v")
    s_p = ctx.sem("s_p")
    sst = ctx.sem("sst")
    swl_base = swl.n

    @block.gpsimd
    def _(g):
        swl.dma(g.dma_start(out=identb[:], in_=ident[:]))
        for c in range(8):
            swl.dma(g.dma_start(out=wr_sb[:, c * 8:(c + 1) * 8], in_=wrt[c]))
        swl.dma(g.dma_start(out=br_sb[:], in_=br[:]))
        swl.dma(g.dma_start(out=b1_sb[:], in_=b1l[:]))
        swl.dma(g.dma_start(out=b2_sb[:], in_=b2i[:]))
        swl.dma(g.dma_start(out=g_row[:], in_=gam[:]))
        swl.dma(g.dma_start(out=b_row[:], in_=bet[:]))
    consts_w = swl.n

    @block.vector
    def _(v):
        v.memset(onesb[:], 1.0)
        s_v.inc(v.drain())
    ones_done = s_v.n

    @block.tensor
    def _(t):
        swl.wait(t, consts_w)
        t.wait_ge(s_v.h, ones_done)
        for nh in range(2):
            ins = t.matmul(ps_o[:, nh * 512:(nh + 1) * 512], onesb[:],
                           g_row[0:1, nh * 512:(nh + 1) * 512],
                           start=True, stop=True)
        s_t.inc(ins)

    @block.vector
    def _(v):
        s_t.wait(v)
        v.tensor_copy(gb_bc[:, 0:1024], ps_o[:])
        s_v.inc(v.drain())

    @block.tensor
    def _(t):
        s_v.wait(t)
        for nh in range(2):
            ins = t.matmul(ps_o[:, nh * 512:(nh + 1) * 512], onesb[:],
                           b_row[0:1, nh * 512:(nh + 1) * 512],
                           start=True, stop=True)
        s_t.inc(ins)

    @block.vector
    def _(v):
        s_t.wait(v)
        v.tensor_copy(gb_bc[:, 1024:2048], ps_o[:])
        s_v.inc(v.drain())
    gb_done = s_v.n

    ln_done = []
    ev_done = []
    for i in range(NBLK):
        @block.sync
        def _(sync, i=i):
            if i >= 2:
                sync.wait_ge(s_v.h, ln_done[i - 2])
            sld.dma(sync.dma_start(out=xf[i % 2][:], in_=x3[i * 128:(i + 1) * 128, :]))

        @block.vector
        def _(v, i=i):
            sld.wait(v)

        emit_ln_stats(block, xf[i % 2][:], stat, sq[:], s_v, s_a)

        @block.vector
        def _(v, i=i):
            if i >= 1:
                v.wait_ge(s_v.h, ev_done[i - 1])
            v.tensor_copy(stats_all[:, 2 * i:2 * i + 1], stat[:, 0:1])
            v.drain()
            v.tensor_copy(stats_all[:, 2 * i + 1:2 * i + 2], stat[:, 3:4])
            v.drain()
            v.tensor_scalar(xn_t[:], xf[i % 2][:], stat[:, 0:1], stat[:, 3:4],
                            ALU.add, ALU.mult)
            s_v.inc(v.drain())
            ln_done.append(s_v.n)

        @block.tensor
        def _(t, i=i):
            swl.wait(t, swl_base + 16)
            s_v.wait(t)
            for c in range(8):
                ins = t.transpose(ps_t[:, c * 128:(c + 1) * 128],
                                  xn_t[:, c * 128:(c + 1) * 128], identb[:])
            s_t.inc(ins)

        @block.vector
        def _(v, i=i):
            s_t.wait(v)
            v.tensor_copy(fm(xnf)[:, :, i * 128:(i + 1) * 128], fm(ps_t)[:, :, :])
            s_v.inc(v.drain())
            ev_done.append(s_v.n)

    rt_done = []
    for i in range(NBLK):
        @block.tensor
        def _(t, i=i):
            if i == 0:
                t.wait_ge(s_v.h, ev_done[-1])
            if i >= 1:
                t.wait_ge(s_a.h, s_a.n)
            for c in range(8):
                t.matmul(ps_r[:], fm(xnf)[:, c, i * 128:(i + 1) * 128],
                         wr_sb[:, c * 8:(c + 1) * 8],
                         start=(c == 0), stop=False)
            ins = t.matmul(ps_r[:], onesb[:], br_sb[:], start=False, stop=True)
            s_t.inc(ins)

        @block.scalar
        def _(s, i=i):
            s_t.wait(s)
            ins = s.activation(out=logit_sb[:], in_=ps_r[:], func=AF.Copy,
                               bias=0.0, scale=1.0)
            s_a.inc(ins)

        @block.vector
        def _(v, i=i):
            s_a.wait(v)
            v.reduce_max(vv_sb[:, 0:1], logit_sb[:], axis=AX.X)
            v.drain()
            v.tensor_scalar(top_sb[:], logit_sb[:], vv_sb[:, 0:1], None,
                            ALU.is_equal)
            v.drain()
            v.scalar_tensor_tensor(top_sb[:], top_sb[:], -1e30, logit_sb[:],
                                   op0=ALU.mult, op1=ALU.add)
            v.drain()
            v.reduce_max(vv_sb[:, 1:2], top_sb[:], axis=AX.X)
            v.drain()
            v.tensor_sub(vv_sb[:, 2:3], vv_sb[:, 0:1], vv_sb[:, 1:2])
            s_v.inc(v.drain())

        @block.scalar
        def _(s, i=i):
            s_v.wait(s)
            s.activation(out=vv_sb[:, 3:4], in_=vv_sb[:, 2:3], func=AF.Sigmoid,
                         bias=0.0, scale=1.0)
            s_a.inc(s.drain())

        @block.vector
        def _(v, i=i):
            s_a.wait(v)
            rw = rw_all[:, i * 8:(i + 1) * 8]
            v.tensor_scalar(sq[:, 0:8], top_sb[:], vv_sb[:, 1:2], None,
                            ALU.is_equal)
            v.drain()
            v.tensor_scalar(sq[:, 8:16], logit_sb[:], vv_sb[:, 0:1], None,
                            ALU.is_equal)
            v.drain()
            v.tensor_scalar(vv_sb[:, 2:3], vv_sb[:, 3:4], -1.0, 1.0,
                            ALU.mult, ALU.add)
            v.drain()
            v.tensor_scalar_mul(rw, sq[:, 8:16], vv_sb[:, 3:4])
            v.drain()
            v.tensor_scalar_mul(sq[:, 0:8], sq[:, 0:8], vv_sb[:, 2:3])
            v.drain()
            v.tensor_add(rw, rw, sq[:, 0:8])
            v.drain()
            v.tensor_copy(rwb[:], rw)
            s_v.inc(v.drain())

        @block.tensor
        def _(t, i=i):
            s_v.wait(t)
            ins = t.transpose(ps_rt[:], rwb[:], identb[:])
            s_t.inc(ins)

        @block.vector
        def _(v, i=i):
            s_t.wait(v)
            v.tensor_copy(rwT_all[:, i * 128:(i + 1) * 128], ps_rt[0:8, :])
            s_v.inc(v.drain())
            rt_done.append(s_v.n)

    up_t = []
    up_a = []
    ob2_a = []
    oev_v = []
    ep_in = []
    ep_done = []
    ep_ln = []
    st_cnt = []
    w2l_cnt = []

    for grp in range(L1FFN_GROUPS):
        for tb in range(4):
            g_tb = grp * 4 + tb

            @block.tensor
            def _(t, grp=grp, tb=tb, g_tb=g_tb):
                if g_tb == 0:
                    t.wait_ge(s_v.h, rt_done[-1])
                if tb == 0 and oev_v:
                    t.wait_ge(s_v.h, oev_v[-1])
                if tb >= 1:
                    t.wait_ge(s_a.h, ob2_a[-1])
                for nh in range(2):
                    ins = t.matmul(
                        ps_o[:, nh * 512:(nh + 1) * 512],
                        rwT_all[:, g_tb * 128:(g_tb + 1) * 128],
                        b2_sb[:, nh * 512:(nh + 1) * 512],
                        start=True, stop=True)
                s_t.inc(ins)

            @block.scalar
            def _(s, grp=grp, tb=tb, g_tb=g_tb):
                s_t.wait(s)
                if len(ep_in) >= (grp - 1) * 4 + tb + 1 and grp >= 1:
                    s.wait_ge(s_v.h, ep_in[(grp - 1) * 4 + tb])
                ins = s.activation(out=acc[:, tb * 1024:(tb + 1) * 1024],
                                   in_=ps_o[:], func=AF.Copy, bias=0.0, scale=1.0)
                s_a.inc(ins)
                ob2_a.append(s_a.n)

        for e in range(L1FFN_EXPERTS):
            @block.gpsimd
            def _(g, grp=grp, e=e):
                if w2l_cnt:
                    g.wait_ge(s_t.h, w2l_cnt[-1])
                for c0 in range(0, 32, 8):
                    sw2.dma(g.dma_start(
                        out=w2sb[:, c0 * 1024:(c0 + 8) * 1024].rearrange(
                            "p (c o) -> p c o", c=8),
                        in_=w2e[e, c0:c0 + 8].rearrange("c p o -> p c o")))
            w2_loaded = sw2.n

            for oc in range(32):
                kk = len(up_t)

                @block.gpsimd
                def _(g, e=e, oc=oc, kk=kk):
                    if kk >= 2:
                        g.wait_ge(s_t.h, up_t[kk - 2])
                    swu[kk % 2].dma(g.dma_start(out=wsl[kk % 2][:, :], in_=w1e[e, oc]))
                ld_sem, ld_cnt = swu[kk % 2].h, swu[kk % 2].n

                @block.tensor
                def _(t, grp=grp, e=e, oc=oc, kk=kk, ld_sem=ld_sem, ld_cnt=ld_cnt):
                    t.wait_ge(ld_sem, ld_cnt)
                    if kk >= 2:
                        t.wait_ge(s_a.h, up_a[kk - 2])
                    for c in range(8):
                        ins = t.matmul(
                            ps_u[kk % 2][:],
                            wsl[kk % 2][:, c * 128:(c + 1) * 128],
                            fm(xnf)[:, c, grp * 512:(grp + 1) * 512],
                            start=(c == 0), stop=(c == 7))
                    s_t.inc(ins)
                    up_t.append(s_t.n)

                @block.scalar
                def _(s, grp=grp, e=e, oc=oc, kk=kk):
                    s.wait_ge(s_t.h, up_t[kk])
                    if oc == 0 and oev_v:
                        s.wait_ge(s_v.h, oev_v[-1])
                    ins = s.activation(out=ubuf[:, oc * 512:(oc + 1) * 512],
                                       in_=ps_u[kk % 2][:], func=AF.Relu,
                                       bias=b1_sb[:, e * 32 + oc:e * 32 + oc + 1],
                                       scale=1.0)
                    s_a.inc(ins)
                    up_a.append(s_a.n)

            for tb in range(4):
                @block.tensor
                def _(t, grp=grp, e=e, tb=tb, w2_loaded=w2_loaded):
                    t.wait_ge(sw2.h, w2_loaded)
                    s_a.wait(t)
                    if oev_v:
                        t.wait_ge(s_v.h, oev_v[-1])
                    for c in range(32):
                        for nh in range(2):
                            ins = t.matmul(
                                ps_o[:, nh * 512:(nh + 1) * 512],
                                ubuf[:, c * 512 + tb * 128:c * 512 + tb * 128 + 128],
                                w2sb[:, c * 1024 + nh * 512:c * 1024 + nh * 512 + 512],
                                start=(c == 0), stop=(c == 31))
                    s_t.inc(ins)
                o_t = s_t.n

                @block.vector
                def _(v, grp=grp, e=e, tb=tb, o_t=o_t):
                    v.wait_ge(s_t.h, o_t)
                    if e == 0:
                        v.wait_ge(s_a.h, ob2_a[grp * 4 + tb])
                    g_tb = grp * 4 + tb
                    v.scalar_tensor_tensor(
                        acc[:, tb * 1024:(tb + 1) * 1024], ps_o[:],
                        rw_all[:, g_tb * 8 + e:g_tb * 8 + e + 1],
                        acc[:, tb * 1024:(tb + 1) * 1024],
                        op0=ALU.mult, op1=ALU.add)
                    s_v.inc(v.drain())
                    oev_v.append(s_v.n)

            w2l_cnt.append(s_t.n)

        for tb in range(4):
            g_tb = grp * 4 + tb

            @block.sync
            def _(sync, g_tb=g_tb):
                if g_tb < 2:
                    sync.wait_ge(s_v.h, ln_done[-1])
                if g_tb >= 2:
                    sync.wait_ge(s_p.h, ep_done[g_tb - 2])
                sld.dma(sync.dma_start(out=x3e[g_tb % 2][:],
                                       in_=x3[g_tb * 128:(g_tb + 1) * 128, :]))

            @block.vector
            def _(v, grp=grp, tb=tb, g_tb=g_tb):
                s_v.wait(v)
                sld.wait(v)
                v.wait_ge(s_v.h, gb_done)
                v.tensor_scalar(t1[:], x3e[g_tb % 2][:],
                                stats_all[:, 2 * g_tb:2 * g_tb + 1],
                                stats_all[:, 2 * g_tb + 1:2 * g_tb + 2],
                                ALU.add, ALU.mult)
                v.drain()
                v.tensor_add(inner[:], t1[:], acc[:, tb * 1024:(tb + 1) * 1024])
                s_v.inc(v.drain())
                ep_in.append(s_v.n)

            emit_ln_stats(block, inner[:], stat, sq[:], s_v, s_a)

            @block.vector
            def _(v, g_tb=g_tb):
                v.tensor_scalar(t1[:], inner[:], stat[:, 0:1], stat[:, 3:4],
                                ALU.add, ALU.mult)
                v.drain()
                v.tensor_mul(t2[:], t1[:], gb_bc[:, 0:1024])
                v.drain()
                v.tensor_add(t2[:], t2[:], gb_bc[:, 1024:2048])
                s_v.inc(v.drain())

            emit_ln_stats(block, t2[:], stat, sq[:], s_v, s_a)

            @block.vector
            def _(v, g_tb=g_tb):
                if g_tb >= 1:
                    v.wait_ge(s_p.h, ep_done[g_tb - 1])
                v.tensor_scalar(t1[:], t2[:], stat[:, 0:1], stat[:, 3:4],
                                ALU.add, ALU.mult)
                s_v.inc(v.drain())
                ep_ln.append(s_v.n)

            @block.vector
            def _(v, g_tb=g_tb):
                sld.wait(v)
                if g_tb >= 2:
                    v.wait_ge(sst.h, st_cnt[g_tb - 2])
                ins = v.tensor_add(out_sb[g_tb % 2][:], t1[:], x3e[g_tb % 2][:])
                s_p.inc(ins)
                ep_done.append(s_p.n)

            @block.sync
            def _(sync, g_tb=g_tb):
                sync.wait_ge(s_p.h, ep_done[g_tb])
                sst.dma(sync.dma_start(out=x4[g_tb * 128:(g_tb + 1) * 128, :],
                                       in_=out_sb[g_tb % 2][:]))
                st_cnt.append(sst.n)

    @block.sync
    def _(sync):
        sst.wait(sync)



def build_fused():
    nc = bass.Bass("TRN2", target_bir_lowering=False)

    def ei(name, shape, dtype=F32):
        return nc.dram_tensor(name, shape, dtype, kind="ExternalInput").ap()

    io = {}
    io["x_seq"] = ei("x_seq", [SEQ, D])
    io["x_own"] = ei("x_own", [TCH, D])
    io["wqt"] = ei("wqt", [8, 8, 128, 128])
    io["wkt"] = ei("wkt", [8, 8, 128, 128])
    io["wvt"] = ei("wvt", [8, 128, 1024])
    io["wot"] = ei("wot", [8, 128, 1024])
    io["maskt"] = ei("maskt", [128, 79 * 128])
    io["ident"] = ei("ident", [128, 128])
    io["w1t"] = ei("w1t", [32, 128, 1024])
    io["w2t"] = ei("w2t", [32, 128, 1024])
    io["h1t"] = ei("h1t", [32, 128, 4096])
    io["h2t"] = ei("h2t", [32, 128, 4096])
    io["w3t"] = ei("w3t", [32, 128, 1024])
    io["w1s"] = ei("w1s", [128, 32])
    io["w2s"] = ei("w2s", [128, 32])
    io["h1s"] = ei("h1s", [128, 32])
    io["h2s"] = ei("h2s", [128, 32])
    io["w3s"] = ei("w3s", [1, 1024])
    io["encq"] = ei("encq", [8, 1024, 128])
    io["encv"] = ei("encv", [8, 1024, 128])
    io["dec"] = ei("dec", [8, 128, 1024])
    io["heb"] = ei("heb", [8, 128, 128])
    io["mw1"] = ei("mw1", [8, 128, 512])
    io["mw2"] = ei("mw2", [4, 128, 1024])
    io["mb1"] = ei("mb1", [1, 512])
    io["mb2"] = ei("mb2", [1, 1024])
    io["base"] = ei("base", [1, 1024])
    io["band"] = ei("band", [128, 256])
    io["wsel"] = ei("wsel", [128, 16])
    io["hsel"] = ei("hsel", [128, 8])
    io["wrt"] = ei("wrt", [8, 128, 8])
    io["br"] = ei("br", [1, 8])
    io["w1e"] = ei("w1e", [8, 32, 128, 1024])
    io["b1l"] = ei("b1l", [128, 8 * 32])
    io["w2e"] = ei("w2e", [8, 32, 128, 1024])
    io["b2i"] = ei("b2i", [8, 1024])
    io["gam"] = ei("gam", [1, 1024])
    io["bet"] = ei("bet", [1, 1024])

    io["x4"] = nc.dram_tensor("x4", [TCH, D], F32, kind="ExternalOutput").ap()

    kind_i = "ExternalOutput" if DUMP_STAGES else "Internal"
    io["x1"] = nc.dram_tensor("x1", [TCH, D], F32, kind=kind_i).ap()
    io["x2"] = nc.dram_tensor("x2", [TCH, D], F32, kind=kind_i).ap()
    io["x3"] = nc.dram_tensor("x3", [TCH, D], F32, kind=kind_i).ap()
    io["k_sc"] = nc.dram_tensor("k_sc", [8, 128, SEQ], BF16, kind="Internal").ap()
    io["v_sc"] = nc.dram_tensor("v_sc", [16, SEQ, 64], BF16, kind="Internal").ap()
    cc_in_h = nc.dram_tensor("cc_in", [128, 1040], F32, kind="Internal")
    cc_out_h = nc.dram_tensor("cc_out", [N_CORES * 128, 1040], F32, kind="Internal")
    io["cc_in"] = cc_in_h.ap()
    io["cc_out"] = cc_out_h.ap()
    io["cc_in_h"] = cc_in_h
    io["cc_out_h"] = cc_out_h
    io["halo_d"] = nc.dram_tensor("halo_d", [128, D], F32, kind=kind_i).ap()
    io["sig_d"] = nc.dram_tensor("sig_d", [128, 8], F32, kind=kind_i).ap()

    es = contextlib.ExitStack()
    with es:
        root = Ctx(nc, es)
        for name in ("sld", "swl", "sw2", "swu0", "swu1", "swu2", "swu3",
                     "swh0", "swh1", "svx0", "svx1", "s_t", "s_a", "s_v",
                     "s_p", "sst", "ssc", "scc"):
            root.sem(name)
        block = es.enter_context(nc.Block())
        emitters = [("a", emit_l0attn), ("f", emit_l0ffn), ("s", emit_sig),
                    ("h", emit_l1attn), ("m", emit_l1ffn)]
        for idx, (tag, fn) in enumerate(emitters):
            if idx > 0:
                barrier(block, root)
            with contextlib.ExitStack() as es_sec:
                ctx = Ctx(nc, es_sec, root.sems, tag + "_")
                fn(nc, ctx, block, io)
    return nc



def _chunk_w_ic_oc(w):
    i_n, o_n = w.shape
    return np.ascontiguousarray(
        w.reshape(i_n // 128, 128, o_n // 128, 128).transpose(0, 2, 1, 3))


def _chunk_w_oc_ic(w):
    i_n, o_n = w.shape
    return np.ascontiguousarray(
        w.reshape(i_n // 128, 128, o_n // 128, 128).transpose(2, 0, 1, 3))


def prep_l0attn_shared(inputs):
    d = {}
    d["wqt"] = _chunk_w_ic_oc(np.ascontiguousarray(inputs["L0_attn_Wq"].T))
    d["wkt"] = _chunk_w_ic_oc(np.ascontiguousarray(inputs["L0_attn_Wk"].T))
    d["wvt"] = np.ascontiguousarray(
        np.ascontiguousarray(inputs["L0_attn_Wv"].T).reshape(8, 128, 1024))
    d["wot"] = np.ascontiguousarray(
        np.ascontiguousarray(inputs["L0_attn_Wo"].T).reshape(8, 128, 1024))
    d["ident"] = np.eye(128, dtype=np.float32)
    return d


def make_l0attn_mask(p):
    r = np.arange(128)[None, :]
    c = np.arange(128)[:, None]
    tiles = []
    for mu in range(79):
        delta = 16 * p + 15 - mu
        dd = 128 * delta + r - c
        act = (dd >= 0) & ((dd <= 255) | (dd % 64 == 0))
        tiles.append(act.astype(np.float32))
    return np.ascontiguousarray(np.concatenate(tiles, axis=1))


def _pmaj(w4):
    n_oc, n_ic = w4.shape[0], w4.shape[1]
    return np.ascontiguousarray(
        w4.transpose(0, 2, 1, 3).reshape(n_oc, 128, n_ic * 128))


def prep_l0ffn_shared(inputs):
    d = {}
    d["w1t"] = _pmaj(_chunk_w_oc_ic(np.ascontiguousarray(inputs["L0_ffn_w1_w"].T)))
    d["w2t"] = _pmaj(_chunk_w_oc_ic(np.ascontiguousarray(inputs["L0_ffn_w2_w"].T)))
    d["h1t"] = _pmaj(_chunk_w_oc_ic(np.ascontiguousarray(inputs["L0_ffn_h1_w"].T)))
    d["h2t"] = _pmaj(_chunk_w_oc_ic(np.ascontiguousarray(inputs["L0_ffn_h2_w"].T)))
    d["w3t"] = np.ascontiguousarray(
        np.ascontiguousarray(inputs["L0_ffn_w3_w"].T).reshape(32, 128, 1024))
    for nm, key in [("w1s", "L0_ffn_w1_s"), ("w2s", "L0_ffn_w2_s"),
                    ("h1s", "L0_ffn_h1_s"), ("h2s", "L0_ffn_h2_s")]:
        d[nm] = np.ascontiguousarray(inputs[key].reshape(32, 128).T)
    d["w3s"] = np.ascontiguousarray(inputs["L0_ffn_w3_s"].reshape(1, 1024))
    return d


def prep_l1attn_shared(inputs):
    d = {}
    d["encq"] = np.ascontiguousarray(inputs["L1_attn_enc_q"])
    d["encv"] = np.ascontiguousarray(inputs["L1_attn_enc_v"])
    d["dec"] = np.ascontiguousarray(inputs["L1_attn_decoder"].reshape(8, 128, 1024))
    d["heb"] = np.ascontiguousarray(inputs["L1_attn_hebbian"])
    d["mw1"] = np.ascontiguousarray(inputs["L1_attn_m_w1"].reshape(8, 128, 512))
    d["mw2"] = np.ascontiguousarray(inputs["L1_attn_m_w2"].reshape(4, 128, 1024))
    d["mb1"] = np.ascontiguousarray(inputs["L1_attn_m_b1"].reshape(1, 512))
    d["mb2"] = np.ascontiguousarray(inputs["L1_attn_m_b2"].reshape(1, 1024))
    d["base"] = np.ascontiguousarray(inputs["L1_attn_base_metric"].reshape(1, 1024))
    rr = np.arange(128)[:, None]
    ss = np.arange(256)[None, :]
    act = (ss >= rr + 97) & (ss <= rr + 128)
    d["band"] = np.where(act, 0.0, 1e30).astype(np.float32)
    return d


def prep_l1ffn_shared(inputs):
    d = {}
    d["wrt"] = np.ascontiguousarray(
        np.ascontiguousarray(inputs["L1_ffn_Wr"].T).reshape(8, 128, 8))
    d["br"] = np.ascontiguousarray(inputs["L1_ffn_br"].reshape(1, 8))
    w1 = inputs["L1_ffn_W1"]
    d["w1e"] = np.stack([_pmaj(_chunk_w_oc_ic(np.ascontiguousarray(w1[e].T)))
                         for e in range(8)])
    d["b1l"] = np.ascontiguousarray(
        inputs["L1_ffn_b1"].reshape(8, 32, 128).transpose(2, 0, 1).reshape(128, 256))
    w2 = inputs["L1_ffn_W2"]
    d["w2e"] = np.stack([np.ascontiguousarray(w2[e].T).reshape(32, 128, 1024)
                         for e in range(8)])
    d["b2i"] = np.ascontiguousarray(inputs["L1_ffn_b2"])
    d["gam"] = np.ascontiguousarray(inputs["L1_ffn_gamma"].reshape(1, 1024))
    d["bet"] = np.ascontiguousarray(inputs["L1_ffn_beta"].reshape(1, 1024))
    return d


def make_wsel(core):
    b = core // 4
    w = np.zeros((16,), np.float32)
    for r in range(8):
        if r // 4 == b:
            w[r * 2 + 0] = 1.0 / SEQ
            w[r * 2 + 1] = 0.5 / 1024.0
    return np.ascontiguousarray(np.tile(w[None, :], (128, 1)))


def make_hsel(core):
    h = np.zeros((8,), np.float32)
    if core % 4 != 0:
        h[core - 1] = 1.0
    return np.ascontiguousarray(np.tile(h[None, :], (128, 1)))


def kernel(**inputs):
    x = np.ascontiguousarray(inputs["L0_attn_x"], dtype=np.float32)

    sh = {}
    sh.update(prep_l0attn_shared(inputs))
    sh.update(prep_l0ffn_shared(inputs))
    sh.update(prep_l1attn_shared(inputs))
    sh.update(prep_l1ffn_shared(inputs))

    nc = build_fused()
    in_maps = []
    for c in range(N_CORES):
        b, p = c // 4, c % 4
        m = dict(sh)
        m["x_seq"] = np.ascontiguousarray(x[b])
        m["x_own"] = np.ascontiguousarray(x[b][p * TCH:(p + 1) * TCH])
        m["maskt"] = make_l0attn_mask(p)
        m["wsel"] = make_wsel(c)
        m["hsel"] = make_hsel(c)
        in_maps.append(m)
    res = run_bass_kernel_spmd(nc, in_maps, core_ids=list(range(N_CORES)))
    x4 = np.stack([np.asarray(res.results[c]["x4"]) for c in range(N_CORES)])
    return x4.reshape(2, SEQ, D).astype(np.float32)
